# version 13
# baseline (speedup 1.0000x reference)
import contextlib
import numpy as np
import ml_dtypes
import concourse.bass as bass
import concourse.mybir as mybir
from concourse.bass_utils import run_bass_kernel_spmd

F32 = mybir.dt.float32
BF16 = mybir.dt.bfloat16
AF = mybir.ActivationFunctionType
ALU = mybir.AluOpType
AX = mybir.AxisListType

B, T, D = 32, 2048, 1024
NCORE, NSEQ, NCH, NH, HD = 8, 4, 8, 16, 64
HF0, FF1 = 16384, 4096
EPS_LN = 1e-5
EPS_F32 = float(np.finfo(np.float32).eps)
NEGBIG = -1e9

ENGINES = ("sync", "tensor", "vector", "scalar", "gpsimd")


class Sem:
    __slots__ = ("h", "n", "name")
    def __init__(self, nc, name):
        self.h = nc.alloc_semaphore(name)
        self.n = 0
        self.name = name


class Em:
    def __init__(self, nc):
        self.nc = nc
        self.q = {e: [] for e in ENGINES}
        self.waited = {e: {} for e in ENGINES}
        self.last = {e: None for e in ENGINES}
        self.pending = {e: 0 for e in ENGINES}
        self.bar = []
        self.dma_prev = {}
        self.dma_q = {}

    def emit(self, eng, method, *args, waits=(), inc=None, amt=1, **kwargs):
        real = []
        w = self.waited[eng]
        for tok in list(waits) + self.bar:
            if tok is None:
                continue
            sem, cnt, src = tok
            if src == eng or w.get(sem, 0) >= cnt:
                continue
            w[sem] = cnt
            real.append((sem.h, cnt))
        tok = None
        if method is not None:
            if inc is not None:
                if amt == 16:
                    assert self.dma_q.setdefault(inc, eng) == eng, inc
                    prev = self.dma_prev.get(inc)
                    if prev is not None and w.get(inc, 0) < prev:
                        w[inc] = prev
                        real.append((inc.h, prev))
                    inc.n += amt
                    self.dma_prev[inc] = inc.n
                    tok = (inc, inc.n, ("dma", eng))
                else:
                    inc.n += amt
                    tok = (inc, inc.n, eng)
                self.last[eng] = tok
                self.pending[eng] = 0
            else:
                self.pending[eng] += 1
        self.q[eng].append((method, args, kwargs, real,
                            inc.h if inc else None, amt))
        return tok

    def wait(self, eng, *toks):
        self.emit(eng, None, waits=[t for t in toks if t is not None])

    def flush(self, blk):
        for e in ENGINES:
            items = self.q[e]
            if not items:
                continue
            self.q[e] = []

            def body(eng, items=items):
                for method, args, kwargs, waits, inch, amt in items:
                    for semh, cnt in waits:
                        eng.wait_ge(semh, cnt)
                    if method is None:
                        continue
                    ins = getattr(eng, method)(*args, **kwargs)
                    if inch is not None:
                        ins.then_inc(inch, amt)
            getattr(blk, e)(body)


class Bank:
    __slots__ = ("ap", "wtok", "rtoks")
    def __init__(self, ap):
        self.ap = ap
        self.wtok = None
        self.rtoks = []

    def pre_write(self):
        toks = self.rtoks
        self.rtoks = []
        return list(toks)

    def set_write(self, tok):
        self.wtok = tok

    def read(self, tok=None):
        if tok is not None:
            self.rtoks.append(tok)
        return self.wtok



def _tile_lhsT(w_kf):
    K, F = w_kf.shape
    t = w_kf.reshape(K // 128, 128, F // 128, 128)
    return np.ascontiguousarray(t.transpose(2, 1, 0, 3).reshape(F // 128, 128, K))


def _flat_blocks(a):
    f = np.ascontiguousarray(a).reshape(-1)
    assert f.size % (128 * 4096) == 0, f.size
    return np.ascontiguousarray(f.reshape(-1, 128, 4096))


def _fmajor_bias(b):
    return np.ascontiguousarray(np.asarray(b, np.float32).reshape(-1, 128).T)


def _rope_tables():
    pos = np.arange(T, dtype=np.float32)[:, None]
    dim_pair = np.arange(0, HD, 2, dtype=np.float32)[None, :]
    theta = (10000.0 ** (-dim_pair.astype(np.float64) / HD)).astype(np.float32)
    freqs = (pos * theta).astype(np.float32)
    rope = np.concatenate([np.cos(freqs), np.sin(freqs)], -1).astype(np.float32)
    return np.ascontiguousarray(rope[:, 0::2]), np.ascontiguousarray(rope[:, 1::2])


def prep_host(inputs):
    f = lambda k: np.asarray(inputs[k], np.float32)
    wqkv, wproj = f("L0_attn_w_qkv"), f("L0_attn_w_proj")
    w1, w2 = f("L0_ffn_w1"), f("L0_ffn_w2")
    Wq, Wk, Wv, Wo = f("L1_attn_Wq"), f("L1_attn_Wk"), f("L1_attn_Wv"), f("L1_attn_Wo")
    W1f, W2f = f("L1_ffn_W1"), f("L1_ffn_W2")
    perm_h = np.concatenate([np.arange(0, HD, 2), np.arange(1, HD, 2)])
    perm = (np.arange(NH)[:, None] * HD + perm_h[None, :]).reshape(-1)

    m = {}
    m["h_wqk"] = _flat_blocks(_tile_lhsT(wqkv[:, 0:2048]))
    wv_ = wqkv[:, 2048:3072].reshape(8, 128, 4, 256).transpose(2, 1, 0, 3)
    m["h_wv"] = _flat_blocks(wv_.reshape(4, 128, 2048))
    m["h_wproj"] = _flat_blocks(_tile_lhsT(wproj))
    m["h_bproj"] = _fmajor_bias(f("L0_attn_b_proj"))
    m["h_gam"] = _fmajor_bias(f("L0_attn_ln_gamma"))
    m["h_bet"] = _fmajor_bias(f("L0_attn_ln_beta"))
    m["h_gatew"] = np.ascontiguousarray(f("L0_ffn_gate_w").T)
    m["h_gateb"] = f("L0_ffn_gate_b").reshape(1, 8)
    m["h_rmsw"] = f("L0_ffn_rms_w").reshape(1, 8)
    w1cat = np.ascontiguousarray(w1.transpose(1, 0, 2).reshape(1024, HF0))
    m["h_w1t"] = _flat_blocks(_tile_lhsT(w1cat))
    m["h_w2t"] = _flat_blocks(_tile_lhsT(w2.reshape(HF0, 1024)))
    m["h_b1c"] = _fmajor_bias(f("L0_ffn_b1").reshape(-1))
    m["h_b2fm"] = np.ascontiguousarray(
        np.stack([_fmajor_bias(f("L0_ffn_b2")[e]) for e in range(8)]))
    m["h_wq"] = _flat_blocks(_tile_lhsT(np.ascontiguousarray(Wq.T)[:, perm]))
    m["h_wk"] = _flat_blocks(_tile_lhsT(np.ascontiguousarray(Wk.T)[:, perm]))
    wvT_ = np.ascontiguousarray(Wv.T).reshape(8, 128, 4, 256).transpose(2, 1, 0, 3)
    m["h_wvT"] = _flat_blocks(wvT_.reshape(4, 128, 2048))
    m["h_wo"] = _flat_blocks(_tile_lhsT(np.ascontiguousarray(Wo.T)))
    m["h_bq"] = _fmajor_bias(f("L1_attn_bq")[perm])
    m["h_bk"] = _fmajor_bias(f("L1_attn_bk")[perm])
    m["h_bv"] = f("L1_attn_bv").reshape(1, 1024)
    m["h_bo"] = _fmajor_bias(f("L1_attn_bo"))
    m["h_w1f"] = _flat_blocks(_tile_lhsT(np.ascontiguousarray(W1f.T)))
    m["h_w2f"] = _flat_blocks(_tile_lhsT(np.ascontiguousarray(W2f.T)))
    m["h_b1f"] = _fmajor_bias(f("L1_ffn_b1"))
    m["h_b2f"] = _fmajor_bias(f("L1_ffn_b2"))
    c, s = _rope_tables()
    cF, sF = np.ascontiguousarray(c.T), np.ascontiguousarray(s.T)
    stk = lambda a, b: np.ascontiguousarray(np.concatenate([a, b, a, b], 0))
    m["h_T1"] = stk(cF, -sF)
    m["h_T2"] = stk(sF, cF)
    m["h_T3"] = stk(-cF, sF)
    m["h_T4"] = stk(-sF, -cF)
    m["h_ct"] = np.ascontiguousarray(c.reshape(16, 128, 32).transpose(1, 0, 2))
    m["h_st"] = np.ascontiguousarray(s.reshape(16, 128, 32).transpose(1, 0, 2))
    s1 = np.zeros((128, 128), np.float32)
    s2 = np.zeros((128, 128), np.float32)
    for blk in (0, 64):
        for j in range(32):
            s1[blk + j, blk + j] = 1.0
            s1[blk + 32 + j, blk + j] = 1.0
            s2[blk + j, blk + 32 + j] = 1.0
            s2[blk + 32 + j, blk + 32 + j] = 1.0
    m["h_SEL1"] = s1
    m["h_SEL2"] = s2
    return m


def _x_to_dev(xcore):
    return np.ascontiguousarray(xcore.transpose(0, 3, 2, 1))


def _x_from_dev(xdev):
    return np.ascontiguousarray(np.asarray(xdev).transpose(0, 3, 2, 1)).reshape(NSEQ, T, D)


HOST_SHAPES = {
    "h_wqk": [4, 128, 4096], "h_wv": [2, 128, 4096], "h_wproj": [2, 128, 4096],
    "h_bproj": [128, 8], "h_gam": [128, 8], "h_bet": [128, 8],
    "h_gatew": [1024, 8], "h_gateb": [1, 8], "h_rmsw": [1, 8],
    "h_w1t": [32, 128, 4096], "h_w2t": [32, 128, 4096],
    "h_b1c": [128, 128], "h_b2fm": [8, 128, 8],
    "h_wq": [2, 128, 4096], "h_wk": [2, 128, 4096], "h_wvT": [2, 128, 4096],
    "h_wo": [2, 128, 4096], "h_bq": [128, 8], "h_bk": [128, 8],
    "h_bv": [1, 1024], "h_bo": [128, 8],
    "h_w1f": [8, 128, 4096], "h_w2f": [8, 128, 4096],
    "h_b1f": [128, 32], "h_b2f": [128, 8],
    "h_T1": [128, 2048], "h_T2": [128, 2048], "h_T3": [128, 2048],
    "h_T4": [128, 2048], "h_ct": [128, 16, 32], "h_st": [128, 16, 32],
    "h_SEL1": [128, 128], "h_SEL2": [128, 128],
}

PART_HOST = {
    "l0a": ["h_wqk", "h_wv", "h_wproj", "h_bproj", "h_gam", "h_bet"],
    "l0f": ["h_gatew", "h_gateb", "h_rmsw", "h_w1t", "h_w2t", "h_b1c",
            "h_b2fm"],
    "l1a": ["h_wq", "h_wk", "h_wvT", "h_wo", "h_bq", "h_bk", "h_bv", "h_bo",
            "h_T1", "h_T2", "h_T3", "h_T4", "h_ct", "h_st",
            "h_SEL1", "h_SEL2"],
    "l1f": ["h_w1f", "h_w2f", "h_b1f", "h_b2f"],
}


class Ctx:
    pass


def build_program(parts, debug_taps=None, n_seq=NSEQ):
    nc = bass.Bass("TRN2", target_bir_lowering=False)
    em = Em(nc)
    cx = Ctx()
    cx.nc, cx.em = nc, em
    cx.debug_taps = debug_taps or []
    need = set(parts)

    def din(name, shape):
        return nc.dram_tensor(name, list(shape), F32, kind="ExternalInput").ap()

    cx.x_in = din("x_in", [NSEQ, 128, NCH, T])
    cx.H = {k: din(k, HOST_SHAPES[k]) for p in parts for k in PART_HOST[p]}
    cx.x_out = nc.dram_tensor("x_out", [NSEQ, 128, NCH, T], F32,
                              kind="ExternalOutput").ap()

    def scr(name, shape):
        return nc.dram_tensor(name, list(shape), BF16, kind="Internal").ap()

    S = {}
    if "l0a" in need:
        S["wqk"] = scr("s_wqk", [16, 128, 1024])
        S["wv"] = scr("s_wv", [4, 128, 2048])
        S["wproj"] = scr("s_wproj", [8, 128, 1024])
    if "l0f" in need:
        S["gatew"] = scr("s_gatew", [1024, 8])
        S["w1t"] = scr("s_w1t", [128, 128, 1024])
        S["w2t"] = scr("s_w2t", [8, 128, 16384])
        S["xb"] = scr("s_xb", [4, 128, NCH * 512])
    if "l1a" in need:
        S["wq"] = scr("s_wq", [8, 128, 1024])
        S["wk"] = scr("s_wk", [8, 128, 1024])
        S["wvT"] = scr("s_wvT", [4, 128, 2048])
        S["wo"] = scr("s_wo", [8, 128, 1024])
    if "l1f" in need:
        S["w1f"] = scr("s_w1f", [32, 128, 1024])
        S["w2f"] = scr("s_w2f", [8, 128, 4096])
    cx.S = S

    cx.all_sems = []
    for nm in ("s_pe", "s_dve", "s_act", "s_pool", "s_ld", "s_sc", "s_g",
               "s_g2", "s_g3", "s_x", "s_cw", "s_w0", "s_w1", "s_w2",
               "s_w3", "s_w4", "s_w5"):
        s = Sem(nc, nm)
        setattr(cx, nm, s)
        cx.all_sems.append(s)

    S4 = nc.alloc_psum_tensor("psum_S", [128, 2048], F32)
    C2 = nc.alloc_psum_tensor("psum_C", [128, 1024], F32)
    D1 = nc.alloc_psum_tensor("psum_D", [128, 512], F32)
    Tp = nc.alloc_psum_tensor("psum_T", [128, 1024], BF16)
    cx.banks = ([Bank(S4[:, i * 512:(i + 1) * 512]) for i in range(4)]
                + [Bank(C2[:, i * 512:(i + 1) * 512]) for i in range(2)]
                + [Bank(D1[:]), Bank(Tp[:])])
    cx.S4 = S4

    cx.X = nc.alloc_sbuf_tensor("X", [128, NCH, T], F32)
    cx.ones_bf = nc.alloc_sbuf_tensor("ones_bf", [128, 1], BF16)
    cx.ones_f = nc.alloc_sbuf_tensor("ones_f", [128, 1], F32)
    cx.ones_r = nc.alloc_sbuf_tensor("ones_r", [1, 128], F32)
    cx.ident_bf = nc.alloc_sbuf_tensor("ident_bf", [128, 128], BF16)
    cx.ident_f = nc.alloc_sbuf_tensor("ident_f", [128, 128], F32)
    cx.mask = nc.alloc_sbuf_tensor("mask", [128, 128], BF16)
    cx.rs0 = nc.alloc_sbuf_tensor("rs0", [1, 512], F32)
    cx.rs1 = nc.alloc_sbuf_tensor("rs1", [1, 512], F32)
    cx.xs = nc.alloc_sbuf_tensor("xs", [128, 512], F32)
    cx.xsB = nc.alloc_sbuf_tensor("xsB", [128, 512], F32)

    emit_setup(cx)
    emit_prologue(cx, need)

    for seq in range(n_seq):
        cx.seq = seq
        emit_x_load(cx, seq)
        for p in parts:
            part_barrier(cx)
            with contextlib.ExitStack() as st:
                cx.st = st
                PART_EMITTERS[p](cx)
                part_barrier(cx)
                for tp, name, fn in cx.debug_taps:
                    if tp == p and seq == n_seq - 1:
                        emit_debug_tap(cx, name, fn)
                part_barrier(cx)
        emit_x_store(cx, seq, seq == n_seq - 1)

    part_barrier(cx)
    for e in ENGINES:
        cx.em.wait("sync", cx.em.last[e])
    for s in cx.all_sems:
        if s.n:
            cx.em.wait("sync", (s, s.n, None))
    for s in cx.all_sems:
        em.emit("sync", "sem_clear", s.h)

    with nc.Block() as blk:
        em.flush(blk)
    return nc


_anon = [0]

def A(cx, shape, dtype):
    _anon[0] += 1
    return cx.st.enter_context(
        cx.nc.sbuf_tensor(f"t{_anon[0]}", list(shape), dtype))


def part_barrier(cx):
    em = cx.em
    if em.pending["vector"]:
        em.emit("vector", "engine_nop", inc=cx.s_dve)
    if em.pending["gpsimd"]:
        em.emit("gpsimd", "engine_nop", inc=cx.s_pool)
    if em.pending["scalar"]:
        em.emit("scalar", "copy", cx.rs1[0:1, 0:1], cx.rs1[0:1, 0:1],
                inc=cx.s_act)
    assert em.pending["tensor"] == 0, em.pending["tensor"]
    assert em.pending["sync"] == 0
    em.bar = [t for t in (em.last[e] for e in ENGINES) if t is not None]


def t_mm(cx, out_ap, lhsT, rhs, start, stop, waits=(), inc=None, tp=None):
    kw = {"tile_position": tp} if tp is not None else {}
    return cx.em.emit("tensor", "matmul", out_ap, lhsT, rhs, start=start,
                      stop=stop, waits=waits, inc=inc, **kw)


def tiny(cx, eng, method, *args, waits=(), **kw):
    sem = {"vector": cx.s_dve, "scalar": cx.s_act, "gpsimd": cx.s_pool}[eng]
    t = cx.em.emit(eng, method, *args, waits=waits, inc=sem, **kw)
    return (t[0], t[1], None)


def emit_setup(cx):
    em, nc = cx.em, cx.nc
    iota_cm = nc.sbuf_tensor("iota_row", [128, 128], F32)
    iota_row = iota_cm.__enter__()
    pc_cm = nc.sbuf_tensor("part_col", [128, 1], F32)
    part_col = pc_cm.__enter__()
    em.emit("gpsimd", "iota", iota_row[:], pattern=[[1, 128]], base=0,
            channel_multiplier=0, allow_small_or_imprecise_dtypes=True)
    tok = em.emit("gpsimd", "iota", part_col[:], pattern=[[1, 1]], base=0,
                  channel_multiplier=1, allow_small_or_imprecise_dtypes=True,
                  inc=cx.s_pool)
    em.emit("vector", "tensor_scalar", cx.ident_f[:], iota_row[:],
            part_col[:, 0:1], None, ALU.is_equal, waits=[tok])
    em.emit("vector", "tensor_copy", cx.ident_bf[:], cx.ident_f[:])
    em.emit("vector", "tensor_scalar", cx.mask[:], iota_row[:],
            part_col[:, 0:1], float(NEGBIG), ALU.is_gt, ALU.mult)
    em.emit("vector", "memset", cx.ones_bf[:], 1.0)
    em.emit("vector", "memset", cx.ones_f[:], 1.0)
    em.emit("vector", "memset", cx.rs1[:], 0.0)
    cx.tok_setup = em.emit("vector", "memset", cx.ones_r[:], 1.0, inc=cx.s_dve)
    pc_cm.__exit__(None, None, None)
    iota_cm.__exit__(None, None, None)


def _flat_view(scr_ap):
    dims = " ".join(f"a{i}" for i in range(scr_ap.ndim))
    flat = scr_ap.rearrange(f"{dims} -> ({dims})")
    return flat.rearrange("(n p c) -> n p c", p=128, c=4096)


def emit_prologue(cx, need):
    em, nc = cx.em, cx.nc
    with contextlib.ExitStack() as st:
        st_b = st.enter_context(nc.sbuf_tensor([128, 2, 4096], BF16))
        free = [None, None]

        def cast(host_ap, scr_ap):
            blocks = _flat_view(scr_ap)
            for i in range(host_ap.shape[0]):
                par = i % 2
                ld = em.emit("gpsimd", "dma_start", st_b[:, par], host_ap[i],
                             waits=[free[par]],
                             inc=(cx.s_g, cx.s_g2)[par], amt=16)
                free[par] = em.emit("sync", "dma_start", blocks[i], st_b[:, par],
                                    waits=[ld], inc=cx.s_ld, amt=16)
            return free[(host_ap.shape[0] - 1) % 2]

        toks = {}
        if "l0a" in need:
            toks["wqk"] = cast(cx.H["h_wqk"], cx.S["wqk"])
            toks["wv"] = cast(cx.H["h_wv"], cx.S["wv"])
            toks["wproj"] = cast(cx.H["h_wproj"], cx.S["wproj"])
        if "l0f" in need:
            gv = cx.H["h_gatew"].rearrange("(a p) e -> p a e", p=128)
            sv = cx.S["gatew"].rearrange("(a p) e -> p a e", p=128)
            stg = st_b[:, 0, 0:64].rearrange("p (a e) -> p a e", a=8)
            ld = em.emit("gpsimd", "dma_start", stg, gv,
                         waits=[free[0]], inc=cx.s_g, amt=16)
            free[0] = toks["gatew"] = em.emit(
                "sync", "dma_start", sv, stg, waits=[ld],
                inc=cx.s_ld, amt=16)
            toks["w1t"] = cast(cx.H["h_w1t"], cx.S["w1t"])
            toks["w2t"] = cast(cx.H["h_w2t"], cx.S["w2t"])
        if "l1a" in need:
            toks["wq"] = cast(cx.H["h_wq"], cx.S["wq"])
            toks["wk"] = cast(cx.H["h_wk"], cx.S["wk"])
            toks["wvT"] = cast(cx.H["h_wvT"], cx.S["wvT"])
            toks["wo"] = cast(cx.H["h_wo"], cx.S["wo"])
        if "l1f" in need:
            toks["w1f"] = _ternary(cx, st, cx.H["h_w1f"], cx.S["w1f"], "w1f")
            toks["w2f"] = _ternary(cx, st, cx.H["h_w2f"], cx.S["w2f"], "w2f")
        cx.pro = toks
        part_barrier_prologue(cx)


def part_barrier_prologue(cx):
    em = cx.em
    if em.pending["vector"]:
        em.emit("vector", "engine_nop", inc=cx.s_dve)
    if em.pending["gpsimd"]:
        em.emit("gpsimd", "engine_nop", inc=cx.s_pool)
    em.bar = [t for t in (em.last[e] for e in ENGINES) if t is not None]


def _ternary(cx, st_outer, host_ap, scr_ap, name):
    em, nc = cx.em, cx.nc
    n = host_ap.shape[0]
    st = contextlib.ExitStack()
    stf = st.enter_context(nc.sbuf_tensor(f"tf_{name}", [128, 2, 4096], F32))
    sb = st.enter_context(nc.sbuf_tensor(f"tb_{name}", [128, 4096], BF16))
    acc = st.enter_context(nc.sbuf_tensor(f"ta_{name}", [128, 8], F32))
    rsc = st.enter_context(nc.sbuf_tensor(f"tr_{name}", [128, 2], F32))
    t1 = st.enter_context(nc.sbuf_tensor(f"t1_{name}", [128, 4096], F32))
    t2 = st.enter_context(nc.sbuf_tensor(f"t2_{name}", [128, 4096], F32))
    lds = [None, None]
    cons = [None, None]
    vtok = None
    for i in range(n):
        par = i % 2
        lds[par] = em.emit("sync", "dma_start", stf[:, par], host_ap[i],
                           waits=[cons[par]], inc=cx.s_ld, amt=16)
        cons[par] = em.emit("scalar", "activation", t1[:], stf[:, par],
                            AF.Abs, waits=[lds[par], vtok], inc=cx.s_act)
        w_ = 2048
        first = True
        while w_ >= 64:
            em.emit("vector", "tensor_add", t1[:, 0:w_], t1[:, 0:w_],
                    t1[:, w_:2 * w_], waits=[cons[par]] if first else ())
            first = False
            w_ //= 2
        tk = None
        while w_ >= 1:
            tk = tiny(cx, "vector", "tensor_add", t1[:, 0:w_], t1[:, 0:w_],
                      t1[:, w_:2 * w_], waits=[tk])
            w_ //= 2
        vtok = tiny(cx, "vector", "tensor_copy", acc[:, i:i + 1],
                    t1[:, 0:1], waits=[tk])
    w_ = n // 2
    while w_ >= 1:
        vtok = tiny(cx, "vector", "tensor_add", acc[:, 0:w_], acc[:, 0:w_],
                    acc[:, w_:2 * w_], waits=[vtok])
        w_ //= 2
    bk = cx.banks[6]
    mm = em.emit("tensor", "transpose", bk.ap[0:1, 0:128], acc[:, 0:1],
                 cx.ident_f[:], waits=bk.pre_write() + [vtok, cx.tok_setup],
                 inc=cx.s_pe)
    bk.set_write(mm)
    rr = tiny(cx, "vector", "tensor_copy", cx.rs0[0:1, 0:128],
              bk.ap[0:1, 0:128], waits=[bk.read()])
    bk.rtoks.append(rr)
    w_ = 64
    while w_ >= 1:
        rr = tiny(cx, "vector", "tensor_add", cx.rs0[0:1, 0:w_],
                  cx.rs0[0:1, 0:w_], cx.rs0[0:1, w_:2 * w_], waits=[rr])
        w_ //= 2
    assert n * 128 * 4096 == 1 << 22
    a = tiny(cx, "vector", "tensor_scalar", cx.rs0[0:1, 0:1],
             cx.rs0[0:1, 0:1], float(2.0 ** -22), None, ALU.mult, waits=[rr])
    a = tiny(cx, "vector", "tensor_scalar_max", cx.rs0[0:1, 0:1],
             cx.rs0[0:1, 0:1], 1e-8, waits=[a])
    a = tiny(cx, "vector", "tensor_scalar", cx.rs0[0:1, 1:2],
             cx.rs0[0:1, 0:1], 0.5, None, ALU.mult, waits=[a])
    stok = tiny(cx, "vector", "tensor_scalar", cx.rs0[0:1, 2:3],
                cx.rs0[0:1, 1:2], -1.0, None, ALU.mult, waits=[a])
    mm2 = t_mm(cx, bk.ap[:, 0:2], cx.ones_r[:], cx.rs0[0:1, 1:3], True, True,
               waits=bk.pre_write() + [stok], inc=cx.s_pe)
    bk.set_write(mm2)
    cp = tiny(cx, "vector", "tensor_copy", rsc[:, 0:2], bk.ap[:, 0:2],
              waits=[bk.read()])
    bk.rtoks.append(cp)
    blocks = _flat_view(scr_ap)
    out_tok = None
    for i in range(n):
        par = i % 2
        lds[par] = em.emit("sync", "dma_start", stf[:, par], host_ap[i],
                           waits=[cp if i < 2 else None, out_tok],
                           inc=cx.s_ld, amt=16)
        em.emit("vector", "tensor_scalar", t2[:], stf[:, par],
                rsc[:, 0:1], None, ALU.is_gt, waits=[lds[par], cp])
        em.emit("vector", "tensor_scalar", t1[:], stf[:, par],
                rsc[:, 1:2], None, ALU.is_lt)
        wtk = em.emit("vector", "tensor_sub", sb[:], t2[:], t1[:],
                      inc=cx.s_dve)
        out_tok = em.emit("sync", "dma_start", blocks[i], sb[:], waits=[wtk],
                          inc=cx.s_ld, amt=16)
    st.close()
    return out_tok


def emit_x_load(cx, seq):
    cx.tok_x = cx.em.emit("sync", "dma_start", cx.X[:], cx.x_in[seq],
                          waits=[getattr(cx, "tok_xstore", None)],
                          inc=cx.s_x, amt=16)


def emit_x_store(cx, seq, final):
    em = cx.em
    cx.tok_xstore = em.emit("scalar", "dma_start", cx.x_out[seq], cx.X[:],
                            waits=[cx.tok_x], inc=cx.s_sc, amt=16)
    if final:
        em.wait("scalar", cx.tok_xstore)


def emit_debug_tap(cx, name, fn):
    ap = fn(cx)
    out = cx.nc.dram_tensor(f"dbg_{name}", list(ap.shape), ap.dtype,
                            kind="ExternalOutput").ap()
    tok = cx.em.emit("scalar", "dma_start", out, ap, inc=cx.s_sc, amt=16)
    cx.em.wait("scalar", tok)



def emit_ln_stats(cx, r_row, b_row, xb):
    em = cx.em
    last = None
    sq_free = [None, None]
    _anon[0] += 1
    sq_cm = cx.nc.sbuf_tensor(f"sq{_anon[0]}", [128, 2, 512], BF16)
    sq = sq_cm.__enter__()
    for tt in range(4):
        sl = slice(tt * 512, (tt + 1) * 512)
        b0, b1 = cx.banks[0], cx.banks[1]
        w0 = b0.pre_write() + [cx.tok_setup]
        w1 = b1.pre_write()
        for c in range(NCH):
            ct = em.emit("vector", "tensor_copy", xb[:, c], cx.X[:, c, sl],
                         waits=[cx.tok_x], inc=cx.s_dve)
            sqt = em.emit("gpsimd", "tensor_mul", sq[:, c % 2], xb[:, c],
                          xb[:, c], waits=[ct, sq_free[c % 2]],
                          inc=cx.s_pool)
            mm0 = t_mm(cx, b0.ap[0:1, :], cx.ones_bf[:], xb[:, c], c == 0,
                       c == NCH - 1, waits=w0 + [ct],
                       inc=cx.s_pe if c == NCH - 1 else None)
            mm1 = t_mm(cx, b1.ap[0:1, :], cx.ones_bf[:], sq[:, c % 2], c == 0,
                       c == NCH - 1, waits=w1 + [sqt], inc=cx.s_pe)
            sq_free[c % 2] = mm1
        b0.set_write(mm0)
        b1.set_write(mm1)
        a0 = tiny(cx, "vector", "tensor_scalar", cx.rs0[0:1, :],
                  b0.ap[0:1, :], 1.0 / D, None, ALU.mult, waits=[b0.read()])
        b0.rtoks.append(a0)
        a1 = tiny(cx, "vector", "tensor_scalar", cx.rs1[0:1, :],
                  b1.ap[0:1, :], 1.0 / D, None, ALU.mult, waits=[b1.read()])
        b1.rtoks.append(a1)
        m2 = tiny(cx, "vector", "tensor_mul", cx.xs[0:1, :], cx.rs0[0:1, :],
                  cx.rs0[0:1, :], waits=[a0])
        s1 = tiny(cx, "vector", "tensor_sub", cx.rs1[0:1, :], cx.rs1[0:1, :],
                  cx.xs[0:1, :], waits=[a1, m2])
        vt = tiny(cx, "vector", "tensor_scalar_add", cx.rs1[0:1, :],
                  cx.rs1[0:1, :], EPS_LN, waits=[s1])
        sq2 = tiny(cx, "scalar", "activation", cx.rs1[0:1, :],
                   cx.rs1[0:1, :], AF.Sqrt, waits=[vt])
        rr = tiny(cx, "vector", "reciprocal", r_row[0:1, sl],
                  cx.rs1[0:1, :], waits=[sq2])
        bb = tiny(cx, "vector", "tensor_mul", b_row[0:1, sl],
                  cx.rs0[0:1, :], r_row[0:1, sl], waits=[rr, a0])
        last = tiny(cx, "vector", "tensor_scalar_mul", b_row[0:1, sl],
                    b_row[0:1, sl], -1.0, waits=[bb])
    sq_cm.__exit__(None, None, None)
    part_barrier(cx)
    return last


def emit_bcast_rb(cx, bank, r_row, b_row, t0, L, stats_tok):
    w = bank.pre_write() + [stats_tok, cx.tok_setup]
    t_mm(cx, bank.ap[:, 0:L], cx.ones_r[:], r_row[0:1, t0:t0 + L], True, True,
         waits=w)
    mm = t_mm(cx, bank.ap[:, 256:256 + L], cx.ones_r[:],
              b_row[0:1, t0:t0 + L], True, True, inc=cx.s_pe)
    bank.set_write(mm)
    return mm


def emit_xn_tile(cx, bank, r_row, b_row, t0, L, out, stats_tok,
                 out_free=None):
    em = cx.em
    last = None
    for h0 in range(0, L, 256):
        Lh = min(256, L - h0)
        emit_bcast_rb(cx, bank, r_row, b_row, t0 + h0, Lh, stats_tok)
        for c in range(NCH):
            em.emit("vector", "tensor_mul", cx.xs[:, 0:Lh],
                    cx.X[:, c, t0 + h0:t0 + h0 + Lh], bank.ap[:, 0:Lh],
                    waits=[bank.read(), cx.tok_x, out_free])
            last = em.emit("vector", "tensor_add", out[:, c, h0:h0 + Lh],
                           cx.xs[:, 0:Lh], bank.ap[:, 256:256 + Lh],
                           inc=cx.s_dve)
            bank.rtoks.append(last)
    return last


class WStream:

    def __init__(self, cx, nkc=8, nbuf=2, sems=None):
        self.cx = cx
        self.nkc = nkc
        self.buf = A(cx, [128, nbuf, nkc, 128], BF16)
        self.nbuf = nbuf
        self.free = [None] * nbuf
        self.cnt = 0
        self.sems = sems or [cx.s_ld] * nbuf

    def load(self, scr, mc, kc0=0, pro=None):
        i = self.cnt % self.nbuf
        self.cnt += 1
        src = scr[mc][:, kc0 * 128:(kc0 + self.nkc) * 128].rearrange(
            "p (kc j) -> p kc j", j=128)
        tok = self.cx.em.emit("sync", "dma_start", self.buf[:, i], src,
                              waits=[self.free[i], pro],
                              inc=self.sems[i], amt=16)
        return i, tok

    def done(self, i, tok):
        self.free[i] = tok



def emit_attn_core(cx, Q, K_, V, AO, P, PT, rc, qk_tok, v_tok, causal,
                   split_pv, T34=None, M3=None, M4=None, t34_toks=None):
    em = cx.em
    banks = cx.banks
    for qt in range(16):
        q0 = qt * 128
        L = q0 + 128 if causal else T
        nb = (L + 511) // 512
        nkb = L // 128

        def s_phase(pair, hh):
            rows = slice(hh * 64, (hh + 1) * 64)
            for kb in range(nb):
                N = min(512, L - kb * 512)
                bk = banks[kb]
                mm = t_mm(cx, bk.ap[:, 0:N],
                          Q[:, pair, q0:q0 + 128][rows, :],
                          K_[:, pair, kb * 512:kb * 512 + N][rows, :],
                          True, True, waits=bk.pre_write() + [qk_tok],
                          inc=cx.s_pe)
                bk.set_write(mm)
            if causal:
                dkb, dof = divmod(L - 128, 512)
                mt = em.emit("vector", "tensor_add",
                             banks[dkb].ap[:, dof:dof + 128],
                             banks[dkb].ap[:, dof:dof + 128], cx.mask[:],
                             waits=[banks[dkb].read()], inc=cx.s_dve)
                banks[dkb].set_write(mt)
            mx = None
            for kb in range(nb):
                N = min(512, L - kb * 512)
                rm = tiny(cx, "vector", "reduce_max", rc[:, kb:kb + 1],
                          banks[kb].ap[:, 0:N], axis=AX.X,
                          waits=[banks[kb].read()])
                if kb == 0:
                    mx = rm
                else:
                    mx = tiny(cx, "vector", "tensor_max", rc[:, 0:1],
                              rc[:, 0:1], rc[:, kb:kb + 1],
                              waits=[rm, mx])
            return tiny(cx, "vector", "tensor_scalar_mul", rc[:, 4:5],
                        rc[:, 0:1], -0.125, waits=[mx])

        def softmax_phase(nm, extra):
            ex = None
            for kb in range(nb):
                N = min(512, L - kb * 512)
                ex = em.emit("scalar", "activation",
                             P[:, kb * 512:kb * 512 + N],
                             cx.S4[:, kb * 512:kb * 512 + N], AF.Exp,
                             bias=rc[:, 4:5], scale=0.125,
                             accum_out=rc[:, kb:kb + 1],
                             waits=[nm] + ([extra] if kb == 0 else []),
                             inc=cx.s_act)
                banks[kb].rtoks.append(ex)
            sm = tiny(cx, "vector", "reduce_sum", rc[:, 5:6], rc[:, 0:nb],
                      axis=AX.X, waits=[ex])
            rp = tiny(cx, "vector", "reciprocal", rc[:, 6:7],
                      rc[:, 5:6], waits=[sm])
            return em.emit("gpsimd", "tensor_scalar_mul", P[:, 0:L],
                           P[:, 0:L], rc[:, 6:7], waits=[rp],
                           inc=cx.s_pool)

        def ptpv_phase(pair, hh, ptok, out_bk, ow):
            head = pair * 2 + hh
            pv_last = None
            last_tr = None
            for g0 in range(0, nkb, 4):
                kbs = list(range(g0, min(g0 + 4, nkb)))
                tb = banks[7]
                wtb = tb.pre_write()
                for kb in kbs:
                    j = kb - g0
                    tr = em.emit("tensor", "transpose",
                                 tb.ap[:, j * 128:(j + 1) * 128],
                                 P[:, kb * 128:(kb + 1) * 128],
                                 cx.ident_bf[:],
                                 waits=(wtb if j == 0 else [])
                                 + [ptok, cx.tok_setup],
                                 inc=cx.s_pe if kb == kbs[-1] else None)
                tb.set_write(tr)
                last_tr = tr
                pt = PT[(g0 // 4) % 2]
                ev = em.emit("scalar", "copy",
                             pt[:, 0:len(kbs) * 128],
                             tb.ap[:, 0:len(kbs) * 128],
                             waits=[tb.read()], inc=cx.s_act)
                tb.rtoks.append(ev)
                for kb in kbs:
                    j = kb - g0
                    ptj = pt[:, j * 128:(j + 1) * 128]
                    w0 = [ev, v_tok] + (ow if kb == 0 and hh == 0 else [])
                    if not split_pv:
                        pv_last = t_mm(
                            cx, out_bk.ap[slice(hh * 64, hh * 64 + 64),
                                          0:128],
                            V[:, kb, head * 64:(head + 1) * 64], ptj,
                            kb == 0, kb == nkb - 1, waits=w0,
                            tp=(0, hh * 64),
                            inc=cx.s_pe if kb == nkb - 1 else None)
                    else:
                        vh = V[:, kb, head * 64:(head + 1) * 64]
                        vv = vh.rearrange("p (j two) -> p two j", two=2)
                        for par in range(2):
                            r0 = hh * 64 + par * 32
                            pv_last = t_mm(
                                cx, out_bk.ap[r0:r0 + 32, 0:128],
                                vv[:, par, :], ptj, kb == 0,
                                kb == nkb - 1, waits=w0, tp=(0, r0),
                                inc=cx.s_pe if kb == nkb - 1 and par == 1
                                else None)
            return pv_last, last_tr

        for pair in range(8):
            out_bk = banks[4 + (pair % 2)]
            ow = out_bk.pre_write()
            nm0 = s_phase(pair, 0)
            ptok0 = softmax_phase(nm0, None)
            nm1 = s_phase(pair, 1)
            _, tr0 = ptpv_phase(pair, 0, ptok0, out_bk, ow)
            ptok1 = softmax_phase(nm1, tr0)
            pv_last, _ = ptpv_phase(pair, 1, ptok1, out_bk, ow)
            out_bk.set_write(pv_last)
            if T34 is None:
                ao_tok = em.emit("vector", "tensor_copy", AO[:, pair, :],
                                 out_bk.ap[:, 0:128], waits=[out_bk.read()],
                                 inc=cx.s_dve)
                out_bk.rtoks.append(ao_tok)
            else:
                tts = qt // 4
                tj = (qt % 4) * 128
                a = em.emit("vector", "tensor_mul", M3[:],
                            out_bk.ap[:, 0:128],
                            T34[0][tts % 2][:, tj:tj + 128],
                            waits=[out_bk.read(), t34_toks[tts % 2]],
                            inc=cx.s_dve)
                out_bk.rtoks.append(a)
                a = em.emit("vector", "tensor_mul", M4[:],
                            out_bk.ap[:, 0:128],
                            T34[1][tts % 2][:, tj:tj + 128], inc=cx.s_dve)
                out_bk.rtoks.append(a)
                rb = banks[6]
                w = rb.pre_write() + [a, cx.tok_l1tab]
                t_mm(cx, rb.ap[:, 0:128], cx.sel1[:], M3[:], True, False,
                     waits=w)
                mm2 = t_mm(cx, rb.ap[:, 0:128], cx.sel2[:], M4[:], False,
                           True, inc=cx.s_pe)
                rb.set_write(mm2)
                ao_tok = em.emit("vector", "tensor_copy", AO[:, pair, :],
                                 rb.ap[:, 0:128], waits=[rb.read()],
                                 inc=cx.s_dve)
                rb.rtoks.append(ao_tok)
        yield qt, q0, ao_tok



def emit_qk_fmajor(cx, ws, Q, K_, r_row, b_row, stats_tok, xb, rope_bufs=None,
                   pros=(None, None), scrs=(None, None)):
    em = cx.em
    rope = rope_bufs is not None
    if rope:
        bq, bk_, TTd, TTs, U1, U2 = rope_bufs
        tt_free = None
    ev = None
    for tt in range(4):
        t0 = tt * 512
        xn_tok = emit_xn_tile(cx, cx.banks[6], r_row, b_row, t0, 512, xb,
                              stats_tok)
        if rope:
            t12 = em.emit("gpsimd", "dma_start", TTs[0][:],
                          TTd[0][:, t0:t0 + 512], waits=[tt_free],
                          inc=cx.s_g3, amt=16)
            t12 = em.emit("gpsimd", "dma_start", TTs[1][:],
                          TTd[1][:, t0:t0 + 512], inc=cx.s_g3, amt=16)
        for fc in range(16):
            if scrs[1] is None:
                scr_w, mc, pro = scrs[0], fc, pros[0]
            else:
                scr_w, mc, pro = ((scrs[0], fc, pros[0]) if fc < 8 else
                                  (scrs[1], fc - 8, pros[1]))
            wi, wtok = ws.load(scr_w, mc, pro=pro)
            bk = cx.banks[fc % 2]
            w = bk.pre_write() + [wtok, xn_tok]
            for kc in range(NCH):
                mm = t_mm(cx, bk.ap, ws.buf[:, wi, kc], xb[:, kc],
                          kc == 0, kc == NCH - 1, waits=w if kc == 0 else (),
                          inc=cx.s_pe if kc == NCH - 1 else None)
            bk.set_write(mm)
            ws.done(wi, mm)
            dst = (Q if fc < 8 else K_)[:, fc % 8, t0:t0 + 512]
            if not rope:
                ev = em.emit("vector", "tensor_copy", dst, bk.ap,
                             waits=[bk.read()], inc=cx.s_dve)
                bk.rtoks.append(ev)
            else:
                bias = (bq if fc < 8 else bk_)[:, fc % 8:fc % 8 + 1]
                a = em.emit("vector", "tensor_scalar_add", cx.xs[:], bk.ap,
                            bias, waits=[bk.read(), t12], inc=cx.s_dve)
                bk.rtoks.append(a)
                u1 = em.emit("vector", "tensor_mul", U1[:], cx.xs[:],
                             TTs[0][:, :])
                u2 = em.emit("vector", "tensor_mul", U2[:], cx.xs[:],
                             TTs[1][:, :], inc=cx.s_dve)
                rb = cx.banks[2]
                w = rb.pre_write() + [u2, cx.tok_l1tab]
                t_mm(cx, rb.ap, cx.sel1[:], U1[:], True, False, waits=w)
                mm2 = t_mm(cx, rb.ap, cx.sel2[:], U2[:], False, True,
                           inc=cx.s_pe)
                rb.set_write(mm2)
                ev = em.emit("vector", "tensor_copy", dst, rb.ap,
                             waits=[rb.read()], inc=cx.s_dve)
                rb.rtoks.append(ev)
                if fc == 15:
                    tt_free = ev
    return ev


def emit_v_tmajor(cx, scr_wv, V, r_row, b_row, stats_tok, xb, wvbuf,
                  rope_bufs=None, pro=None, nbuf=2, sems=None):
    em = cx.em
    rope = rope_bufs is not None
    if rope:
        bvb, ct, st_ = rope_bufs
    wv = scr_wv
    free = [None, None]
    ev = None
    for tt in range(4):
        t0 = tt * 512
        xn_tok = emit_xn_tile(cx, cx.banks[6], r_row, b_row, t0, 512, xb,
                              stats_tok)
        for g in range(4):
            vi = g % nbuf
            vtok = em.emit("sync", "dma_start", wvbuf[:, vi],
                           wv[g].rearrange("p (kc j) -> p kc j", j=256),
                           waits=[pro, free[vi]],
                           inc=(sems or (cx.s_w2,) * nbuf)[vi], amt=16)
            for q4 in range(4):
                tq = tt * 4 + q4
                bk = cx.banks[2 + (q4 % 2)]
                w = bk.pre_write() + [vtok, xn_tok]
                for kc in range(NCH):
                    mm = t_mm(cx, bk.ap[:, 0:256],
                              xb[:, kc, q4 * 128:(q4 + 1) * 128],
                              wvbuf[:, vi, kc], kc == 0, kc == NCH - 1,
                              waits=w if kc == 0 else (),
                              inc=cx.s_pe if kc == NCH - 1 else None)
                bk.set_write(mm)
                last_mm = mm
                dst = V[:, tq, g * 256:(g + 1) * 256]
                if not rope:
                    ev = em.emit("vector", "tensor_copy", dst,
                                 bk.ap[:, 0:256], waits=[bk.read()],
                                 inc=cx.s_dve)
                    bk.rtoks.append(ev)
                else:
                    a = em.emit("vector", "tensor_add", cx.xs[:, 0:256],
                                bk.ap[:, 0:256],
                                bvb[:, g * 256:(g + 1) * 256],
                                waits=[bk.read()], inc=cx.s_dve)
                    bk.rtoks.append(a)
                    vv = cx.xs[:, 0:256].rearrange(
                        "p (h j two) -> p h j two", h=4, two=2)
                    evn, od = vv[:, :, :, 0], vv[:, :, :, 1]
                    ctb = ct[:, tq:tq + 1, :].broadcast_to([128, 4, 32])
                    stb = st_[:, tq:tq + 1, :].broadcast_to([128, 4, 32])
                    dh = dst.rearrange("p (h half j) -> p h half j", h=4,
                                       half=2)
                    xA = cx.xsB[:, 0:128].rearrange("p (h j) -> p h j", h=4)
                    xBv = cx.xsB[:, 128:256].rearrange("p (h j) -> p h j", h=4)
                    em.emit("vector", "tensor_mul", xA, evn, ctb)
                    em.emit("vector", "tensor_mul", xBv, od, stb)
                    em.emit("vector", "tensor_sub", dh[:, :, 0], xA, xBv)
                    em.emit("vector", "tensor_mul", xA, evn, stb)
                    em.emit("vector", "tensor_mul", xBv, od, ctb)
                    ev = em.emit("vector", "tensor_add", dh[:, :, 1], xA, xBv,
                                 inc=cx.s_dve)
            free[vi] = last_mm
    return ev



def part_l0a(cx):
    em, nc = cx.em, cx.nc
    if not hasattr(cx, "bias_bproj"):
        cx.bias_bproj = nc.alloc_sbuf_tensor("bias_bproj", [128, 8], F32)
        cx.bias_gam = nc.alloc_sbuf_tensor("bias_gam", [128, 8], F32)
        cx.bias_bet = nc.alloc_sbuf_tensor("bias_bet", [128, 8], F32)
        em.emit("sync", "dma_start", cx.bias_bproj[:], cx.H["h_bproj"],
                inc=cx.s_ld, amt=16)
        em.emit("sync", "dma_start", cx.bias_gam[:], cx.H["h_gam"],
                inc=cx.s_ld, amt=16)
        cx.tok_l0bias = em.emit("sync", "dma_start", cx.bias_bet[:],
                                cx.H["h_bet"], inc=cx.s_ld, amt=16)

    if cx.debug_taps:
        cx.dbg_ln2 = cx.nc.dram_tensor("dbg_ln2", [1, 4 * T], F32,
                                       kind="ExternalOutput").ap()
    Q = A(cx, [128, NCH, T], BF16)
    K_ = A(cx, [128, NCH, T], BF16)
    r_row = A(cx, [1, T], F32)
    b_row = A(cx, [1, T], F32)
    V = A(cx, [128, 16, 1024], BF16)

    with contextlib.ExitStack() as st2:
        old = cx.st
        cx.st = st2
        xb = A(cx, [128, NCH, 512], BF16)
        stats_tok = emit_ln_stats(cx, r_row, b_row, xb)
        cx.tok_stats_l0a = stats_tok
        ws = WStream(cx, sems=(cx.s_w0, cx.s_w1))
        qk_tok = emit_qk_fmajor(cx, ws, Q, K_, r_row, b_row, stats_tok, xb,
                                pros=(cx.pro.get("wqk"), None),
                                scrs=(cx.S["wqk"], None))
        wvbuf = A(cx, [128, 1, NCH, 256], BF16)
        v_tok = emit_v_tmajor(cx, cx.S["wv"], V, r_row, b_row, stats_tok, xb,
                              wvbuf, pro=cx.pro.get("wv"), nbuf=1,
                              sems=(cx.s_w2,))
        cx.st = old
    part_barrier(cx)

    P = A(cx, [128, T], BF16)
    PT = [A(cx, [128, 512], BF16) for _ in range(2)]
    AO = A(cx, [128, NCH, 128], BF16)
    rc = A(cx, [128, 8], F32)
    Tsc = A(cx, [128, NCH, 128], F32)
    Tb = A(cx, [128, NCH, 128], BF16)
    sq2 = A(cx, [128, 2, 128], BF16)
    wp = WStream(cx, nbuf=1, sems=(cx.s_w3,))
    cx._sq2_free = [None, None]

    cx.dbg = dict(Q=Q, K=K_, V=V, P=P, AO=AO, r_row=r_row, b_row=b_row,
                  rc=rc, Tsc=Tsc)
    for qt, q0, ao_tok in emit_attn_core(cx, Q, K_, V, AO, P, PT, rc,
                                         qk_tok, v_tok, causal=True,
                                         split_pv=False):
        _l0a_epilogue(cx, AO, ao_tok, q0, r_row, b_row, Tsc, Tb, sq2, wp)


def _l0a_epilogue(cx, AO, ao_tok, q0, r_row, b_row, Tsc, Tb, sq2, wp):
    em = cx.em
    banks = cx.banks
    bc1 = banks[2]
    emit_bcast_rb(cx, bc1, r_row, b_row, q0, 128, cx.tok_stats_l0a)
    proj_toks = []
    for mc in range(NCH):
        wi, wtok = wp.load(cx.S["wproj"], mc, pro=cx.pro.get("wproj"))
        bk = banks[mc % 2]
        w = bk.pre_write() + [wtok, ao_tok, cx.tok_l0bias]
        for kc in range(NCH):
            mm = t_mm(cx, bk.ap[:, 0:128], wp.buf[:, wi, kc], AO[:, kc, :],
                      kc == 0, kc == NCH - 1, waits=w if kc == 0 else (),
                      inc=cx.s_pe if kc == NCH - 1 else None)
        bk.set_write(mm)
        wp.done(wi, mm)
        em.emit("vector", "tensor_mul", cx.xs[:, 0:128],
                cx.X[:, mc, q0:q0 + 128], bc1.ap[:, 0:128],
                waits=[bc1.read(), bk.read()])
        em.emit("vector", "tensor_add", cx.xs[:, 0:128], cx.xs[:, 0:128],
                bc1.ap[:, 256:384])
        tk = em.emit("vector", "scalar_tensor_tensor", Tsc[:, mc],
                     bk.ap[:, 0:128], cx.bias_bproj[:, mc:mc + 1],
                     cx.xs[:, 0:128], ALU.add, ALU.add, inc=cx.s_dve)
        bk.rtoks.append(tk)
        bc1.rtoks.append(tk)
        pt = em.emit("vector", "tensor_copy", Tb[:, mc], Tsc[:, mc],
                     inc=cx.s_dve)
        proj_toks.append(pt)
    b0, b1 = banks[5], banks[6]
    w0 = b0.pre_write()
    w1 = b1.pre_write()
    for c in range(NCH):
        sqt = em.emit("gpsimd", "tensor_mul", sq2[:, c % 2], Tb[:, c],
                      Tb[:, c], waits=[proj_toks[c], cx._sq2_free[c % 2]],
                      inc=cx.s_pool)
        mm0 = t_mm(cx, b0.ap[0:1, 0:128], cx.ones_bf[:], Tb[:, c], c == 0,
                   c == NCH - 1, waits=w0 + [proj_toks[c]],
                   inc=cx.s_pe if c == NCH - 1 else None)
        mm1 = t_mm(cx, b1.ap[0:1, 0:128], cx.ones_bf[:], sq2[:, c % 2],
                   c == 0, c == NCH - 1, waits=w1 + [sqt],
                   inc=cx.s_pe)
        cx._sq2_free[c % 2] = mm1
    b0.set_write(mm0)
    b1.set_write(mm1)
    a0 = tiny(cx, "vector", "tensor_scalar", cx.rs0[0:1, 0:128],
              b0.ap[0:1, 0:128], 1.0 / D, None, ALU.mult, waits=[b0.read()])
    b0.rtoks.append(a0)
    a1 = tiny(cx, "vector", "tensor_scalar", cx.rs1[0:1, 0:128],
              b1.ap[0:1, 0:128], 1.0 / D, None, ALU.mult, waits=[b1.read()])
    b1.rtoks.append(a1)
    if hasattr(cx, "dbg_ln2"):
        d1 = em.emit("scalar", "dma_start",
                     cx.dbg_ln2[0:1, 2 * T + q0:2 * T + q0 + 128],
                     cx.rs0[0:1, 0:128], waits=[a0], inc=cx.s_sc, amt=16)
        d2 = em.emit("scalar", "dma_start",
                     cx.dbg_ln2[0:1, 3 * T + q0:3 * T + q0 + 128],
                     cx.rs1[0:1, 0:128], waits=[a1], inc=cx.s_sc, amt=16)
        em.wait("vector", (d2[0], d2[1], None))
    t1 = tiny(cx, "vector", "tensor_mul", cx.xs[0:1, 0:128],
              cx.rs0[0:1, 0:128], cx.rs0[0:1, 0:128], waits=[a0])
    t1 = tiny(cx, "vector", "tensor_sub", cx.rs1[0:1, 0:128],
              cx.rs1[0:1, 0:128], cx.xs[0:1, 0:128], waits=[a1, t1])
    vt = tiny(cx, "vector", "tensor_scalar_add", cx.rs1[0:1, 0:128],
              cx.rs1[0:1, 0:128], EPS_LN, waits=[t1])
    sqr = tiny(cx, "scalar", "activation", cx.rs1[0:1, 0:128],
               cx.rs1[0:1, 0:128], AF.Sqrt, waits=[vt])
    t1 = tiny(cx, "vector", "reciprocal", cx.xs[0:1, 0:128],
              cx.rs1[0:1, 0:128], waits=[sqr])
    t1 = tiny(cx, "vector", "tensor_mul", cx.rs0[0:1, 0:128],
              cx.rs0[0:1, 0:128], cx.xs[0:1, 0:128], waits=[t1, a0])
    nmu = tiny(cx, "vector", "tensor_scalar_mul", cx.rs0[0:1, 0:128],
               cx.rs0[0:1, 0:128], -1.0, waits=[t1])
    if hasattr(cx, "dbg_ln2"):
        d1 = em.emit("scalar", "dma_start", cx.dbg_ln2[0:1, q0:q0 + 128],
                     cx.xs[0:1, 0:128], waits=[t1], inc=cx.s_sc, amt=16)
        d2 = em.emit("scalar", "dma_start",
                     cx.dbg_ln2[0:1, T + q0:T + q0 + 128],
                     cx.rs0[0:1, 0:128], waits=[nmu], inc=cx.s_sc, amt=16)
        em.wait("vector", (d2[0], d2[1], None))
    bch = banks[3]
    wb = bch.pre_write() + [nmu]
    t_mm(cx, bch.ap[:, 0:128], cx.ones_r[:], cx.xs[0:1, 0:128], True, True,
         waits=wb)
    mmb = t_mm(cx, bch.ap[:, 256:384], cx.ones_r[:], cx.rs0[0:1, 0:128],
               True, True, inc=cx.s_pe)
    bch.set_write(mmb)
    for mc in range(NCH):
        em.emit("vector", "tensor_mul", cx.xs[:, 0:128], Tsc[:, mc],
                bch.ap[:, 0:128], waits=[bch.read()])
        em.emit("vector", "tensor_add", cx.xs[:, 0:128], cx.xs[:, 0:128],
                bch.ap[:, 256:384])
        em.emit("vector", "tensor_scalar", cx.xs[:, 0:128], cx.xs[:, 0:128],
                cx.bias_gam[:, mc:mc + 1], cx.bias_bet[:, mc:mc + 1],
                ALU.mult, ALU.add)
        xt = em.emit("vector", "tensor_add", cx.X[:, mc, q0:q0 + 128],
                     cx.X[:, mc, q0:q0 + 128], cx.xs[:, 0:128], inc=cx.s_dve)
        bch.rtoks.append(xt)


def part_l0f(cx):
    em, nc = cx.em, cx.nc
    if not hasattr(cx, "b1c"):
        cx.b1c = nc.alloc_sbuf_tensor("b1c", [128, 128], F32)
        em.emit("sync", "dma_start", cx.b1c[:], cx.H["h_b1c"],
                inc=cx.s_ld, amt=16)
        cx.gbb = nc.alloc_sbuf_tensor("gbb", [128, 8], F32)
        cx.rwb = nc.alloc_sbuf_tensor("rwb", [128, 8], F32)
        cx.b2fm = nc.alloc_sbuf_tensor("b2fm", [128, 8], F32)
        gb_row = nc.alloc_sbuf_tensor("gb_row", [1, 8], F32)
        rw_row = nc.alloc_sbuf_tensor("rw_row", [1, 8], F32)
        em.emit("sync", "dma_start", gb_row[:], cx.H["h_gateb"],
                inc=cx.s_ld, amt=16)
        tld = em.emit("sync", "dma_start", rw_row[:], cx.H["h_rmsw"],
                      inc=cx.s_ld, amt=16)
        bk = cx.banks[6]
        w = bk.pre_write() + [tld, cx.tok_setup]
        t_mm(cx, bk.ap[:, 0:8], cx.ones_r[:], gb_row[0:1, :], True, True,
             waits=w)
        mm = t_mm(cx, bk.ap[:, 8:16], cx.ones_r[:], rw_row[0:1, :],
                  True, True, inc=cx.s_pe)
        bk.set_write(mm)
        g1 = tiny(cx, "vector", "tensor_copy", cx.gbb[:], bk.ap[:, 0:8],
                  waits=[bk.read()])
        a = tiny(cx, "vector", "tensor_copy", cx.rwb[:], bk.ap[:, 8:16],
                 waits=[g1])
        bk.rtoks.append(a)
        b2stage = nc.alloc_sbuf_tensor("b2stage", [128, 8], F32)
        t0 = em.emit("sync", "dma_start", cx.b2fm[:], cx.H["h_b2fm"][0],
                     inc=cx.s_ld, amt=16)
        cx.tok_l0f = t0
        prev = None
        for e in range(1, 8):
            te = em.emit("sync", "dma_start", b2stage[:], cx.H["h_b2fm"][e],
                         waits=[prev], inc=cx.s_ld, amt=16)
            prev = tiny(cx, "vector", "tensor_add", cx.b2fm[:], cx.b2fm[:],
                        b2stage[:], waits=[te, t0, prev])
        cx.tok_l0f = prev

    ngcol = A(cx, [128, 16], F32)
    with contextlib.ExitStack() as st1:
        old = cx.st
        cx.st = st1
        r_row = A(cx, [1, T], F32)
        b_row = A(cx, [1, T], F32)
        xb = A(cx, [128, NCH, 512], BF16)
        gwb = A(cx, [128, NCH, 8], BF16)
        gcol = A(cx, [128, 8], F32)
        stats_tok = emit_ln_stats(cx, r_row, b_row, xb)
        gw_tok = em.emit("sync", "dma_start", gwb[:],
                         cx.S["gatew"].rearrange("(kc p) e -> p kc e", p=128),
                         waits=[cx.pro.get("gatew")], inc=cx.s_ld, amt=16)
        xb_dma = None
        for tt in range(4):
            t0 = tt * 512
            xn_tok = emit_xn_tile(cx, cx.banks[6], r_row, b_row, t0, 512, xb,
                                  stats_tok, out_free=xb_dma)
            xb_dma = em.emit("sync", "dma_start",
                             cx.S["xb"][tt].rearrange("p (c l) -> p c l", l=512),
                             xb[:], waits=[xn_tok, xb_dma], inc=cx.s_cw,
                             amt=16)
            for q4 in range(4):
                bk = cx.banks[2]
                w = bk.pre_write() + [gw_tok, xn_tok]
                for kc in range(NCH):
                    mm = t_mm(cx, bk.ap[:, 0:8],
                              xb[:, kc, q4 * 128:(q4 + 1) * 128],
                              gwb[:, kc], kc == 0, kc == NCH - 1,
                              waits=w if kc == 0 else (),
                              inc=cx.s_pe if kc == NCH - 1 else None)
                bk.set_write(mm)
                a = tiny(cx, "vector", "tensor_add", gcol[:, 0:8],
                         bk.ap[:, 0:8], cx.gbb[:],
                         waits=[bk.read(), cx.tok_l0f])
                bk.rtoks.append(a)
                t1 = tiny(cx, "vector", "tensor_mul", cx.xs[:, 0:8],
                          gcol[:, 0:8], gcol[:, 0:8], waits=[a])
                t1 = tiny(cx, "vector", "reduce_sum", cx.xs[:, 8:9],
                          cx.xs[:, 0:8], axis=AX.X, waits=[t1])
                vt = tiny(cx, "vector", "tensor_scalar", cx.xs[:, 8:9],
                          cx.xs[:, 8:9], 0.125, EPS_F32, ALU.mult,
                          ALU.add, waits=[t1])
                sqt = tiny(cx, "scalar", "activation", cx.xs[:, 8:9],
                           cx.xs[:, 8:9], AF.Sqrt, waits=[vt])
                t1 = tiny(cx, "vector", "reciprocal", cx.xs[:, 9:10],
                          cx.xs[:, 8:9], waits=[sqt])
                t1 = tiny(cx, "vector", "tensor_scalar_mul", gcol[:, 0:8],
                          gcol[:, 0:8], cx.xs[:, 9:10], waits=[t1])
                t1 = tiny(cx, "vector", "tensor_mul", gcol[:, 0:8],
                          gcol[:, 0:8], cx.rwb[:], waits=[t1])
                t1 = tiny(cx, "vector", "reduce_max", cx.xs[:, 10:11],
                          gcol[:, 0:8], axis=AX.X, waits=[t1])
                nm = tiny(cx, "vector", "tensor_scalar_mul", cx.xs[:, 10:11],
                          cx.xs[:, 10:11], -2.0, waits=[t1])
                ext = tiny(cx, "scalar", "activation", gcol[:, 0:8],
                           gcol[:, 0:8], AF.Exp, bias=cx.xs[:, 10:11],
                           scale=2.0, accum_out=cx.xs[:, 11:12],
                           waits=[nm])
                t1 = tiny(cx, "vector", "reciprocal", cx.xs[:, 12:13],
                          cx.xs[:, 11:12], waits=[ext])
                t2 = tiny(cx, "vector", "tensor_scalar_add", cx.xs[:, 13:14],
                          cx.xs[:, 12:13], 1e-6, waits=[t1])
                t2 = tiny(cx, "vector", "reciprocal", cx.xs[:, 13:14],
                          cx.xs[:, 13:14], waits=[t2])
                gi = tt * 4 + q4
                em.emit("vector", "tensor_mul", ngcol[:, gi:gi + 1],
                        cx.xs[:, 12:13], cx.xs[:, 13:14],
                        waits=[t1, t2], inc=cx.s_dve)
        ng_tok = part_barrier(cx) or cx.em.last["vector"]
        cx.st = old
    part_barrier(cx)

    ysb = A(cx, [128, NCH, T], F32)
    Hb = A(cx, [128, NCH, T], BF16)
    xb2 = A(cx, [128, 2, NCH, 512], BF16)
    cx.dbg = dict(ysb=ysb, Hb=Hb, ng=ngcol)
    w1s = WStream(cx, sems=(cx.s_w0, cx.s_w1))
    w2s = WStream(cx, sems=(cx.s_w2, cx.s_w3))
    xb_free = [None, None]
    ytoks = [[None] * 4 for _ in range(NCH)]
    for hbp in range(16):
        h_last = None
        for ttp in range(2):
            for par in range(2):
                tt = ttp * 2 + par
                xb_free[par] = em.emit(
                    "sync", "dma_start", xb2[:, par],
                    cx.S["xb"][tt].rearrange("p (c l) -> p c l", l=512),
                    waits=[xb_free[par], xb_dma],
                    inc=(cx.s_w4, cx.s_w5)[par], amt=16)
            for mc in range(NCH):
                wi, wtok = w1s.load(cx.S["w1t"], hbp * 8 + mc,
                                    pro=cx.pro.get("w1t"))
                for par in range(2):
                    tt = ttp * 2 + par
                    bk = cx.banks[par]
                    w = bk.pre_write() + [wtok, xb_free[par]]
                    for kc in range(NCH):
                        mm = t_mm(cx, bk.ap, w1s.buf[:, wi, kc],
                                  xb2[:, par, kc], kc == 0, kc == NCH - 1,
                                  waits=w if kc == 0 else (),
                                  inc=cx.s_pe if kc == NCH - 1 else None)
                    bk.set_write(mm)
                    g = em.emit("scalar", "activation",
                                Hb[:, mc, tt * 512:(tt + 1) * 512], bk.ap,
                                AF.Gelu,
                                bias=cx.b1c[:, hbp * 8 + mc:hbp * 8 + mc + 1],
                                waits=[bk.read()], inc=cx.s_act)
                    bk.rtoks.append(g)
                    h_last = g
                w1s.done(wi, mm)
            xb_free = [h_last, h_last]
        for mc in range(NCH):
            wi, wtok = w2s.load(cx.S["w2t"], mc, kc0=hbp * 8,
                                pro=cx.pro.get("w2t"))
            for tt in range(4):
                bk = cx.banks[2 + (tt % 2)]
                w = bk.pre_write() + [wtok, h_last]
                for kc in range(NCH):
                    mm = t_mm(cx, bk.ap, w2s.buf[:, wi, kc],
                              Hb[:, kc, tt * 512:(tt + 1) * 512],
                              kc == 0, kc == NCH - 1,
                              waits=w if kc == 0 else (),
                              inc=cx.s_pe if kc == NCH - 1 else None)
                bk.set_write(mm)
                dst = ysb[:, mc, tt * 512:(tt + 1) * 512]
                if hbp == 0:
                    a = em.emit("vector", "tensor_copy", dst, bk.ap,
                                waits=[bk.read()], inc=cx.s_dve)
                else:
                    a = em.emit("vector", "tensor_add", dst, dst, bk.ap,
                                waits=[bk.read(), ytoks[mc][tt]],
                                inc=cx.s_dve)
                bk.rtoks.append(a)
                ytoks[mc][tt] = a
            w2s.done(wi, ytoks[mc][3])
    for tt in range(4):
        t0 = tt * 512
        tb = cx.banks[6]
        wtb = tb.pre_write()
        for q4 in range(4):
            gi = tt * 4 + q4
            tr = em.emit("tensor", "transpose",
                         tb.ap[0:1, q4 * 128:(q4 + 1) * 128],
                         ngcol[:, gi:gi + 1], cx.ident_f[:],
                         waits=(wtb if q4 == 0 else []) + [ng_tok],
                         inc=cx.s_pe if q4 == 3 else None)
        tb.set_write(tr)
        rcp = em.emit("vector", "tensor_copy", cx.rs0[0:1, :],
                      tb.ap[0:1, 0:512], waits=[tb.read()], inc=cx.s_dve)
        tb.rtoks.append(rcp)
        bk2 = cx.banks[5]
        w = bk2.pre_write() + [rcp]
        mm = t_mm(cx, bk2.ap, cx.ones_r[:], cx.rs0[0:1, :], True, True,
                  waits=w, inc=cx.s_pe)
        bk2.set_write(mm)
        ngb = em.emit("vector", "tensor_copy", cx.xsB[:], bk2.ap,
                      waits=[bk2.read()], inc=cx.s_dve)
        bk2.rtoks.append(ngb)
        for mc in range(NCH):
            em.emit("vector", "tensor_scalar_add", cx.xs[:],
                    ysb[:, mc, t0:t0 + 512], cx.b2fm[:, mc:mc + 1],
                    waits=[ytoks[mc][tt], cx.tok_l0f])
            em.emit("vector", "tensor_mul", cx.xs[:], cx.xs[:], cx.xsB[:])
            em.emit("vector", "tensor_add", cx.X[:, mc, t0:t0 + 512],
                    cx.X[:, mc, t0:t0 + 512], cx.xs[:], inc=cx.s_dve)


def part_l1a(cx):
    em, nc = cx.em, cx.nc
    if not hasattr(cx, "bias_bo"):
        cx.bias_bo = nc.alloc_sbuf_tensor("bias_bo", [128, 8], F32)
        cx.bias_bq = nc.alloc_sbuf_tensor("bias_bq", [128, 8], F32)
        cx.bias_bk = nc.alloc_sbuf_tensor("bias_bk", [128, 8], F32)
        em.emit("sync", "dma_start", cx.bias_bo[:], cx.H["h_bo"],
                inc=cx.s_ld, amt=16)
        em.emit("sync", "dma_start", cx.bias_bq[:], cx.H["h_bq"],
                inc=cx.s_ld, amt=16)
        cx.tok_l1bias = em.emit("sync", "dma_start", cx.bias_bk[:],
                                cx.H["h_bk"], inc=cx.s_ld, amt=16)
        cx.ct = nc.alloc_sbuf_tensor("ct", [128, 16, 32], F32)
        cx.stb = nc.alloc_sbuf_tensor("stb", [128, 16, 32], F32)
        em.emit("sync", "dma_start", cx.ct[:], cx.H["h_ct"],
                inc=cx.s_ld, amt=16)
        em.emit("sync", "dma_start", cx.stb[:], cx.H["h_st"],
                inc=cx.s_ld, amt=16)
        cx.sel1 = nc.alloc_sbuf_tensor("sel1", [128, 128], BF16)
        cx.sel2 = nc.alloc_sbuf_tensor("sel2", [128, 128], BF16)
        em.emit("gpsimd", "dma_start", cx.sel1[:], cx.H["h_SEL1"],
                inc=cx.s_g3, amt=16)
        cx.tok_l1tab = em.emit("gpsimd", "dma_start", cx.sel2[:],
                               cx.H["h_SEL2"], inc=cx.s_g3, amt=16)

    Q = A(cx, [128, NCH, T], BF16)
    K_ = A(cx, [128, NCH, T], BF16)
    r_row = A(cx, [1, T], F32)
    b_row = A(cx, [1, T], F32)
    V = A(cx, [128, 16, 1024], BF16)

    with contextlib.ExitStack() as st2:
        old = cx.st
        cx.st = st2
        xb = A(cx, [128, NCH, 512], BF16)
        stats_tok = emit_ln_stats(cx, r_row, b_row, xb)
        with contextlib.ExitStack() as st3:
            old3 = cx.st
            cx.st = st3
            ws = WStream(cx, sems=(cx.s_w0, cx.s_w1))
            TTs = [A(cx, [128, 512], BF16), A(cx, [128, 512], BF16)]
            U1 = A(cx, [128, 512], BF16)
            U2 = A(cx, [128, 512], BF16)
            qk_tok = emit_qk_fmajor(
                cx, ws, Q, K_, r_row, b_row, stats_tok, xb,
                rope_bufs=(cx.bias_bq, cx.bias_bk,
                           (cx.H["h_T1"], cx.H["h_T2"]), TTs, U1, U2),
                pros=(cx.pro.get("wq"), cx.pro.get("wk")),
                scrs=(cx.S["wq"], cx.S["wk"]))
            cx.st = old3
        part_barrier(cx)
        bvb = A(cx, [128, 1024], F32)
        bvt0 = em.emit("sync", "dma_start", cx.rs0[:], cx.H["h_bv"][0:1, 0:512],
                       inc=cx.s_ld, amt=16)
        bvt1 = em.emit("sync", "dma_start", cx.rs1[:], cx.H["h_bv"][0:1, 512:1024],
                       inc=cx.s_ld, amt=16)
        for half, (rsx, bvt) in enumerate(((cx.rs0, bvt0), (cx.rs1, bvt1))):
            bk = cx.banks[6]
            w = bk.pre_write() + [bvt, cx.tok_setup]
            mm = t_mm(cx, bk.ap, cx.ones_r[:], rsx[0:1, :], True, True,
                      waits=w, inc=cx.s_pe)
            bk.set_write(mm)
            a = em.emit("vector", "tensor_copy",
                        bvb[:, half * 512:(half + 1) * 512], bk.ap,
                        waits=[bk.read()], inc=cx.s_dve)
            bk.rtoks.append(a)
        wvbuf = A(cx, [128, 1, NCH, 256], BF16)
        v_tok = emit_v_tmajor(cx, cx.S["wvT"], V, r_row, b_row, stats_tok,
                              xb, wvbuf, rope_bufs=(bvb, cx.ct, cx.stb),
                              pro=cx.pro.get("wvT"), nbuf=1,
                              sems=(cx.s_w2,))
        cx.st = old
    part_barrier(cx)

    P = A(cx, [128, T], BF16)
    PT = [A(cx, [128, 512], BF16) for _ in range(2)]
    AO = A(cx, [128, NCH, 128], BF16)
    rc = A(cx, [128, 8], F32)
    M3 = A(cx, [128, 128], BF16)
    M4 = A(cx, [128, 128], BF16)
    T34 = [[A(cx, [128, 512], BF16), A(cx, [128, 512], BF16)],
           [A(cx, [128, 512], BF16), A(cx, [128, 512], BF16)]]
    wo = WStream(cx, nbuf=1, sems=(cx.s_w3,))
    t34_toks = [None, None]

    def load_t34(par, t0, pre):
        em.emit("gpsimd", "dma_start", T34[0][par][:],
                cx.H["h_T3"][:, t0:t0 + 512], waits=[pre], inc=cx.s_g3,
                amt=16)
        return em.emit("gpsimd", "dma_start", T34[1][par][:],
                       cx.H["h_T4"][:, t0:t0 + 512], inc=cx.s_g3, amt=16)

    t34_toks[0] = load_t34(0, 0, qk_tok)
    t34_toks[1] = load_t34(1, 512, v_tok)
    cx.dbg = dict(Q=Q, K=K_, V=V, P=P, AO=AO, rc=rc)
    last_ao = [None, None]
    for qt, q0, ao_tok in emit_attn_core(cx, Q, K_, V, AO, P, PT, rc,
                                         qk_tok, v_tok, causal=False,
                                         split_pv=True,
                                         T34=(T34[0], T34[1]), M3=M3, M4=M4,
                                         t34_toks=t34_toks):
        tts = qt // 4
        last_ao[tts % 2] = ao_tok
        if qt % 4 == 3 and qt < 15:
            nxt = tts + 2
            if nxt < 4:
                t34_toks[nxt % 2] = load_t34(nxt % 2, nxt * 512,
                                             last_ao[nxt % 2])
        for mc in range(NCH):
            wi, wtok = wo.load(cx.S["wo"], mc, pro=cx.pro.get("wo"))
            bk = cx.banks[mc % 2]
            w = bk.pre_write() + [wtok, ao_tok, cx.tok_l1bias]
            for kc in range(NCH):
                mm = t_mm(cx, bk.ap[:, 0:128], wo.buf[:, wi, kc],
                          AO[:, kc, :], kc == 0, kc == NCH - 1,
                          waits=w if kc == 0 else (),
                          inc=cx.s_pe if kc == NCH - 1 else None)
            bk.set_write(mm)
            wo.done(wi, mm)
            xt = em.emit("vector", "scalar_tensor_tensor",
                         cx.X[:, mc, q0:q0 + 128], bk.ap[:, 0:128],
                         cx.bias_bo[:, mc:mc + 1], cx.X[:, mc, q0:q0 + 128],
                         ALU.add, ALU.add, waits=[bk.read(), cx.tok_x],
                         inc=cx.s_dve)
            bk.rtoks.append(xt)


def part_l1f(cx):
    em, nc = cx.em, cx.nc
    if not hasattr(cx, "b1f"):
        cx.b1f = nc.alloc_sbuf_tensor("b1f", [128, 32], F32)
        em.emit("sync", "dma_start", cx.b1f[:], cx.H["h_b1f"],
                inc=cx.s_ld, amt=16)
        cx.b2f = nc.alloc_sbuf_tensor("b2f", [128, 8], F32)
        cx.tok_l1f = em.emit("sync", "dma_start", cx.b2f[:], cx.H["h_b2f"],
                             inc=cx.s_ld, amt=16)

    r_row = A(cx, [1, T], F32)
    b_row = A(cx, [1, T], F32)
    xb = A(cx, [128, NCH, 512], BF16)
    stats_tok = emit_ln_stats(cx, r_row, b_row, xb)

    Hb = A(cx, [128, 32, 1024], BF16)
    xbf = A(cx, [128, NCH, 1024], BF16)
    w1s = WStream(cx, sems=(cx.s_w0, cx.s_w1))
    w2s = WStream(cx, sems=(cx.s_w2, cx.s_w3))
    cx.dbg = dict(Hb=Hb, xbf=xbf)

    for ft in range(2):
        f0 = ft * 1024
        emit_xn_tile(cx, cx.banks[6], r_row, b_row, f0, 512,
                     xbf[:, :, 0:512], stats_tok)
        xn_tok = emit_xn_tile(cx, cx.banks[6], r_row, b_row, f0 + 512, 512,
                              xbf[:, :, 512:1024], stats_tok)
        h_last = None
        for mc in range(32):
            wi, wtok = w1s.load(cx.S["w1f"], mc, pro=cx.pro.get("w1f"))
            for g2 in range(2):
                bk = cx.banks[g2 % 2]
                w = bk.pre_write() + [wtok, xn_tok]
                for kc in range(NCH):
                    mm = t_mm(cx, bk.ap, w1s.buf[:, wi, kc],
                              xbf[:, kc, g2 * 512:(g2 + 1) * 512],
                              kc == 0, kc == NCH - 1,
                              waits=w if kc == 0 else (),
                              inc=cx.s_pe if kc == NCH - 1 else None)
                bk.set_write(mm)
                h_last = em.emit("scalar", "activation",
                                 Hb[:, mc, g2 * 512:(g2 + 1) * 512], bk.ap,
                                 AF.Gelu, bias=cx.b1f[:, mc:mc + 1],
                                 waits=[bk.read(), cx.tok_l1f],
                                 inc=cx.s_act)
                bk.rtoks.append(h_last)
            w1s.done(wi, mm)
        for mc in range(NCH):
            pw = {g2: cx.banks[2 + g2].pre_write() for g2 in range(2)}
            for kq in range(4):
                wi, wtok = w2s.load(cx.S["w2f"], mc, kc0=kq * 8,
                                    pro=cx.pro.get("w2f"))
                for g2 in range(2):
                    bk = cx.banks[2 + g2]
                    w = (pw[g2] if kq == 0 else []) + [wtok, h_last]
                    for kc in range(NCH):
                        mm = t_mm(cx, bk.ap, w2s.buf[:, wi, kc],
                                  Hb[:, kq * 8 + kc, g2 * 512:(g2 + 1) * 512],
                                  kq == 0 and kc == 0,
                                  kq == 3 and kc == NCH - 1,
                                  waits=w if kc == 0 else (),
                                  inc=cx.s_pe if kc == NCH - 1 else None)
                    if kq == 3:
                        bk.set_write(mm)
                        sl = slice(f0 + g2 * 512, f0 + (g2 + 1) * 512)
                        xt = em.emit("vector", "scalar_tensor_tensor",
                                     cx.X[:, mc, sl], bk.ap,
                                     cx.b2f[:, mc:mc + 1], cx.X[:, mc, sl],
                                     ALU.add, ALU.add,
                                     waits=[bk.read(), cx.tok_x],
                                     inc=cx.s_dve)
                        bk.rtoks.append(xt)
                w2s.done(wi, mm)


PART_EMITTERS = {"l0a": part_l0a, "l0f": part_l0f, "l1a": part_l1a,
                 "l1f": part_l1f}

PART_GROUPS = [["l0a", "l0f", "l1a", "l1f"]]


def kernel(**inputs):
    common = prep_host(inputs)
    x = np.asarray(inputs["L0_attn_target"], np.float32)
    xs = x.reshape(NCORE, NSEQ, T, NCH, 128)
    cur = [_x_to_dev(xs[c]) for c in range(NCORE)]

    for parts in PART_GROUPS:
        nc = build_program(parts)
        keys = [k for p in parts for k in PART_HOST[p]]
        in_maps = []
        for c in range(NCORE):
            m = {k: common[k] for k in keys}
            m["x_in"] = cur[c]
            in_maps.append(m)
        res = run_bass_kernel_spmd(nc, in_maps, core_ids=list(range(NCORE)))
        cur = [np.asarray(res.results[c]["x_out"]) for c in range(NCORE)]

    out = np.empty((B, T, D), np.float32)
    for c in range(NCORE):
        out[c * NSEQ:(c + 1) * NSEQ] = _x_from_dev(cur[c])
    return out


# version 14
# speedup vs baseline: 1.9172x; 1.9172x over previous
import contextlib
import numpy as np
import ml_dtypes
import concourse.bass as bass
import concourse.mybir as mybir
from concourse.bass_utils import run_bass_kernel_spmd

F32 = mybir.dt.float32
BF16 = mybir.dt.bfloat16
AF = mybir.ActivationFunctionType
ALU = mybir.AluOpType
AX = mybir.AxisListType

B, T, D = 32, 2048, 1024
NCORE, NSEQ, NCH, NH, HD = 8, 4, 8, 16, 64
HF0, FF1 = 16384, 4096
EPS_LN = 1e-5
EPS_F32 = float(np.finfo(np.float32).eps)
NEGBIG = -1e9

ENGINES = ("sync", "tensor", "vector", "scalar", "gpsimd")


class Sem:
    __slots__ = ("h", "n", "name")
    def __init__(self, nc, name):
        self.h = nc.alloc_semaphore(name)
        self.n = 0
        self.name = name


class Em:
    def __init__(self, nc):
        self.nc = nc
        self.q = {e: [] for e in ENGINES}
        self.waited = {e: {} for e in ENGINES}
        self.last = {e: None for e in ENGINES}
        self.pending = {e: 0 for e in ENGINES}
        self.bar = []
        self.dma_prev = {}
        self.dma_q = {}

    def emit(self, eng, method, *args, waits=(), inc=None, amt=1, **kwargs):
        real = []
        w = self.waited[eng]
        for tok in list(waits) + self.bar:
            if tok is None:
                continue
            sem, cnt, src = tok
            if src == eng or w.get(sem, 0) >= cnt:
                continue
            w[sem] = cnt
            real.append((sem.h, cnt))
        tok = None
        if method is not None:
            if inc is not None:
                if amt == 16:
                    assert self.dma_q.setdefault(inc, eng) == eng, inc
                    prev = self.dma_prev.get(inc)
                    if prev is not None and w.get(inc, 0) < prev:
                        w[inc] = prev
                        real.append((inc.h, prev))
                    inc.n += amt
                    self.dma_prev[inc] = inc.n
                    tok = (inc, inc.n, ("dma", eng))
                else:
                    inc.n += amt
                    tok = (inc, inc.n, eng)
                self.last[eng] = tok
                self.pending[eng] = 0
            else:
                self.pending[eng] += 1
        self.q[eng].append((method, args, kwargs, real,
                            inc.h if inc else None, amt))
        return tok

    def wait(self, eng, *toks):
        self.emit(eng, None, waits=[t for t in toks if t is not None])

    def flush(self, blk):
        for e in ENGINES:
            items = self.q[e]
            if not items:
                continue
            self.q[e] = []

            def body(eng, items=items):
                for method, args, kwargs, waits, inch, amt in items:
                    for semh, cnt in waits:
                        eng.wait_ge(semh, cnt)
                    if method is None:
                        continue
                    ins = getattr(eng, method)(*args, **kwargs)
                    if inch is not None:
                        ins.then_inc(inch, amt)
            getattr(blk, e)(body)


class Bank:
    __slots__ = ("ap", "wtok", "rtoks")
    def __init__(self, ap):
        self.ap = ap
        self.wtok = None
        self.rtoks = []

    def pre_write(self):
        toks = self.rtoks
        self.rtoks = []
        return list(toks)

    def set_write(self, tok):
        self.wtok = tok

    def read(self, tok=None):
        if tok is not None:
            self.rtoks.append(tok)
        return self.wtok



def _tile_lhsT(w_kf):
    K, F = w_kf.shape
    t = w_kf.reshape(K // 128, 128, F // 128, 128)
    return np.ascontiguousarray(t.transpose(2, 1, 0, 3).reshape(F // 128, 128, K))


def _flat_blocks(a):
    f = np.ascontiguousarray(a).reshape(-1)
    assert f.size % (128 * 4096) == 0, f.size
    return np.ascontiguousarray(f.reshape(-1, 128, 4096))


def _fmajor_bias(b):
    return np.ascontiguousarray(np.asarray(b, np.float32).reshape(-1, 128).T)


def _rope_tables():
    pos = np.arange(T, dtype=np.float32)[:, None]
    dim_pair = np.arange(0, HD, 2, dtype=np.float32)[None, :]
    theta = (10000.0 ** (-dim_pair.astype(np.float64) / HD)).astype(np.float32)
    freqs = (pos * theta).astype(np.float32)
    rope = np.concatenate([np.cos(freqs), np.sin(freqs)], -1).astype(np.float32)
    return np.ascontiguousarray(rope[:, 0::2]), np.ascontiguousarray(rope[:, 1::2])


def prep_host(inputs):
    f = lambda k: np.asarray(inputs[k], np.float32)
    wqkv, wproj = f("L0_attn_w_qkv"), f("L0_attn_w_proj")
    w1, w2 = f("L0_ffn_w1"), f("L0_ffn_w2")
    Wq, Wk, Wv, Wo = f("L1_attn_Wq"), f("L1_attn_Wk"), f("L1_attn_Wv"), f("L1_attn_Wo")
    W1f, W2f = f("L1_ffn_W1"), f("L1_ffn_W2")
    perm_h = np.concatenate([np.arange(0, HD, 2), np.arange(1, HD, 2)])
    perm = (np.arange(NH)[:, None] * HD + perm_h[None, :]).reshape(-1)

    m = {}
    m["h_wqk"] = _flat_blocks(_tile_lhsT(wqkv[:, 0:2048]))
    wv_ = wqkv[:, 2048:3072].reshape(8, 128, 4, 256).transpose(2, 1, 0, 3)
    m["h_wv"] = _flat_blocks(wv_.reshape(4, 128, 2048))
    m["h_wproj"] = _flat_blocks(_tile_lhsT(wproj))
    m["h_bproj"] = _fmajor_bias(f("L0_attn_b_proj"))
    m["h_gam"] = _fmajor_bias(f("L0_attn_ln_gamma"))
    m["h_bet"] = _fmajor_bias(f("L0_attn_ln_beta"))
    m["h_gatew"] = np.ascontiguousarray(f("L0_ffn_gate_w").T)
    m["h_gateb"] = f("L0_ffn_gate_b").reshape(1, 8)
    m["h_rmsw"] = f("L0_ffn_rms_w").reshape(1, 8)
    w1cat = np.ascontiguousarray(w1.transpose(1, 0, 2).reshape(1024, HF0))
    m["h_w1t"] = _flat_blocks(_tile_lhsT(w1cat))
    m["h_w2t"] = _flat_blocks(_tile_lhsT(w2.reshape(HF0, 1024)))
    m["h_b1c"] = _fmajor_bias(f("L0_ffn_b1").reshape(-1))
    m["h_b2fm"] = np.ascontiguousarray(
        np.stack([_fmajor_bias(f("L0_ffn_b2")[e]) for e in range(8)]))
    m["h_wq"] = _flat_blocks(_tile_lhsT(np.ascontiguousarray(Wq.T)[:, perm]))
    m["h_wk"] = _flat_blocks(_tile_lhsT(np.ascontiguousarray(Wk.T)[:, perm]))
    wvT_ = np.ascontiguousarray(Wv.T).reshape(8, 128, 4, 256).transpose(2, 1, 0, 3)
    m["h_wvT"] = _flat_blocks(wvT_.reshape(4, 128, 2048))
    m["h_wo"] = _flat_blocks(_tile_lhsT(np.ascontiguousarray(Wo.T)))
    m["h_bq"] = _fmajor_bias(f("L1_attn_bq")[perm])
    m["h_bk"] = _fmajor_bias(f("L1_attn_bk")[perm])
    m["h_bv"] = f("L1_attn_bv").reshape(1, 1024)
    m["h_bo"] = _fmajor_bias(f("L1_attn_bo"))
    m["h_w1f"] = _flat_blocks(_tile_lhsT(np.ascontiguousarray(W1f.T)))
    m["h_w2f"] = _flat_blocks(_tile_lhsT(np.ascontiguousarray(W2f.T)))
    m["h_b1f"] = _fmajor_bias(f("L1_ffn_b1"))
    m["h_b2f"] = _fmajor_bias(f("L1_ffn_b2"))
    c, s = _rope_tables()
    cF, sF = np.ascontiguousarray(c.T), np.ascontiguousarray(s.T)
    stk = lambda a, b: np.ascontiguousarray(np.concatenate([a, b, a, b], 0))
    m["h_T1"] = stk(cF, -sF)
    m["h_T2"] = stk(sF, cF)
    m["h_T3"] = stk(-cF, sF)
    m["h_T4"] = stk(-sF, -cF)
    m["h_ct"] = np.ascontiguousarray(c.reshape(16, 128, 32).transpose(1, 0, 2))
    m["h_st"] = np.ascontiguousarray(s.reshape(16, 128, 32).transpose(1, 0, 2))
    s1 = np.zeros((128, 128), np.float32)
    s2 = np.zeros((128, 128), np.float32)
    for blk in (0, 64):
        for j in range(32):
            s1[blk + j, blk + j] = 1.0
            s1[blk + 32 + j, blk + j] = 1.0
            s2[blk + j, blk + 32 + j] = 1.0
            s2[blk + 32 + j, blk + 32 + j] = 1.0
    m["h_SEL1"] = s1
    m["h_SEL2"] = s2
    return m


def _x_to_dev(xcore):
    return np.ascontiguousarray(xcore.transpose(0, 3, 2, 1))


def _x_from_dev(xdev):
    return np.ascontiguousarray(np.asarray(xdev).transpose(0, 3, 2, 1)).reshape(NSEQ, T, D)


HOST_SHAPES = {
    "h_wqk": [4, 128, 4096], "h_wv": [2, 128, 4096], "h_wproj": [2, 128, 4096],
    "h_bproj": [128, 8], "h_gam": [128, 8], "h_bet": [128, 8],
    "h_gatew": [1024, 8], "h_gateb": [1, 8], "h_rmsw": [1, 8],
    "h_w1t": [32, 128, 4096], "h_w2t": [32, 128, 4096],
    "h_b1c": [128, 128], "h_b2fm": [8, 128, 8],
    "h_wq": [2, 128, 4096], "h_wk": [2, 128, 4096], "h_wvT": [2, 128, 4096],
    "h_wo": [2, 128, 4096], "h_bq": [128, 8], "h_bk": [128, 8],
    "h_bv": [1, 1024], "h_bo": [128, 8],
    "h_w1f": [8, 128, 4096], "h_w2f": [8, 128, 4096],
    "h_b1f": [128, 32], "h_b2f": [128, 8],
    "h_T1": [128, 2048], "h_T2": [128, 2048], "h_T3": [128, 2048],
    "h_T4": [128, 2048], "h_ct": [128, 16, 32], "h_st": [128, 16, 32],
    "h_SEL1": [128, 128], "h_SEL2": [128, 128],
}

PART_HOST = {
    "l0a": ["h_wqk", "h_wv", "h_wproj", "h_bproj", "h_gam", "h_bet"],
    "l0f": ["h_gatew", "h_gateb", "h_rmsw", "h_w1t", "h_w2t", "h_b1c",
            "h_b2fm"],
    "l1a": ["h_wq", "h_wk", "h_wvT", "h_wo", "h_bq", "h_bk", "h_bv", "h_bo",
            "h_T1", "h_T2", "h_T3", "h_T4", "h_ct", "h_st",
            "h_SEL1", "h_SEL2"],
    "l1f": ["h_w1f", "h_w2f", "h_b1f", "h_b2f"],
}


class Ctx:
    pass


def build_program(parts, debug_taps=None, n_seq=NSEQ):
    nc = bass.Bass("TRN2", target_bir_lowering=False)
    em = Em(nc)
    cx = Ctx()
    cx.nc, cx.em = nc, em
    cx.debug_taps = debug_taps or []
    need = set(parts)

    def din(name, shape):
        return nc.dram_tensor(name, list(shape), F32, kind="ExternalInput").ap()

    cx.x_in = din("x_in", [NSEQ, 128, NCH, T])
    cx.H = {k: din(k, HOST_SHAPES[k]) for p in parts for k in PART_HOST[p]}
    cx.x_out = nc.dram_tensor("x_out", [NSEQ, 128, NCH, T], F32,
                              kind="ExternalOutput").ap()

    def scr(name, shape):
        return nc.dram_tensor(name, list(shape), BF16, kind="Internal").ap()

    S = {}
    if "l0a" in need:
        S["wqk"] = scr("s_wqk", [16, 128, 1024])
        S["wv"] = scr("s_wv", [4, 128, 2048])
        S["wproj"] = scr("s_wproj", [8, 128, 1024])
    if "l0f" in need:
        S["gatew"] = scr("s_gatew", [1024, 8])
        S["w1t"] = scr("s_w1t", [128, 128, 1024])
        S["w2t"] = scr("s_w2t", [8, 128, 16384])
        S["xb"] = scr("s_xb", [4, 128, NCH * 512])
    if "l1a" in need:
        S["wq"] = scr("s_wq", [8, 128, 1024])
        S["wk"] = scr("s_wk", [8, 128, 1024])
        S["wvT"] = scr("s_wvT", [4, 128, 2048])
        S["wo"] = scr("s_wo", [8, 128, 1024])
    if "l1f" in need:
        S["w1f"] = scr("s_w1f", [32, 128, 1024])
        S["w2f"] = scr("s_w2f", [8, 128, 4096])
    cx.S = S

    cx.all_sems = []
    for nm in ("s_pe", "s_dve", "s_act", "s_pool", "s_ld", "s_sc", "s_g",
               "s_g2", "s_g3", "s_x", "s_cw", "s_w0", "s_w1", "s_w2",
               "s_w3", "s_w4", "s_w5"):
        s = Sem(nc, nm)
        setattr(cx, nm, s)
        cx.all_sems.append(s)

    S4 = nc.alloc_psum_tensor("psum_S", [128, 2048], F32)
    C2 = nc.alloc_psum_tensor("psum_C", [128, 1024], F32)
    D1 = nc.alloc_psum_tensor("psum_D", [128, 512], F32)
    Tp = nc.alloc_psum_tensor("psum_T", [128, 1024], BF16)
    cx.banks = ([Bank(S4[:, i * 512:(i + 1) * 512]) for i in range(4)]
                + [Bank(C2[:, i * 512:(i + 1) * 512]) for i in range(2)]
                + [Bank(D1[:]), Bank(Tp[:])])
    cx.S4 = S4

    cx.X = nc.alloc_sbuf_tensor("X", [128, NCH, T], F32)
    cx.ones_bf = nc.alloc_sbuf_tensor("ones_bf", [128, 1], BF16)
    cx.ones_f = nc.alloc_sbuf_tensor("ones_f", [128, 1], F32)
    cx.ones_r = nc.alloc_sbuf_tensor("ones_r", [1, 128], F32)
    cx.ident_bf = nc.alloc_sbuf_tensor("ident_bf", [128, 128], BF16)
    cx.ident_f = nc.alloc_sbuf_tensor("ident_f", [128, 128], F32)
    cx.mask = nc.alloc_sbuf_tensor("mask", [128, 128], BF16)
    cx.rs0 = nc.alloc_sbuf_tensor("rs0", [1, 512], F32)
    cx.rs1 = nc.alloc_sbuf_tensor("rs1", [1, 512], F32)
    cx.xs = nc.alloc_sbuf_tensor("xs", [128, 512], F32)
    cx.xsB = nc.alloc_sbuf_tensor("xsB", [128, 512], F32)

    emit_setup(cx)
    emit_prologue(cx, need)

    for seq in range(n_seq):
        cx.seq = seq
        emit_x_load(cx, seq)
        for p in parts:
            part_barrier(cx)
            with contextlib.ExitStack() as st:
                cx.st = st
                PART_EMITTERS[p](cx)
                part_barrier(cx)
                for tp, name, fn in cx.debug_taps:
                    if tp == p and seq == n_seq - 1:
                        emit_debug_tap(cx, name, fn)
                part_barrier(cx)
        emit_x_store(cx, seq, seq == n_seq - 1)

    part_barrier(cx)
    for e in ENGINES:
        cx.em.wait("sync", cx.em.last[e])
    for s in cx.all_sems:
        if s.n:
            cx.em.wait("sync", (s, s.n, None))
    for s in cx.all_sems:
        em.emit("sync", "sem_clear", s.h)

    with nc.Block() as blk:
        em.flush(blk)
    return nc


_anon = [0]

def A(cx, shape, dtype):
    _anon[0] += 1
    return cx.st.enter_context(
        cx.nc.sbuf_tensor(f"t{_anon[0]}", list(shape), dtype))


def part_barrier(cx):
    em = cx.em
    if em.pending["vector"]:
        em.emit("vector", "engine_nop", inc=cx.s_dve)
    if em.pending["gpsimd"]:
        em.emit("gpsimd", "engine_nop", inc=cx.s_pool)
    if em.pending["scalar"]:
        em.emit("scalar", "copy", cx.rs1[0:1, 0:1], cx.rs1[0:1, 0:1],
                inc=cx.s_act)
    assert em.pending["tensor"] == 0, em.pending["tensor"]
    assert em.pending["sync"] == 0
    em.bar = [t for t in (em.last[e] for e in ENGINES) if t is not None]


def t_mm(cx, out_ap, lhsT, rhs, start, stop, waits=(), inc=None, tp=None):
    kw = {"tile_position": tp} if tp is not None else {}
    return cx.em.emit("tensor", "matmul", out_ap, lhsT, rhs, start=start,
                      stop=stop, waits=waits, inc=inc, **kw)


def tiny(cx, eng, method, *args, waits=(), **kw):
    sem = {"vector": cx.s_dve, "scalar": cx.s_act, "gpsimd": cx.s_pool}[eng]
    t = cx.em.emit(eng, method, *args, waits=waits, inc=sem, **kw)
    return (t[0], t[1], None)


def emit_setup(cx):
    em, nc = cx.em, cx.nc
    iota_cm = nc.sbuf_tensor("iota_row", [128, 128], F32)
    iota_row = iota_cm.__enter__()
    pc_cm = nc.sbuf_tensor("part_col", [128, 1], F32)
    part_col = pc_cm.__enter__()
    em.emit("gpsimd", "iota", iota_row[:], pattern=[[1, 128]], base=0,
            channel_multiplier=0, allow_small_or_imprecise_dtypes=True)
    tok = em.emit("gpsimd", "iota", part_col[:], pattern=[[1, 1]], base=0,
                  channel_multiplier=1, allow_small_or_imprecise_dtypes=True,
                  inc=cx.s_pool)
    em.emit("vector", "tensor_scalar", cx.ident_f[:], iota_row[:],
            part_col[:, 0:1], None, ALU.is_equal, waits=[tok])
    em.emit("vector", "tensor_copy", cx.ident_bf[:], cx.ident_f[:])
    em.emit("vector", "tensor_scalar", cx.mask[:], iota_row[:],
            part_col[:, 0:1], float(NEGBIG), ALU.is_gt, ALU.mult)
    em.emit("vector", "memset", cx.ones_bf[:], 1.0)
    em.emit("vector", "memset", cx.ones_f[:], 1.0)
    em.emit("vector", "memset", cx.rs1[:], 0.0)
    cx.tok_setup = em.emit("vector", "memset", cx.ones_r[:], 1.0, inc=cx.s_dve)
    pc_cm.__exit__(None, None, None)
    iota_cm.__exit__(None, None, None)


def _flat_view(scr_ap):
    dims = " ".join(f"a{i}" for i in range(scr_ap.ndim))
    flat = scr_ap.rearrange(f"{dims} -> ({dims})")
    return flat.rearrange("(n p c) -> n p c", p=128, c=4096)


def emit_prologue(cx, need):
    em, nc = cx.em, cx.nc
    with contextlib.ExitStack() as st:
        st_b = st.enter_context(nc.sbuf_tensor([128, 2, 4096], BF16))
        free = [None, None]

        def cast(host_ap, scr_ap):
            blocks = _flat_view(scr_ap)
            for i in range(host_ap.shape[0]):
                par = i % 2
                ld = em.emit("gpsimd", "dma_start", st_b[:, par], host_ap[i],
                             waits=[free[par]],
                             inc=(cx.s_g, cx.s_g2)[par], amt=16)
                free[par] = em.emit("sync", "dma_start", blocks[i], st_b[:, par],
                                    waits=[ld], inc=cx.s_ld, amt=16)
            return free[(host_ap.shape[0] - 1) % 2]

        toks = {}
        if "l0a" in need:
            toks["wqk"] = cast(cx.H["h_wqk"], cx.S["wqk"])
            toks["wv"] = cast(cx.H["h_wv"], cx.S["wv"])
            toks["wproj"] = cast(cx.H["h_wproj"], cx.S["wproj"])
        if "l0f" in need:
            gv = cx.H["h_gatew"].rearrange("(a p) e -> p a e", p=128)
            sv = cx.S["gatew"].rearrange("(a p) e -> p a e", p=128)
            stg = st_b[:, 0, 0:64].rearrange("p (a e) -> p a e", a=8)
            ld = em.emit("gpsimd", "dma_start", stg, gv,
                         waits=[free[0]], inc=cx.s_g, amt=16)
            free[0] = toks["gatew"] = em.emit(
                "sync", "dma_start", sv, stg, waits=[ld],
                inc=cx.s_ld, amt=16)
            toks["w1t"] = cast(cx.H["h_w1t"], cx.S["w1t"])
            toks["w2t"] = cast(cx.H["h_w2t"], cx.S["w2t"])
        if "l1a" in need:
            toks["wq"] = cast(cx.H["h_wq"], cx.S["wq"])
            toks["wk"] = cast(cx.H["h_wk"], cx.S["wk"])
            toks["wvT"] = cast(cx.H["h_wvT"], cx.S["wvT"])
            toks["wo"] = cast(cx.H["h_wo"], cx.S["wo"])
        if "l1f" in need:
            toks["w1f"] = _ternary(cx, st, cx.H["h_w1f"], cx.S["w1f"], "w1f")
            toks["w2f"] = _ternary(cx, st, cx.H["h_w2f"], cx.S["w2f"], "w2f")
        cx.pro = toks
        part_barrier_prologue(cx)


def part_barrier_prologue(cx):
    em = cx.em
    if em.pending["vector"]:
        em.emit("vector", "engine_nop", inc=cx.s_dve)
    if em.pending["gpsimd"]:
        em.emit("gpsimd", "engine_nop", inc=cx.s_pool)
    em.bar = [t for t in (em.last[e] for e in ENGINES) if t is not None]


def _ternary(cx, st_outer, host_ap, scr_ap, name):
    em, nc = cx.em, cx.nc
    n = host_ap.shape[0]
    st = contextlib.ExitStack()
    stf = st.enter_context(nc.sbuf_tensor(f"tf_{name}", [128, 2, 4096], F32))
    sb = st.enter_context(nc.sbuf_tensor(f"tb_{name}", [128, 4096], BF16))
    acc = st.enter_context(nc.sbuf_tensor(f"ta_{name}", [128, 8], F32))
    rsc = st.enter_context(nc.sbuf_tensor(f"tr_{name}", [128, 2], F32))
    t1 = st.enter_context(nc.sbuf_tensor(f"t1_{name}", [128, 4096], F32))
    t2 = st.enter_context(nc.sbuf_tensor(f"t2_{name}", [128, 4096], F32))
    lds = [None, None]
    cons = [None, None]
    vtok = None
    for i in range(n):
        par = i % 2
        lds[par] = em.emit("sync", "dma_start", stf[:, par], host_ap[i],
                           waits=[cons[par]], inc=cx.s_ld, amt=16)
        cons[par] = em.emit("scalar", "activation", t1[:], stf[:, par],
                            AF.Abs, waits=[lds[par], vtok], inc=cx.s_act)
        w_ = 2048
        first = True
        while w_ >= 64:
            em.emit("vector", "tensor_add", t1[:, 0:w_], t1[:, 0:w_],
                    t1[:, w_:2 * w_], waits=[cons[par]] if first else ())
            first = False
            w_ //= 2
        tk = None
        while w_ >= 1:
            tk = tiny(cx, "vector", "tensor_add", t1[:, 0:w_], t1[:, 0:w_],
                      t1[:, w_:2 * w_], waits=[tk])
            w_ //= 2
        vtok = tiny(cx, "vector", "tensor_copy", acc[:, i:i + 1],
                    t1[:, 0:1], waits=[tk])
    w_ = n // 2
    while w_ >= 1:
        vtok = tiny(cx, "vector", "tensor_add", acc[:, 0:w_], acc[:, 0:w_],
                    acc[:, w_:2 * w_], waits=[vtok])
        w_ //= 2
    bk = cx.banks[6]
    mm = em.emit("tensor", "transpose", bk.ap[0:1, 0:128], acc[:, 0:1],
                 cx.ident_f[:], waits=bk.pre_write() + [vtok, cx.tok_setup],
                 inc=cx.s_pe)
    bk.set_write(mm)
    rr = tiny(cx, "vector", "tensor_copy", cx.rs0[0:1, 0:128],
              bk.ap[0:1, 0:128], waits=[bk.read()])
    bk.rtoks.append(rr)
    w_ = 64
    while w_ >= 1:
        rr = tiny(cx, "vector", "tensor_add", cx.rs0[0:1, 0:w_],
                  cx.rs0[0:1, 0:w_], cx.rs0[0:1, w_:2 * w_], waits=[rr])
        w_ //= 2
    assert n * 128 * 4096 == 1 << 22
    a = tiny(cx, "vector", "tensor_scalar", cx.rs0[0:1, 0:1],
             cx.rs0[0:1, 0:1], float(2.0 ** -22), None, ALU.mult, waits=[rr])
    a = tiny(cx, "vector", "tensor_scalar_max", cx.rs0[0:1, 0:1],
             cx.rs0[0:1, 0:1], 1e-8, waits=[a])
    a = tiny(cx, "vector", "tensor_scalar", cx.rs0[0:1, 1:2],
             cx.rs0[0:1, 0:1], 0.5, None, ALU.mult, waits=[a])
    stok = tiny(cx, "vector", "tensor_scalar", cx.rs0[0:1, 2:3],
                cx.rs0[0:1, 1:2], -1.0, None, ALU.mult, waits=[a])
    mm2 = t_mm(cx, bk.ap[:, 0:2], cx.ones_r[:], cx.rs0[0:1, 1:3], True, True,
               waits=bk.pre_write() + [stok], inc=cx.s_pe)
    bk.set_write(mm2)
    cp = tiny(cx, "vector", "tensor_copy", rsc[:, 0:2], bk.ap[:, 0:2],
              waits=[bk.read()])
    bk.rtoks.append(cp)
    blocks = _flat_view(scr_ap)
    out_tok = None
    for i in range(n):
        par = i % 2
        lds[par] = em.emit("sync", "dma_start", stf[:, par], host_ap[i],
                           waits=[cp if i < 2 else None, out_tok],
                           inc=cx.s_ld, amt=16)
        em.emit("vector", "tensor_scalar", t2[:], stf[:, par],
                rsc[:, 0:1], None, ALU.is_gt, waits=[lds[par], cp])
        em.emit("vector", "tensor_scalar", t1[:], stf[:, par],
                rsc[:, 1:2], None, ALU.is_lt)
        wtk = em.emit("vector", "tensor_sub", sb[:], t2[:], t1[:],
                      inc=cx.s_dve)
        out_tok = em.emit("sync", "dma_start", blocks[i], sb[:], waits=[wtk],
                          inc=cx.s_ld, amt=16)
    st.close()
    return out_tok


def emit_x_load(cx, seq):
    cx.tok_x = cx.em.emit("sync", "dma_start", cx.X[:], cx.x_in[seq],
                          waits=[getattr(cx, "tok_xstore", None)],
                          inc=cx.s_x, amt=16)


def emit_x_store(cx, seq, final):
    em = cx.em
    cx.tok_xstore = em.emit("scalar", "dma_start", cx.x_out[seq], cx.X[:],
                            waits=[cx.tok_x], inc=cx.s_sc, amt=16)
    if final:
        em.wait("scalar", cx.tok_xstore)


def emit_debug_tap(cx, name, fn):
    ap = fn(cx)
    out = cx.nc.dram_tensor(f"dbg_{name}", list(ap.shape), ap.dtype,
                            kind="ExternalOutput").ap()
    tok = cx.em.emit("scalar", "dma_start", out, ap, inc=cx.s_sc, amt=16)
    cx.em.wait("scalar", tok)



def emit_ln_stats(cx, r_row, b_row, xb):
    em = cx.em
    last = None
    sq_free = [None, None]
    _anon[0] += 1
    sq_cm = cx.nc.sbuf_tensor(f"sq{_anon[0]}", [128, 2, 512], BF16)
    sq = sq_cm.__enter__()
    for tt in range(4):
        sl = slice(tt * 512, (tt + 1) * 512)
        b0, b1 = cx.banks[0], cx.banks[1]
        w0 = b0.pre_write() + [cx.tok_setup]
        w1 = b1.pre_write()
        for c in range(NCH):
            ct = em.emit("vector", "tensor_copy", xb[:, c], cx.X[:, c, sl],
                         waits=[cx.tok_x], inc=cx.s_dve)
            sqt = em.emit("gpsimd", "tensor_mul", sq[:, c % 2], xb[:, c],
                          xb[:, c], waits=[ct, sq_free[c % 2]],
                          inc=cx.s_pool)
            mm0 = t_mm(cx, b0.ap[0:1, :], cx.ones_bf[:], xb[:, c], c == 0,
                       c == NCH - 1, waits=w0 + [ct],
                       inc=cx.s_pe if c == NCH - 1 else None)
            mm1 = t_mm(cx, b1.ap[0:1, :], cx.ones_bf[:], sq[:, c % 2], c == 0,
                       c == NCH - 1, waits=w1 + [sqt], inc=cx.s_pe)
            sq_free[c % 2] = mm1
        b0.set_write(mm0)
        b1.set_write(mm1)
        a0 = tiny(cx, "vector", "tensor_scalar", cx.rs0[0:1, :],
                  b0.ap[0:1, :], 1.0 / D, None, ALU.mult, waits=[b0.read()])
        b0.rtoks.append(a0)
        a1 = tiny(cx, "vector", "tensor_scalar", cx.rs1[0:1, :],
                  b1.ap[0:1, :], 1.0 / D, None, ALU.mult, waits=[b1.read()])
        b1.rtoks.append(a1)
        m2 = tiny(cx, "vector", "tensor_mul", cx.xs[0:1, :], cx.rs0[0:1, :],
                  cx.rs0[0:1, :], waits=[a0])
        s1 = tiny(cx, "vector", "tensor_sub", cx.rs1[0:1, :], cx.rs1[0:1, :],
                  cx.xs[0:1, :], waits=[a1, m2])
        vt = tiny(cx, "vector", "tensor_scalar_add", cx.rs1[0:1, :],
                  cx.rs1[0:1, :], EPS_LN, waits=[s1])
        sq2 = tiny(cx, "scalar", "activation", cx.rs1[0:1, :],
                   cx.rs1[0:1, :], AF.Sqrt, waits=[vt])
        rr = tiny(cx, "vector", "reciprocal", r_row[0:1, sl],
                  cx.rs1[0:1, :], waits=[sq2])
        bb = tiny(cx, "vector", "tensor_mul", b_row[0:1, sl],
                  cx.rs0[0:1, :], r_row[0:1, sl], waits=[rr, a0])
        last = tiny(cx, "vector", "tensor_scalar_mul", b_row[0:1, sl],
                    b_row[0:1, sl], -1.0, waits=[bb])
    sq_cm.__exit__(None, None, None)
    part_barrier(cx)
    return last


def emit_bcast_rb(cx, bank, r_row, b_row, t0, L, stats_tok):
    w = bank.pre_write() + [stats_tok, cx.tok_setup]
    t_mm(cx, bank.ap[:, 0:L], cx.ones_r[:], r_row[0:1, t0:t0 + L], True, True,
         waits=w)
    mm = t_mm(cx, bank.ap[:, 256:256 + L], cx.ones_r[:],
              b_row[0:1, t0:t0 + L], True, True, inc=cx.s_pe)
    bank.set_write(mm)
    return mm


def emit_xn_tile(cx, bank, r_row, b_row, t0, L, out, stats_tok,
                 out_free=None):
    em = cx.em
    last = None
    for h0 in range(0, L, 256):
        Lh = min(256, L - h0)
        emit_bcast_rb(cx, bank, r_row, b_row, t0 + h0, Lh, stats_tok)
        for c in range(NCH):
            em.emit("vector", "tensor_mul", cx.xs[:, 0:Lh],
                    cx.X[:, c, t0 + h0:t0 + h0 + Lh], bank.ap[:, 0:Lh],
                    waits=[bank.read(), cx.tok_x, out_free])
            last = em.emit("vector", "tensor_add", out[:, c, h0:h0 + Lh],
                           cx.xs[:, 0:Lh], bank.ap[:, 256:256 + Lh],
                           inc=cx.s_dve)
            bank.rtoks.append(last)
    return last


class WStream:

    def __init__(self, cx, nkc=8, nbuf=2, sems=None):
        self.cx = cx
        self.nkc = nkc
        self.buf = A(cx, [128, nbuf, nkc, 128], BF16)
        self.nbuf = nbuf
        self.free = [None] * nbuf
        self.cnt = 0
        self.sems = sems or [cx.s_ld] * nbuf

    def load(self, scr, mc, kc0=0, pro=None):
        i = self.cnt % self.nbuf
        self.cnt += 1
        src = scr[mc][:, kc0 * 128:(kc0 + self.nkc) * 128].rearrange(
            "p (kc j) -> p kc j", j=128)
        tok = self.cx.em.emit("sync", "dma_start", self.buf[:, i], src,
                              waits=[self.free[i], pro],
                              inc=self.sems[i], amt=16)
        return i, tok

    def done(self, i, tok):
        self.free[i] = tok



def emit_attn_core(cx, Q, K_, V, AO, P, PT, rc, qk_tok, v_tok, causal,
                   split_pv, T34=None, M3=None, M4=None, t34_toks=None):
    em = cx.em
    banks = cx.banks
    for qt in range(16):
        q0 = qt * 128
        L = q0 + 128 if causal else T
        nb = (L + 511) // 512
        nkb = L // 128

        def s_phase(pair, hh):
            rows = slice(hh * 64, (hh + 1) * 64)
            for kb in range(nb):
                N = min(512, L - kb * 512)
                bk = banks[kb]
                mm = t_mm(cx, bk.ap[:, 0:N],
                          Q[:, pair, q0:q0 + 128][rows, :],
                          K_[:, pair, kb * 512:kb * 512 + N][rows, :],
                          True, True, waits=bk.pre_write() + [qk_tok],
                          inc=cx.s_pe)
                bk.set_write(mm)
            if causal:
                dkb, dof = divmod(L - 128, 512)
                mt = em.emit("vector", "tensor_add",
                             banks[dkb].ap[:, dof:dof + 128],
                             banks[dkb].ap[:, dof:dof + 128], cx.mask[:],
                             waits=[banks[dkb].read()], inc=cx.s_dve)
                banks[dkb].set_write(mt)
            mx = None
            for kb in range(nb):
                N = min(512, L - kb * 512)
                rm = tiny(cx, "vector", "reduce_max", rc[:, kb:kb + 1],
                          banks[kb].ap[:, 0:N], axis=AX.X,
                          waits=[banks[kb].read()])
                if kb == 0:
                    mx = rm
                else:
                    mx = tiny(cx, "vector", "tensor_max", rc[:, 0:1],
                              rc[:, 0:1], rc[:, kb:kb + 1],
                              waits=[rm, mx])
            return tiny(cx, "vector", "tensor_scalar_mul", rc[:, 4:5],
                        rc[:, 0:1], -0.125, waits=[mx])

        def softmax_phase(nm, extra):
            ex = None
            for kb in range(nb):
                N = min(512, L - kb * 512)
                ex = em.emit("scalar", "activation",
                             P[:, kb * 512:kb * 512 + N],
                             cx.S4[:, kb * 512:kb * 512 + N], AF.Exp,
                             bias=rc[:, 4:5], scale=0.125,
                             accum_out=rc[:, kb:kb + 1],
                             waits=[nm] + ([extra] if kb == 0 else []),
                             inc=cx.s_act)
                banks[kb].rtoks.append(ex)
            sm = tiny(cx, "vector", "reduce_sum", rc[:, 5:6], rc[:, 0:nb],
                      axis=AX.X, waits=[ex])
            rp = tiny(cx, "vector", "reciprocal", rc[:, 6:7],
                      rc[:, 5:6], waits=[sm])
            return em.emit("vector", "tensor_scalar_mul", P[:, 0:L],
                           P[:, 0:L], rc[:, 6:7], waits=[rp],
                           inc=cx.s_dve)

        def ptpv_phase(pair, hh, ptok, out_bk, ow):
            head = pair * 2 + hh
            pv_last = None
            last_tr = None
            for g0 in range(0, nkb, 4):
                kbs = list(range(g0, min(g0 + 4, nkb)))
                tb = banks[7]
                wtb = tb.pre_write()
                for kb in kbs:
                    j = kb - g0
                    tr = em.emit("tensor", "transpose",
                                 tb.ap[:, j * 128:(j + 1) * 128],
                                 P[:, kb * 128:(kb + 1) * 128],
                                 cx.ident_bf[:],
                                 waits=(wtb if j == 0 else [])
                                 + [ptok, cx.tok_setup],
                                 inc=cx.s_pe if kb == kbs[-1] else None)
                tb.set_write(tr)
                last_tr = tr
                pt = PT[(g0 // 4) % 2]
                ev = em.emit("scalar", "copy",
                             pt[:, 0:len(kbs) * 128],
                             tb.ap[:, 0:len(kbs) * 128],
                             waits=[tb.read()], inc=cx.s_act)
                tb.rtoks.append(ev)
                for kb in kbs:
                    j = kb - g0
                    ptj = pt[:, j * 128:(j + 1) * 128]
                    w0 = [ev, v_tok] + (ow if kb == 0 and hh == 0 else [])
                    if not split_pv:
                        pv_last = t_mm(
                            cx, out_bk.ap[slice(hh * 64, hh * 64 + 64),
                                          0:128],
                            V[:, kb, head * 64:(head + 1) * 64], ptj,
                            kb == 0, kb == nkb - 1, waits=w0,
                            tp=(0, hh * 64),
                            inc=cx.s_pe if kb == nkb - 1 else None)
                    else:
                        vh = V[:, kb, head * 64:(head + 1) * 64]
                        vv = vh.rearrange("p (j two) -> p two j", two=2)
                        for par in range(2):
                            r0 = hh * 64 + par * 32
                            pv_last = t_mm(
                                cx, out_bk.ap[r0:r0 + 32, 0:128],
                                vv[:, par, :], ptj, kb == 0,
                                kb == nkb - 1, waits=w0, tp=(0, r0),
                                inc=cx.s_pe if kb == nkb - 1 and par == 1
                                else None)
            return pv_last, last_tr

        for pair in range(8):
            out_bk = banks[4 + (pair % 2)]
            ow = out_bk.pre_write()
            nm0 = s_phase(pair, 0)
            ptok0 = softmax_phase(nm0, None)
            nm1 = s_phase(pair, 1)
            _, tr0 = ptpv_phase(pair, 0, ptok0, out_bk, ow)
            ptok1 = softmax_phase(nm1, tr0)
            pv_last, _ = ptpv_phase(pair, 1, ptok1, out_bk, ow)
            out_bk.set_write(pv_last)
            if T34 is None:
                ao_tok = em.emit("vector", "tensor_copy", AO[:, pair, :],
                                 out_bk.ap[:, 0:128], waits=[out_bk.read()],
                                 inc=cx.s_dve)
                out_bk.rtoks.append(ao_tok)
            else:
                tts = qt // 4
                tj = (qt % 4) * 128
                a = em.emit("vector", "tensor_mul", M3[:],
                            out_bk.ap[:, 0:128],
                            T34[0][tts % 2][:, tj:tj + 128],
                            waits=[out_bk.read(), t34_toks[tts % 2]],
                            inc=cx.s_dve)
                out_bk.rtoks.append(a)
                a = em.emit("vector", "tensor_mul", M4[:],
                            out_bk.ap[:, 0:128],
                            T34[1][tts % 2][:, tj:tj + 128], inc=cx.s_dve)
                out_bk.rtoks.append(a)
                rb = banks[6]
                w = rb.pre_write() + [a, cx.tok_l1tab]
                t_mm(cx, rb.ap[:, 0:128], cx.sel1[:], M3[:], True, False,
                     waits=w)
                mm2 = t_mm(cx, rb.ap[:, 0:128], cx.sel2[:], M4[:], False,
                           True, inc=cx.s_pe)
                rb.set_write(mm2)
                ao_tok = em.emit("vector", "tensor_copy", AO[:, pair, :],
                                 rb.ap[:, 0:128], waits=[rb.read()],
                                 inc=cx.s_dve)
                rb.rtoks.append(ao_tok)
        yield qt, q0, ao_tok



def emit_qk_fmajor(cx, ws, Q, K_, r_row, b_row, stats_tok, xb, rope_bufs=None,
                   pros=(None, None), scrs=(None, None)):
    em = cx.em
    rope = rope_bufs is not None
    if rope:
        bq, bk_, TTd, TTs, U1, U2 = rope_bufs
        tt_free = None
    ev = None
    for tt in range(4):
        t0 = tt * 512
        xn_tok = emit_xn_tile(cx, cx.banks[6], r_row, b_row, t0, 512, xb,
                              stats_tok)
        if rope:
            t12 = em.emit("gpsimd", "dma_start", TTs[0][:],
                          TTd[0][:, t0:t0 + 512], waits=[tt_free],
                          inc=cx.s_g3, amt=16)
            t12 = em.emit("gpsimd", "dma_start", TTs[1][:],
                          TTd[1][:, t0:t0 + 512], inc=cx.s_g3, amt=16)
        for fc in range(16):
            if scrs[1] is None:
                scr_w, mc, pro = scrs[0], fc, pros[0]
            else:
                scr_w, mc, pro = ((scrs[0], fc, pros[0]) if fc < 8 else
                                  (scrs[1], fc - 8, pros[1]))
            wi, wtok = ws.load(scr_w, mc, pro=pro)
            bk = cx.banks[fc % 2]
            w = bk.pre_write() + [wtok, xn_tok]
            for kc in range(NCH):
                mm = t_mm(cx, bk.ap, ws.buf[:, wi, kc], xb[:, kc],
                          kc == 0, kc == NCH - 1, waits=w if kc == 0 else (),
                          inc=cx.s_pe if kc == NCH - 1 else None)
            bk.set_write(mm)
            ws.done(wi, mm)
            dst = (Q if fc < 8 else K_)[:, fc % 8, t0:t0 + 512]
            if not rope:
                ev = em.emit("vector", "tensor_copy", dst, bk.ap,
                             waits=[bk.read()], inc=cx.s_dve)
                bk.rtoks.append(ev)
            else:
                bias = (bq if fc < 8 else bk_)[:, fc % 8:fc % 8 + 1]
                a = em.emit("vector", "tensor_scalar_add", cx.xs[:], bk.ap,
                            bias, waits=[bk.read(), t12], inc=cx.s_dve)
                bk.rtoks.append(a)
                u1 = em.emit("vector", "tensor_mul", U1[:], cx.xs[:],
                             TTs[0][:, :])
                u2 = em.emit("vector", "tensor_mul", U2[:], cx.xs[:],
                             TTs[1][:, :], inc=cx.s_dve)
                rb = cx.banks[2]
                w = rb.pre_write() + [u2, cx.tok_l1tab]
                t_mm(cx, rb.ap, cx.sel1[:], U1[:], True, False, waits=w)
                mm2 = t_mm(cx, rb.ap, cx.sel2[:], U2[:], False, True,
                           inc=cx.s_pe)
                rb.set_write(mm2)
                ev = em.emit("vector", "tensor_copy", dst, rb.ap,
                             waits=[rb.read()], inc=cx.s_dve)
                rb.rtoks.append(ev)
                if fc == 15:
                    tt_free = ev
    return ev


def emit_v_tmajor(cx, scr_wv, V, r_row, b_row, stats_tok, xb, wvbuf,
                  rope_bufs=None, pro=None, nbuf=2, sems=None):
    em = cx.em
    rope = rope_bufs is not None
    if rope:
        bvb, ct, st_ = rope_bufs
    wv = scr_wv
    free = [None, None]
    ev = None
    for tt in range(4):
        t0 = tt * 512
        xn_tok = emit_xn_tile(cx, cx.banks[6], r_row, b_row, t0, 512, xb,
                              stats_tok)
        for g in range(4):
            vi = g % nbuf
            vtok = em.emit("sync", "dma_start", wvbuf[:, vi],
                           wv[g].rearrange("p (kc j) -> p kc j", j=256),
                           waits=[pro, free[vi]],
                           inc=(sems or (cx.s_w2,) * nbuf)[vi], amt=16)
            for q4 in range(4):
                tq = tt * 4 + q4
                bk = cx.banks[2 + (q4 % 2)]
                w = bk.pre_write() + [vtok, xn_tok]
                for kc in range(NCH):
                    mm = t_mm(cx, bk.ap[:, 0:256],
                              xb[:, kc, q4 * 128:(q4 + 1) * 128],
                              wvbuf[:, vi, kc], kc == 0, kc == NCH - 1,
                              waits=w if kc == 0 else (),
                              inc=cx.s_pe if kc == NCH - 1 else None)
                bk.set_write(mm)
                last_mm = mm
                dst = V[:, tq, g * 256:(g + 1) * 256]
                if not rope:
                    ev = em.emit("vector", "tensor_copy", dst,
                                 bk.ap[:, 0:256], waits=[bk.read()],
                                 inc=cx.s_dve)
                    bk.rtoks.append(ev)
                else:
                    a = em.emit("vector", "tensor_add", cx.xs[:, 0:256],
                                bk.ap[:, 0:256],
                                bvb[:, g * 256:(g + 1) * 256],
                                waits=[bk.read()], inc=cx.s_dve)
                    bk.rtoks.append(a)
                    vv = cx.xs[:, 0:256].rearrange(
                        "p (h j two) -> p h j two", h=4, two=2)
                    evn, od = vv[:, :, :, 0], vv[:, :, :, 1]
                    ctb = ct[:, tq:tq + 1, :].broadcast_to([128, 4, 32])
                    stb = st_[:, tq:tq + 1, :].broadcast_to([128, 4, 32])
                    dh = dst.rearrange("p (h half j) -> p h half j", h=4,
                                       half=2)
                    xA = cx.xsB[:, 0:128].rearrange("p (h j) -> p h j", h=4)
                    xBv = cx.xsB[:, 128:256].rearrange("p (h j) -> p h j", h=4)
                    em.emit("vector", "tensor_mul", xA, evn, ctb)
                    em.emit("vector", "tensor_mul", xBv, od, stb)
                    em.emit("vector", "tensor_sub", dh[:, :, 0], xA, xBv)
                    em.emit("vector", "tensor_mul", xA, evn, stb)
                    em.emit("vector", "tensor_mul", xBv, od, ctb)
                    ev = em.emit("vector", "tensor_add", dh[:, :, 1], xA, xBv,
                                 inc=cx.s_dve)
            free[vi] = last_mm
    return ev



def part_l0a(cx):
    em, nc = cx.em, cx.nc
    if not hasattr(cx, "bias_bproj"):
        cx.bias_bproj = nc.alloc_sbuf_tensor("bias_bproj", [128, 8], F32)
        cx.bias_gam = nc.alloc_sbuf_tensor("bias_gam", [128, 8], F32)
        cx.bias_bet = nc.alloc_sbuf_tensor("bias_bet", [128, 8], F32)
        em.emit("sync", "dma_start", cx.bias_bproj[:], cx.H["h_bproj"],
                inc=cx.s_ld, amt=16)
        em.emit("sync", "dma_start", cx.bias_gam[:], cx.H["h_gam"],
                inc=cx.s_ld, amt=16)
        cx.tok_l0bias = em.emit("sync", "dma_start", cx.bias_bet[:],
                                cx.H["h_bet"], inc=cx.s_ld, amt=16)

    if cx.debug_taps:
        cx.dbg_ln2 = cx.nc.dram_tensor("dbg_ln2", [1, 4 * T], F32,
                                       kind="ExternalOutput").ap()
    Q = A(cx, [128, NCH, T], BF16)
    K_ = A(cx, [128, NCH, T], BF16)
    r_row = A(cx, [1, T], F32)
    b_row = A(cx, [1, T], F32)
    V = A(cx, [128, 16, 1024], BF16)

    with contextlib.ExitStack() as st2:
        old = cx.st
        cx.st = st2
        xb = A(cx, [128, NCH, 512], BF16)
        stats_tok = emit_ln_stats(cx, r_row, b_row, xb)
        cx.tok_stats_l0a = stats_tok
        ws = WStream(cx, sems=(cx.s_w0, cx.s_w1))
        qk_tok = emit_qk_fmajor(cx, ws, Q, K_, r_row, b_row, stats_tok, xb,
                                pros=(cx.pro.get("wqk"), None),
                                scrs=(cx.S["wqk"], None))
        wvbuf = A(cx, [128, 1, NCH, 256], BF16)
        v_tok = emit_v_tmajor(cx, cx.S["wv"], V, r_row, b_row, stats_tok, xb,
                              wvbuf, pro=cx.pro.get("wv"), nbuf=1,
                              sems=(cx.s_w2,))
        cx.st = old
    part_barrier(cx)

    P = A(cx, [128, T], BF16)
    PT = [A(cx, [128, 512], BF16) for _ in range(2)]
    AO = A(cx, [128, NCH, 128], BF16)
    rc = A(cx, [128, 8], F32)
    Tsc = A(cx, [128, NCH, 128], F32)
    Tb = A(cx, [128, NCH, 128], BF16)
    sq2 = A(cx, [128, 2, 128], BF16)
    wp = WStream(cx, nbuf=1, sems=(cx.s_w3,))
    cx._sq2_free = [None, None]

    cx.dbg = dict(Q=Q, K=K_, V=V, P=P, AO=AO, r_row=r_row, b_row=b_row,
                  rc=rc, Tsc=Tsc)
    for qt, q0, ao_tok in emit_attn_core(cx, Q, K_, V, AO, P, PT, rc,
                                         qk_tok, v_tok, causal=True,
                                         split_pv=False):
        _l0a_epilogue(cx, AO, ao_tok, q0, r_row, b_row, Tsc, Tb, sq2, wp)


def _l0a_epilogue(cx, AO, ao_tok, q0, r_row, b_row, Tsc, Tb, sq2, wp):
    em = cx.em
    banks = cx.banks
    bc1 = banks[2]
    emit_bcast_rb(cx, bc1, r_row, b_row, q0, 128, cx.tok_stats_l0a)
    proj_toks = []
    for mc in range(NCH):
        wi, wtok = wp.load(cx.S["wproj"], mc, pro=cx.pro.get("wproj"))
        bk = banks[mc % 2]
        w = bk.pre_write() + [wtok, ao_tok, cx.tok_l0bias]
        for kc in range(NCH):
            mm = t_mm(cx, bk.ap[:, 0:128], wp.buf[:, wi, kc], AO[:, kc, :],
                      kc == 0, kc == NCH - 1, waits=w if kc == 0 else (),
                      inc=cx.s_pe if kc == NCH - 1 else None)
        bk.set_write(mm)
        wp.done(wi, mm)
        em.emit("vector", "tensor_mul", cx.xs[:, 0:128],
                cx.X[:, mc, q0:q0 + 128], bc1.ap[:, 0:128],
                waits=[bc1.read(), bk.read()])
        em.emit("vector", "tensor_add", cx.xs[:, 0:128], cx.xs[:, 0:128],
                bc1.ap[:, 256:384])
        tk = em.emit("vector", "scalar_tensor_tensor", Tsc[:, mc],
                     bk.ap[:, 0:128], cx.bias_bproj[:, mc:mc + 1],
                     cx.xs[:, 0:128], ALU.add, ALU.add, inc=cx.s_dve)
        bk.rtoks.append(tk)
        bc1.rtoks.append(tk)
        pt = em.emit("vector", "tensor_copy", Tb[:, mc], Tsc[:, mc],
                     inc=cx.s_dve)
        proj_toks.append(pt)
    b0, b1 = banks[5], banks[6]
    w0 = b0.pre_write()
    w1 = b1.pre_write()
    for c in range(NCH):
        sqt = em.emit("gpsimd", "tensor_mul", sq2[:, c % 2], Tb[:, c],
                      Tb[:, c], waits=[proj_toks[c], cx._sq2_free[c % 2]],
                      inc=cx.s_pool)
        mm0 = t_mm(cx, b0.ap[0:1, 0:128], cx.ones_bf[:], Tb[:, c], c == 0,
                   c == NCH - 1, waits=w0 + [proj_toks[c]],
                   inc=cx.s_pe if c == NCH - 1 else None)
        mm1 = t_mm(cx, b1.ap[0:1, 0:128], cx.ones_bf[:], sq2[:, c % 2],
                   c == 0, c == NCH - 1, waits=w1 + [sqt],
                   inc=cx.s_pe)
        cx._sq2_free[c % 2] = mm1
    b0.set_write(mm0)
    b1.set_write(mm1)
    a0 = tiny(cx, "vector", "tensor_scalar", cx.rs0[0:1, 0:128],
              b0.ap[0:1, 0:128], 1.0 / D, None, ALU.mult, waits=[b0.read()])
    b0.rtoks.append(a0)
    a1 = tiny(cx, "vector", "tensor_scalar", cx.rs1[0:1, 0:128],
              b1.ap[0:1, 0:128], 1.0 / D, None, ALU.mult, waits=[b1.read()])
    b1.rtoks.append(a1)
    if hasattr(cx, "dbg_ln2"):
        d1 = em.emit("scalar", "dma_start",
                     cx.dbg_ln2[0:1, 2 * T + q0:2 * T + q0 + 128],
                     cx.rs0[0:1, 0:128], waits=[a0], inc=cx.s_sc, amt=16)
        d2 = em.emit("scalar", "dma_start",
                     cx.dbg_ln2[0:1, 3 * T + q0:3 * T + q0 + 128],
                     cx.rs1[0:1, 0:128], waits=[a1], inc=cx.s_sc, amt=16)
        em.wait("vector", (d2[0], d2[1], None))
    t1 = tiny(cx, "vector", "tensor_mul", cx.xs[0:1, 0:128],
              cx.rs0[0:1, 0:128], cx.rs0[0:1, 0:128], waits=[a0])
    t1 = tiny(cx, "vector", "tensor_sub", cx.rs1[0:1, 0:128],
              cx.rs1[0:1, 0:128], cx.xs[0:1, 0:128], waits=[a1, t1])
    vt = tiny(cx, "vector", "tensor_scalar_add", cx.rs1[0:1, 0:128],
              cx.rs1[0:1, 0:128], EPS_LN, waits=[t1])
    sqr = tiny(cx, "scalar", "activation", cx.rs1[0:1, 0:128],
               cx.rs1[0:1, 0:128], AF.Sqrt, waits=[vt])
    t1 = tiny(cx, "vector", "reciprocal", cx.xs[0:1, 0:128],
              cx.rs1[0:1, 0:128], waits=[sqr])
    t1 = tiny(cx, "vector", "tensor_mul", cx.rs0[0:1, 0:128],
              cx.rs0[0:1, 0:128], cx.xs[0:1, 0:128], waits=[t1, a0])
    nmu = tiny(cx, "vector", "tensor_scalar_mul", cx.rs0[0:1, 0:128],
               cx.rs0[0:1, 0:128], -1.0, waits=[t1])
    if hasattr(cx, "dbg_ln2"):
        d1 = em.emit("scalar", "dma_start", cx.dbg_ln2[0:1, q0:q0 + 128],
                     cx.xs[0:1, 0:128], waits=[t1], inc=cx.s_sc, amt=16)
        d2 = em.emit("scalar", "dma_start",
                     cx.dbg_ln2[0:1, T + q0:T + q0 + 128],
                     cx.rs0[0:1, 0:128], waits=[nmu], inc=cx.s_sc, amt=16)
        em.wait("vector", (d2[0], d2[1], None))
    bch = banks[3]
    wb = bch.pre_write() + [nmu]
    t_mm(cx, bch.ap[:, 0:128], cx.ones_r[:], cx.xs[0:1, 0:128], True, True,
         waits=wb)
    mmb = t_mm(cx, bch.ap[:, 256:384], cx.ones_r[:], cx.rs0[0:1, 0:128],
               True, True, inc=cx.s_pe)
    bch.set_write(mmb)
    for mc in range(NCH):
        em.emit("vector", "tensor_mul", cx.xs[:, 0:128], Tsc[:, mc],
                bch.ap[:, 0:128], waits=[bch.read()])
        em.emit("vector", "tensor_add", cx.xs[:, 0:128], cx.xs[:, 0:128],
                bch.ap[:, 256:384])
        em.emit("vector", "tensor_scalar", cx.xs[:, 0:128], cx.xs[:, 0:128],
                cx.bias_gam[:, mc:mc + 1], cx.bias_bet[:, mc:mc + 1],
                ALU.mult, ALU.add)
        xt = em.emit("vector", "tensor_add", cx.X[:, mc, q0:q0 + 128],
                     cx.X[:, mc, q0:q0 + 128], cx.xs[:, 0:128], inc=cx.s_dve)
        bch.rtoks.append(xt)


def part_l0f(cx):
    em, nc = cx.em, cx.nc
    if not hasattr(cx, "b1c"):
        cx.b1c = nc.alloc_sbuf_tensor("b1c", [128, 128], F32)
        em.emit("sync", "dma_start", cx.b1c[:], cx.H["h_b1c"],
                inc=cx.s_ld, amt=16)
        cx.gbb = nc.alloc_sbuf_tensor("gbb", [128, 8], F32)
        cx.rwb = nc.alloc_sbuf_tensor("rwb", [128, 8], F32)
        cx.b2fm = nc.alloc_sbuf_tensor("b2fm", [128, 8], F32)
        gb_row = nc.alloc_sbuf_tensor("gb_row", [1, 8], F32)
        rw_row = nc.alloc_sbuf_tensor("rw_row", [1, 8], F32)
        em.emit("sync", "dma_start", gb_row[:], cx.H["h_gateb"],
                inc=cx.s_ld, amt=16)
        tld = em.emit("sync", "dma_start", rw_row[:], cx.H["h_rmsw"],
                      inc=cx.s_ld, amt=16)
        bk = cx.banks[6]
        w = bk.pre_write() + [tld, cx.tok_setup]
        t_mm(cx, bk.ap[:, 0:8], cx.ones_r[:], gb_row[0:1, :], True, True,
             waits=w)
        mm = t_mm(cx, bk.ap[:, 8:16], cx.ones_r[:], rw_row[0:1, :],
                  True, True, inc=cx.s_pe)
        bk.set_write(mm)
        g1 = tiny(cx, "vector", "tensor_copy", cx.gbb[:], bk.ap[:, 0:8],
                  waits=[bk.read()])
        a = tiny(cx, "vector", "tensor_copy", cx.rwb[:], bk.ap[:, 8:16],
                 waits=[g1])
        bk.rtoks.append(a)
        b2stage = nc.alloc_sbuf_tensor("b2stage", [128, 8], F32)
        t0 = em.emit("sync", "dma_start", cx.b2fm[:], cx.H["h_b2fm"][0],
                     inc=cx.s_ld, amt=16)
        cx.tok_l0f = t0
        prev = None
        for e in range(1, 8):
            te = em.emit("sync", "dma_start", b2stage[:], cx.H["h_b2fm"][e],
                         waits=[prev], inc=cx.s_ld, amt=16)
            prev = tiny(cx, "vector", "tensor_add", cx.b2fm[:], cx.b2fm[:],
                        b2stage[:], waits=[te, t0, prev])
        cx.tok_l0f = prev

    ngcol = A(cx, [128, 16], F32)
    with contextlib.ExitStack() as st1:
        old = cx.st
        cx.st = st1
        r_row = A(cx, [1, T], F32)
        b_row = A(cx, [1, T], F32)
        xb = A(cx, [128, NCH, 512], BF16)
        gwb = A(cx, [128, NCH, 8], BF16)
        gcol = A(cx, [128, 8], F32)
        stats_tok = emit_ln_stats(cx, r_row, b_row, xb)
        gw_tok = em.emit("sync", "dma_start", gwb[:],
                         cx.S["gatew"].rearrange("(kc p) e -> p kc e", p=128),
                         waits=[cx.pro.get("gatew")], inc=cx.s_ld, amt=16)
        xb_dma = None
        for tt in range(4):
            t0 = tt * 512
            xn_tok = emit_xn_tile(cx, cx.banks[6], r_row, b_row, t0, 512, xb,
                                  stats_tok, out_free=xb_dma)
            xb_dma = em.emit("sync", "dma_start",
                             cx.S["xb"][tt].rearrange("p (c l) -> p c l", l=512),
                             xb[:], waits=[xn_tok, xb_dma], inc=cx.s_cw,
                             amt=16)
            for q4 in range(4):
                bk = cx.banks[2]
                w = bk.pre_write() + [gw_tok, xn_tok]
                for kc in range(NCH):
                    mm = t_mm(cx, bk.ap[:, 0:8],
                              xb[:, kc, q4 * 128:(q4 + 1) * 128],
                              gwb[:, kc], kc == 0, kc == NCH - 1,
                              waits=w if kc == 0 else (),
                              inc=cx.s_pe if kc == NCH - 1 else None)
                bk.set_write(mm)
                a = tiny(cx, "vector", "tensor_add", gcol[:, 0:8],
                         bk.ap[:, 0:8], cx.gbb[:],
                         waits=[bk.read(), cx.tok_l0f])
                bk.rtoks.append(a)
                t1 = tiny(cx, "vector", "tensor_mul", cx.xs[:, 0:8],
                          gcol[:, 0:8], gcol[:, 0:8], waits=[a])
                t1 = tiny(cx, "vector", "reduce_sum", cx.xs[:, 8:9],
                          cx.xs[:, 0:8], axis=AX.X, waits=[t1])
                vt = tiny(cx, "vector", "tensor_scalar", cx.xs[:, 8:9],
                          cx.xs[:, 8:9], 0.125, EPS_F32, ALU.mult,
                          ALU.add, waits=[t1])
                sqt = tiny(cx, "scalar", "activation", cx.xs[:, 8:9],
                           cx.xs[:, 8:9], AF.Sqrt, waits=[vt])
                t1 = tiny(cx, "vector", "reciprocal", cx.xs[:, 9:10],
                          cx.xs[:, 8:9], waits=[sqt])
                t1 = tiny(cx, "vector", "tensor_scalar_mul", gcol[:, 0:8],
                          gcol[:, 0:8], cx.xs[:, 9:10], waits=[t1])
                t1 = tiny(cx, "vector", "tensor_mul", gcol[:, 0:8],
                          gcol[:, 0:8], cx.rwb[:], waits=[t1])
                t1 = tiny(cx, "vector", "reduce_max", cx.xs[:, 10:11],
                          gcol[:, 0:8], axis=AX.X, waits=[t1])
                nm = tiny(cx, "vector", "tensor_scalar_mul", cx.xs[:, 10:11],
                          cx.xs[:, 10:11], -2.0, waits=[t1])
                ext = tiny(cx, "scalar", "activation", gcol[:, 0:8],
                           gcol[:, 0:8], AF.Exp, bias=cx.xs[:, 10:11],
                           scale=2.0, accum_out=cx.xs[:, 11:12],
                           waits=[nm])
                t1 = tiny(cx, "vector", "reciprocal", cx.xs[:, 12:13],
                          cx.xs[:, 11:12], waits=[ext])
                t2 = tiny(cx, "vector", "tensor_scalar_add", cx.xs[:, 13:14],
                          cx.xs[:, 12:13], 1e-6, waits=[t1])
                t2 = tiny(cx, "vector", "reciprocal", cx.xs[:, 13:14],
                          cx.xs[:, 13:14], waits=[t2])
                gi = tt * 4 + q4
                em.emit("vector", "tensor_mul", ngcol[:, gi:gi + 1],
                        cx.xs[:, 12:13], cx.xs[:, 13:14],
                        waits=[t1, t2], inc=cx.s_dve)
        ng_tok = part_barrier(cx) or cx.em.last["vector"]
        cx.st = old
    part_barrier(cx)

    ysb = A(cx, [128, NCH, T], F32)
    Hb = A(cx, [128, NCH, T], BF16)
    xb2 = A(cx, [128, 2, NCH, 512], BF16)
    cx.dbg = dict(ysb=ysb, Hb=Hb, ng=ngcol)
    w1s = WStream(cx, sems=(cx.s_w0, cx.s_w1))
    w2s = WStream(cx, sems=(cx.s_w2, cx.s_w3))
    xb_free = [None, None]
    ytoks = [[None] * 4 for _ in range(NCH)]
    for hbp in range(16):
        h_last = None
        for ttp in range(2):
            for par in range(2):
                tt = ttp * 2 + par
                xb_free[par] = em.emit(
                    "sync", "dma_start", xb2[:, par],
                    cx.S["xb"][tt].rearrange("p (c l) -> p c l", l=512),
                    waits=[xb_free[par], xb_dma],
                    inc=(cx.s_w4, cx.s_w5)[par], amt=16)
            for mc in range(NCH):
                wi, wtok = w1s.load(cx.S["w1t"], hbp * 8 + mc,
                                    pro=cx.pro.get("w1t"))
                for par in range(2):
                    tt = ttp * 2 + par
                    bk = cx.banks[par]
                    w = bk.pre_write() + [wtok, xb_free[par]]
                    for kc in range(NCH):
                        mm = t_mm(cx, bk.ap, w1s.buf[:, wi, kc],
                                  xb2[:, par, kc], kc == 0, kc == NCH - 1,
                                  waits=w if kc == 0 else (),
                                  inc=cx.s_pe if kc == NCH - 1 else None)
                    bk.set_write(mm)
                    g = em.emit("scalar", "activation",
                                Hb[:, mc, tt * 512:(tt + 1) * 512], bk.ap,
                                AF.Gelu,
                                bias=cx.b1c[:, hbp * 8 + mc:hbp * 8 + mc + 1],
                                waits=[bk.read()], inc=cx.s_act)
                    bk.rtoks.append(g)
                    h_last = g
                w1s.done(wi, mm)
            xb_free = [h_last, h_last]
        for mc in range(NCH):
            wi, wtok = w2s.load(cx.S["w2t"], mc, kc0=hbp * 8,
                                pro=cx.pro.get("w2t"))
            for tt in range(4):
                bk = cx.banks[2 + (tt % 2)]
                w = bk.pre_write() + [wtok, h_last]
                for kc in range(NCH):
                    mm = t_mm(cx, bk.ap, w2s.buf[:, wi, kc],
                              Hb[:, kc, tt * 512:(tt + 1) * 512],
                              kc == 0, kc == NCH - 1,
                              waits=w if kc == 0 else (),
                              inc=cx.s_pe if kc == NCH - 1 else None)
                bk.set_write(mm)
                dst = ysb[:, mc, tt * 512:(tt + 1) * 512]
                if hbp == 0:
                    a = em.emit("vector", "tensor_copy", dst, bk.ap,
                                waits=[bk.read()], inc=cx.s_dve)
                else:
                    a = em.emit("vector", "tensor_add", dst, dst, bk.ap,
                                waits=[bk.read(), ytoks[mc][tt]],
                                inc=cx.s_dve)
                bk.rtoks.append(a)
                ytoks[mc][tt] = a
            w2s.done(wi, ytoks[mc][3])
    for tt in range(4):
        t0 = tt * 512
        tb = cx.banks[6]
        wtb = tb.pre_write()
        for q4 in range(4):
            gi = tt * 4 + q4
            tr = em.emit("tensor", "transpose",
                         tb.ap[0:1, q4 * 128:(q4 + 1) * 128],
                         ngcol[:, gi:gi + 1], cx.ident_f[:],
                         waits=(wtb if q4 == 0 else []) + [ng_tok],
                         inc=cx.s_pe if q4 == 3 else None)
        tb.set_write(tr)
        rcp = em.emit("vector", "tensor_copy", cx.rs0[0:1, :],
                      tb.ap[0:1, 0:512], waits=[tb.read()], inc=cx.s_dve)
        tb.rtoks.append(rcp)
        bk2 = cx.banks[5]
        w = bk2.pre_write() + [rcp]
        mm = t_mm(cx, bk2.ap, cx.ones_r[:], cx.rs0[0:1, :], True, True,
                  waits=w, inc=cx.s_pe)
        bk2.set_write(mm)
        ngb = em.emit("vector", "tensor_copy", cx.xsB[:], bk2.ap,
                      waits=[bk2.read()], inc=cx.s_dve)
        bk2.rtoks.append(ngb)
        for mc in range(NCH):
            em.emit("vector", "tensor_scalar_add", cx.xs[:],
                    ysb[:, mc, t0:t0 + 512], cx.b2fm[:, mc:mc + 1],
                    waits=[ytoks[mc][tt], cx.tok_l0f])
            em.emit("vector", "tensor_mul", cx.xs[:], cx.xs[:], cx.xsB[:])
            em.emit("vector", "tensor_add", cx.X[:, mc, t0:t0 + 512],
                    cx.X[:, mc, t0:t0 + 512], cx.xs[:], inc=cx.s_dve)


def part_l1a(cx):
    em, nc = cx.em, cx.nc
    if not hasattr(cx, "bias_bo"):
        cx.bias_bo = nc.alloc_sbuf_tensor("bias_bo", [128, 8], F32)
        cx.bias_bq = nc.alloc_sbuf_tensor("bias_bq", [128, 8], F32)
        cx.bias_bk = nc.alloc_sbuf_tensor("bias_bk", [128, 8], F32)
        em.emit("sync", "dma_start", cx.bias_bo[:], cx.H["h_bo"],
                inc=cx.s_ld, amt=16)
        em.emit("sync", "dma_start", cx.bias_bq[:], cx.H["h_bq"],
                inc=cx.s_ld, amt=16)
        cx.tok_l1bias = em.emit("sync", "dma_start", cx.bias_bk[:],
                                cx.H["h_bk"], inc=cx.s_ld, amt=16)
        cx.ct = nc.alloc_sbuf_tensor("ct", [128, 16, 32], F32)
        cx.stb = nc.alloc_sbuf_tensor("stb", [128, 16, 32], F32)
        em.emit("sync", "dma_start", cx.ct[:], cx.H["h_ct"],
                inc=cx.s_ld, amt=16)
        em.emit("sync", "dma_start", cx.stb[:], cx.H["h_st"],
                inc=cx.s_ld, amt=16)
        cx.sel1 = nc.alloc_sbuf_tensor("sel1", [128, 128], BF16)
        cx.sel2 = nc.alloc_sbuf_tensor("sel2", [128, 128], BF16)
        em.emit("gpsimd", "dma_start", cx.sel1[:], cx.H["h_SEL1"],
                inc=cx.s_g3, amt=16)
        cx.tok_l1tab = em.emit("gpsimd", "dma_start", cx.sel2[:],
                               cx.H["h_SEL2"], inc=cx.s_g3, amt=16)

    Q = A(cx, [128, NCH, T], BF16)
    K_ = A(cx, [128, NCH, T], BF16)
    r_row = A(cx, [1, T], F32)
    b_row = A(cx, [1, T], F32)
    V = A(cx, [128, 16, 1024], BF16)

    with contextlib.ExitStack() as st2:
        old = cx.st
        cx.st = st2
        xb = A(cx, [128, NCH, 512], BF16)
        stats_tok = emit_ln_stats(cx, r_row, b_row, xb)
        with contextlib.ExitStack() as st3:
            old3 = cx.st
            cx.st = st3
            ws = WStream(cx, sems=(cx.s_w0, cx.s_w1))
            TTs = [A(cx, [128, 512], BF16), A(cx, [128, 512], BF16)]
            U1 = A(cx, [128, 512], BF16)
            U2 = A(cx, [128, 512], BF16)
            qk_tok = emit_qk_fmajor(
                cx, ws, Q, K_, r_row, b_row, stats_tok, xb,
                rope_bufs=(cx.bias_bq, cx.bias_bk,
                           (cx.H["h_T1"], cx.H["h_T2"]), TTs, U1, U2),
                pros=(cx.pro.get("wq"), cx.pro.get("wk")),
                scrs=(cx.S["wq"], cx.S["wk"]))
            cx.st = old3
        part_barrier(cx)
        bvb = A(cx, [128, 1024], F32)
        bvt0 = em.emit("sync", "dma_start", cx.rs0[:], cx.H["h_bv"][0:1, 0:512],
                       inc=cx.s_ld, amt=16)
        bvt1 = em.emit("sync", "dma_start", cx.rs1[:], cx.H["h_bv"][0:1, 512:1024],
                       inc=cx.s_ld, amt=16)
        for half, (rsx, bvt) in enumerate(((cx.rs0, bvt0), (cx.rs1, bvt1))):
            bk = cx.banks[6]
            w = bk.pre_write() + [bvt, cx.tok_setup]
            mm = t_mm(cx, bk.ap, cx.ones_r[:], rsx[0:1, :], True, True,
                      waits=w, inc=cx.s_pe)
            bk.set_write(mm)
            a = em.emit("vector", "tensor_copy",
                        bvb[:, half * 512:(half + 1) * 512], bk.ap,
                        waits=[bk.read()], inc=cx.s_dve)
            bk.rtoks.append(a)
        wvbuf = A(cx, [128, 1, NCH, 256], BF16)
        v_tok = emit_v_tmajor(cx, cx.S["wvT"], V, r_row, b_row, stats_tok,
                              xb, wvbuf, rope_bufs=(bvb, cx.ct, cx.stb),
                              pro=cx.pro.get("wvT"), nbuf=1,
                              sems=(cx.s_w2,))
        cx.st = old
    part_barrier(cx)

    P = A(cx, [128, T], BF16)
    PT = [A(cx, [128, 512], BF16) for _ in range(2)]
    AO = A(cx, [128, NCH, 128], BF16)
    rc = A(cx, [128, 8], F32)
    M3 = A(cx, [128, 128], BF16)
    M4 = A(cx, [128, 128], BF16)
    T34 = [[A(cx, [128, 512], BF16), A(cx, [128, 512], BF16)],
           [A(cx, [128, 512], BF16), A(cx, [128, 512], BF16)]]
    wo = WStream(cx, nbuf=1, sems=(cx.s_w3,))
    t34_toks = [None, None]

    def load_t34(par, t0, pre):
        em.emit("gpsimd", "dma_start", T34[0][par][:],
                cx.H["h_T3"][:, t0:t0 + 512], waits=[pre], inc=cx.s_g3,
                amt=16)
        return em.emit("gpsimd", "dma_start", T34[1][par][:],
                       cx.H["h_T4"][:, t0:t0 + 512], inc=cx.s_g3, amt=16)

    t34_toks[0] = load_t34(0, 0, qk_tok)
    t34_toks[1] = load_t34(1, 512, v_tok)
    cx.dbg = dict(Q=Q, K=K_, V=V, P=P, AO=AO, rc=rc)
    last_ao = [None, None]
    for qt, q0, ao_tok in emit_attn_core(cx, Q, K_, V, AO, P, PT, rc,
                                         qk_tok, v_tok, causal=False,
                                         split_pv=True,
                                         T34=(T34[0], T34[1]), M3=M3, M4=M4,
                                         t34_toks=t34_toks):
        tts = qt // 4
        last_ao[tts % 2] = ao_tok
        if qt % 4 == 3 and qt < 15:
            nxt = tts + 2
            if nxt < 4:
                t34_toks[nxt % 2] = load_t34(nxt % 2, nxt * 512,
                                             last_ao[nxt % 2])
        for mc in range(NCH):
            wi, wtok = wo.load(cx.S["wo"], mc, pro=cx.pro.get("wo"))
            bk = cx.banks[mc % 2]
            w = bk.pre_write() + [wtok, ao_tok, cx.tok_l1bias]
            for kc in range(NCH):
                mm = t_mm(cx, bk.ap[:, 0:128], wo.buf[:, wi, kc],
                          AO[:, kc, :], kc == 0, kc == NCH - 1,
                          waits=w if kc == 0 else (),
                          inc=cx.s_pe if kc == NCH - 1 else None)
            bk.set_write(mm)
            wo.done(wi, mm)
            xt = em.emit("vector", "scalar_tensor_tensor",
                         cx.X[:, mc, q0:q0 + 128], bk.ap[:, 0:128],
                         cx.bias_bo[:, mc:mc + 1], cx.X[:, mc, q0:q0 + 128],
                         ALU.add, ALU.add, waits=[bk.read(), cx.tok_x],
                         inc=cx.s_dve)
            bk.rtoks.append(xt)


def part_l1f(cx):
    em, nc = cx.em, cx.nc
    if not hasattr(cx, "b1f"):
        cx.b1f = nc.alloc_sbuf_tensor("b1f", [128, 32], F32)
        em.emit("sync", "dma_start", cx.b1f[:], cx.H["h_b1f"],
                inc=cx.s_ld, amt=16)
        cx.b2f = nc.alloc_sbuf_tensor("b2f", [128, 8], F32)
        cx.tok_l1f = em.emit("sync", "dma_start", cx.b2f[:], cx.H["h_b2f"],
                             inc=cx.s_ld, amt=16)

    r_row = A(cx, [1, T], F32)
    b_row = A(cx, [1, T], F32)
    xb = A(cx, [128, NCH, 512], BF16)
    stats_tok = emit_ln_stats(cx, r_row, b_row, xb)

    Hb = A(cx, [128, 32, 1024], BF16)
    xbf = A(cx, [128, NCH, 1024], BF16)
    w1s = WStream(cx, sems=(cx.s_w0, cx.s_w1))
    w2s = WStream(cx, sems=(cx.s_w2, cx.s_w3))
    cx.dbg = dict(Hb=Hb, xbf=xbf)

    for ft in range(2):
        f0 = ft * 1024
        emit_xn_tile(cx, cx.banks[6], r_row, b_row, f0, 512,
                     xbf[:, :, 0:512], stats_tok)
        xn_tok = emit_xn_tile(cx, cx.banks[6], r_row, b_row, f0 + 512, 512,
                              xbf[:, :, 512:1024], stats_tok)
        h_last = None
        for mc in range(32):
            wi, wtok = w1s.load(cx.S["w1f"], mc, pro=cx.pro.get("w1f"))
            for g2 in range(2):
                bk = cx.banks[g2 % 2]
                w = bk.pre_write() + [wtok, xn_tok]
                for kc in range(NCH):
                    mm = t_mm(cx, bk.ap, w1s.buf[:, wi, kc],
                              xbf[:, kc, g2 * 512:(g2 + 1) * 512],
                              kc == 0, kc == NCH - 1,
                              waits=w if kc == 0 else (),
                              inc=cx.s_pe if kc == NCH - 1 else None)
                bk.set_write(mm)
                h_last = em.emit("scalar", "activation",
                                 Hb[:, mc, g2 * 512:(g2 + 1) * 512], bk.ap,
                                 AF.Gelu, bias=cx.b1f[:, mc:mc + 1],
                                 waits=[bk.read(), cx.tok_l1f],
                                 inc=cx.s_act)
                bk.rtoks.append(h_last)
            w1s.done(wi, mm)
        for mc in range(NCH):
            pw = {g2: cx.banks[2 + g2].pre_write() for g2 in range(2)}
            for kq in range(4):
                wi, wtok = w2s.load(cx.S["w2f"], mc, kc0=kq * 8,
                                    pro=cx.pro.get("w2f"))
                for g2 in range(2):
                    bk = cx.banks[2 + g2]
                    w = (pw[g2] if kq == 0 else []) + [wtok, h_last]
                    for kc in range(NCH):
                        mm = t_mm(cx, bk.ap, w2s.buf[:, wi, kc],
                                  Hb[:, kq * 8 + kc, g2 * 512:(g2 + 1) * 512],
                                  kq == 0 and kc == 0,
                                  kq == 3 and kc == NCH - 1,
                                  waits=w if kc == 0 else (),
                                  inc=cx.s_pe if kc == NCH - 1 else None)
                    if kq == 3:
                        bk.set_write(mm)
                        sl = slice(f0 + g2 * 512, f0 + (g2 + 1) * 512)
                        xt = em.emit("vector", "scalar_tensor_tensor",
                                     cx.X[:, mc, sl], bk.ap,
                                     cx.b2f[:, mc:mc + 1], cx.X[:, mc, sl],
                                     ALU.add, ALU.add,
                                     waits=[bk.read(), cx.tok_x],
                                     inc=cx.s_dve)
                        bk.rtoks.append(xt)
                w2s.done(wi, mm)


PART_EMITTERS = {"l0a": part_l0a, "l0f": part_l0f, "l1a": part_l1a,
                 "l1f": part_l1f}

PART_GROUPS = [["l0a", "l0f", "l1a", "l1f"]]


def kernel(**inputs):
    common = prep_host(inputs)
    x = np.asarray(inputs["L0_attn_target"], np.float32)
    xs = x.reshape(NCORE, NSEQ, T, NCH, 128)
    cur = [_x_to_dev(xs[c]) for c in range(NCORE)]

    for parts in PART_GROUPS:
        nc = build_program(parts)
        keys = [k for p in parts for k in PART_HOST[p]]
        in_maps = []
        for c in range(NCORE):
            m = {k: common[k] for k in keys}
            m["x_in"] = cur[c]
            in_maps.append(m)
        res = run_bass_kernel_spmd(nc, in_maps, core_ids=list(range(NCORE)))
        cur = [np.asarray(res.results[c]["x_out"]) for c in range(NCORE)]

    out = np.empty((B, T, D), np.float32)
    for c in range(NCORE):
        out[c * NSEQ:(c + 1) * NSEQ] = _x_from_dev(cur[c])
    return out


# version 15 (final)
# speedup vs baseline: 1.9405x; 1.0122x over previous
import contextlib
import numpy as np
import ml_dtypes
import concourse.bass as bass
import concourse.mybir as mybir
from concourse.bass_utils import run_bass_kernel_spmd

F32 = mybir.dt.float32
BF16 = mybir.dt.bfloat16
AF = mybir.ActivationFunctionType
ALU = mybir.AluOpType
AX = mybir.AxisListType

B, T, D = 32, 2048, 1024
NCORE, NSEQ, NCH, NH, HD = 8, 4, 8, 16, 64
HF0, FF1 = 16384, 4096
EPS_LN = 1e-5
EPS_F32 = float(np.finfo(np.float32).eps)
NEGBIG = -1e9

ENGINES = ("sync", "tensor", "vector", "scalar", "gpsimd")


class Sem:
    __slots__ = ("h", "n", "name")
    def __init__(self, nc, name):
        self.h = nc.alloc_semaphore(name)
        self.n = 0
        self.name = name


class Em:
    def __init__(self, nc):
        self.nc = nc
        self.q = {e: [] for e in ENGINES}
        self.waited = {e: {} for e in ENGINES}
        self.last = {e: None for e in ENGINES}
        self.pending = {e: 0 for e in ENGINES}
        self.bar = []
        self.dma_prev = {}
        self.dma_q = {}

    def emit(self, eng, method, *args, waits=(), inc=None, amt=1, **kwargs):
        real = []
        w = self.waited[eng]
        for tok in list(waits) + self.bar:
            if tok is None:
                continue
            sem, cnt, src = tok
            if src == eng or w.get(sem, 0) >= cnt:
                continue
            w[sem] = cnt
            real.append((sem.h, cnt))
        tok = None
        if method is not None:
            if inc is not None:
                if amt == 16:
                    assert self.dma_q.setdefault(inc, eng) == eng, inc
                    prev = self.dma_prev.get(inc)
                    if prev is not None and w.get(inc, 0) < prev:
                        w[inc] = prev
                        real.append((inc.h, prev))
                    inc.n += amt
                    self.dma_prev[inc] = inc.n
                    tok = (inc, inc.n, ("dma", eng))
                else:
                    inc.n += amt
                    tok = (inc, inc.n, eng)
                self.last[eng] = tok
                self.pending[eng] = 0
            else:
                self.pending[eng] += 1
        self.q[eng].append((method, args, kwargs, real,
                            inc.h if inc else None, amt))
        return tok

    def wait(self, eng, *toks):
        self.emit(eng, None, waits=[t for t in toks if t is not None])

    def flush(self, blk):
        for e in ENGINES:
            items = self.q[e]
            if not items:
                continue
            self.q[e] = []

            def body(eng, items=items):
                for method, args, kwargs, waits, inch, amt in items:
                    for semh, cnt in waits:
                        eng.wait_ge(semh, cnt)
                    if method is None:
                        continue
                    ins = getattr(eng, method)(*args, **kwargs)
                    if inch is not None:
                        ins.then_inc(inch, amt)
            getattr(blk, e)(body)


class Bank:
    __slots__ = ("ap", "wtok", "rtoks")
    def __init__(self, ap):
        self.ap = ap
        self.wtok = None
        self.rtoks = []

    def pre_write(self):
        toks = self.rtoks
        self.rtoks = []
        return list(toks)

    def set_write(self, tok):
        self.wtok = tok

    def read(self, tok=None):
        if tok is not None:
            self.rtoks.append(tok)
        return self.wtok



def _tile_lhsT(w_kf):
    K, F = w_kf.shape
    t = w_kf.reshape(K // 128, 128, F // 128, 128)
    return np.ascontiguousarray(t.transpose(2, 1, 0, 3).reshape(F // 128, 128, K))


def _flat_blocks(a):
    f = np.ascontiguousarray(a).reshape(-1)
    assert f.size % (128 * 4096) == 0, f.size
    return np.ascontiguousarray(f.reshape(-1, 128, 4096))


def _fmajor_bias(b):
    return np.ascontiguousarray(np.asarray(b, np.float32).reshape(-1, 128).T)


def _rope_tables():
    pos = np.arange(T, dtype=np.float32)[:, None]
    dim_pair = np.arange(0, HD, 2, dtype=np.float32)[None, :]
    theta = (10000.0 ** (-dim_pair.astype(np.float64) / HD)).astype(np.float32)
    freqs = (pos * theta).astype(np.float32)
    rope = np.concatenate([np.cos(freqs), np.sin(freqs)], -1).astype(np.float32)
    return np.ascontiguousarray(rope[:, 0::2]), np.ascontiguousarray(rope[:, 1::2])


def prep_host(inputs):
    f = lambda k: np.asarray(inputs[k], np.float32)
    wqkv, wproj = f("L0_attn_w_qkv"), f("L0_attn_w_proj")
    w1, w2 = f("L0_ffn_w1"), f("L0_ffn_w2")
    Wq, Wk, Wv, Wo = f("L1_attn_Wq"), f("L1_attn_Wk"), f("L1_attn_Wv"), f("L1_attn_Wo")
    W1f, W2f = f("L1_ffn_W1"), f("L1_ffn_W2")
    perm_h = np.concatenate([np.arange(0, HD, 2), np.arange(1, HD, 2)])
    perm = (np.arange(NH)[:, None] * HD + perm_h[None, :]).reshape(-1)

    m = {}
    m["h_wqk"] = _flat_blocks(_tile_lhsT(wqkv[:, 0:2048]))
    wv_ = wqkv[:, 2048:3072].reshape(8, 128, 4, 256).transpose(2, 1, 0, 3)
    m["h_wv"] = _flat_blocks(wv_.reshape(4, 128, 2048))
    m["h_wproj"] = _flat_blocks(_tile_lhsT(wproj))
    m["h_bproj"] = _fmajor_bias(f("L0_attn_b_proj"))
    m["h_gam"] = _fmajor_bias(f("L0_attn_ln_gamma"))
    m["h_bet"] = _fmajor_bias(f("L0_attn_ln_beta"))
    m["h_gatew"] = np.ascontiguousarray(f("L0_ffn_gate_w").T)
    m["h_gateb"] = f("L0_ffn_gate_b").reshape(1, 8)
    m["h_rmsw"] = f("L0_ffn_rms_w").reshape(1, 8)
    w1cat = np.ascontiguousarray(w1.transpose(1, 0, 2).reshape(1024, HF0))
    m["h_w1t"] = _flat_blocks(_tile_lhsT(w1cat))
    m["h_w2t"] = _flat_blocks(_tile_lhsT(w2.reshape(HF0, 1024)))
    m["h_b1c"] = _fmajor_bias(f("L0_ffn_b1").reshape(-1))
    m["h_b2fm"] = np.ascontiguousarray(
        np.stack([_fmajor_bias(f("L0_ffn_b2")[e]) for e in range(8)]))
    m["h_wq"] = _flat_blocks(_tile_lhsT(np.ascontiguousarray(Wq.T)[:, perm]))
    m["h_wk"] = _flat_blocks(_tile_lhsT(np.ascontiguousarray(Wk.T)[:, perm]))
    wvT_ = np.ascontiguousarray(Wv.T).reshape(8, 128, 4, 256).transpose(2, 1, 0, 3)
    m["h_wvT"] = _flat_blocks(wvT_.reshape(4, 128, 2048))
    m["h_wo"] = _flat_blocks(_tile_lhsT(np.ascontiguousarray(Wo.T)))
    m["h_bq"] = _fmajor_bias(f("L1_attn_bq")[perm])
    m["h_bk"] = _fmajor_bias(f("L1_attn_bk")[perm])
    m["h_bv"] = f("L1_attn_bv").reshape(1, 1024)
    m["h_bo"] = _fmajor_bias(f("L1_attn_bo"))
    m["h_w1f"] = _flat_blocks(_tile_lhsT(np.ascontiguousarray(W1f.T)))
    m["h_w2f"] = _flat_blocks(_tile_lhsT(np.ascontiguousarray(W2f.T)))
    m["h_b1f"] = _fmajor_bias(f("L1_ffn_b1"))
    m["h_b2f"] = _fmajor_bias(f("L1_ffn_b2"))
    c, s = _rope_tables()
    cF, sF = np.ascontiguousarray(c.T), np.ascontiguousarray(s.T)
    stk = lambda a, b: np.ascontiguousarray(np.concatenate([a, b, a, b], 0))
    m["h_T1"] = stk(cF, -sF)
    m["h_T2"] = stk(sF, cF)
    m["h_T3"] = stk(-cF, sF)
    m["h_T4"] = stk(-sF, -cF)
    m["h_ct"] = np.ascontiguousarray(c.reshape(16, 128, 32).transpose(1, 0, 2))
    m["h_st"] = np.ascontiguousarray(s.reshape(16, 128, 32).transpose(1, 0, 2))
    s1 = np.zeros((128, 128), np.float32)
    s2 = np.zeros((128, 128), np.float32)
    for blk in (0, 64):
        for j in range(32):
            s1[blk + j, blk + j] = 1.0
            s1[blk + 32 + j, blk + j] = 1.0
            s2[blk + j, blk + 32 + j] = 1.0
            s2[blk + 32 + j, blk + 32 + j] = 1.0
    m["h_SEL1"] = s1
    m["h_SEL2"] = s2
    return m


def _x_to_dev(xcore):
    return np.ascontiguousarray(xcore.transpose(0, 3, 2, 1))


def _x_from_dev(xdev):
    return np.ascontiguousarray(np.asarray(xdev).transpose(0, 3, 2, 1)).reshape(NSEQ, T, D)


HOST_SHAPES = {
    "h_wqk": [4, 128, 4096], "h_wv": [2, 128, 4096], "h_wproj": [2, 128, 4096],
    "h_bproj": [128, 8], "h_gam": [128, 8], "h_bet": [128, 8],
    "h_gatew": [1024, 8], "h_gateb": [1, 8], "h_rmsw": [1, 8],
    "h_w1t": [32, 128, 4096], "h_w2t": [32, 128, 4096],
    "h_b1c": [128, 128], "h_b2fm": [8, 128, 8],
    "h_wq": [2, 128, 4096], "h_wk": [2, 128, 4096], "h_wvT": [2, 128, 4096],
    "h_wo": [2, 128, 4096], "h_bq": [128, 8], "h_bk": [128, 8],
    "h_bv": [1, 1024], "h_bo": [128, 8],
    "h_w1f": [8, 128, 4096], "h_w2f": [8, 128, 4096],
    "h_b1f": [128, 32], "h_b2f": [128, 8],
    "h_T1": [128, 2048], "h_T2": [128, 2048], "h_T3": [128, 2048],
    "h_T4": [128, 2048], "h_ct": [128, 16, 32], "h_st": [128, 16, 32],
    "h_SEL1": [128, 128], "h_SEL2": [128, 128],
}

PART_HOST = {
    "l0a": ["h_wqk", "h_wv", "h_wproj", "h_bproj", "h_gam", "h_bet"],
    "l0f": ["h_gatew", "h_gateb", "h_rmsw", "h_w1t", "h_w2t", "h_b1c",
            "h_b2fm"],
    "l1a": ["h_wq", "h_wk", "h_wvT", "h_wo", "h_bq", "h_bk", "h_bv", "h_bo",
            "h_T1", "h_T2", "h_T3", "h_T4", "h_ct", "h_st",
            "h_SEL1", "h_SEL2"],
    "l1f": ["h_w1f", "h_w2f", "h_b1f", "h_b2f"],
}


class Ctx:
    pass


def build_program(parts, debug_taps=None, n_seq=NSEQ):
    nc = bass.Bass("TRN2", target_bir_lowering=False)
    em = Em(nc)
    cx = Ctx()
    cx.nc, cx.em = nc, em
    cx.debug_taps = debug_taps or []
    need = set(parts)

    def din(name, shape):
        return nc.dram_tensor(name, list(shape), F32, kind="ExternalInput").ap()

    cx.x_in = din("x_in", [NSEQ, 128, NCH, T])
    cx.H = {k: din(k, HOST_SHAPES[k]) for p in parts for k in PART_HOST[p]}
    cx.x_out = nc.dram_tensor("x_out", [NSEQ, 128, NCH, T], F32,
                              kind="ExternalOutput").ap()

    def scr(name, shape):
        return nc.dram_tensor(name, list(shape), BF16, kind="Internal").ap()

    S = {}
    if "l0a" in need:
        S["wqk"] = scr("s_wqk", [16, 128, 1024])
        S["wv"] = scr("s_wv", [4, 128, 2048])
        S["wproj"] = scr("s_wproj", [8, 128, 1024])
    if "l0f" in need:
        S["gatew"] = scr("s_gatew", [1024, 8])
        S["w1t"] = scr("s_w1t", [128, 128, 1024])
        S["w2t"] = scr("s_w2t", [8, 128, 16384])
        S["xb"] = scr("s_xb", [4, 128, NCH * 512])
    if "l1a" in need:
        S["wq"] = scr("s_wq", [8, 128, 1024])
        S["wk"] = scr("s_wk", [8, 128, 1024])
        S["wvT"] = scr("s_wvT", [4, 128, 2048])
        S["wo"] = scr("s_wo", [8, 128, 1024])
    if "l1f" in need:
        S["w1f"] = scr("s_w1f", [32, 128, 1024])
        S["w2f"] = scr("s_w2f", [8, 128, 4096])
    cx.S = S

    cx.all_sems = []
    for nm in ("s_pe", "s_dve", "s_act", "s_pool", "s_ld", "s_sc", "s_g",
               "s_g2", "s_g3", "s_x", "s_cw", "s_w0", "s_w1", "s_w2",
               "s_w3", "s_w4", "s_w5"):
        s = Sem(nc, nm)
        setattr(cx, nm, s)
        cx.all_sems.append(s)

    S4 = nc.alloc_psum_tensor("psum_S", [128, 2048], F32)
    C2 = nc.alloc_psum_tensor("psum_C", [128, 1024], F32)
    D1 = nc.alloc_psum_tensor("psum_D", [128, 512], F32)
    Tp = nc.alloc_psum_tensor("psum_T", [128, 1024], BF16)
    cx.banks = ([Bank(S4[:, i * 512:(i + 1) * 512]) for i in range(4)]
                + [Bank(C2[:, i * 512:(i + 1) * 512]) for i in range(2)]
                + [Bank(D1[:]), Bank(Tp[:])])
    cx.S4 = S4

    cx.X = nc.alloc_sbuf_tensor("X", [128, NCH, T], F32)
    cx.ones_bf = nc.alloc_sbuf_tensor("ones_bf", [128, 1], BF16)
    cx.ones_f = nc.alloc_sbuf_tensor("ones_f", [128, 1], F32)
    cx.ones_r = nc.alloc_sbuf_tensor("ones_r", [1, 128], F32)
    cx.ident_bf = nc.alloc_sbuf_tensor("ident_bf", [128, 128], BF16)
    cx.ident_f = nc.alloc_sbuf_tensor("ident_f", [128, 128], F32)
    cx.mask = nc.alloc_sbuf_tensor("mask", [128, 128], BF16)
    cx.rs0 = nc.alloc_sbuf_tensor("rs0", [1, 512], F32)
    cx.rs1 = nc.alloc_sbuf_tensor("rs1", [1, 512], F32)
    cx.xs = nc.alloc_sbuf_tensor("xs", [128, 512], F32)
    cx.xsB = nc.alloc_sbuf_tensor("xsB", [128, 512], F32)

    emit_setup(cx)
    emit_prologue(cx, need)

    for seq in range(n_seq):
        cx.seq = seq
        emit_x_load(cx, seq)
        for p in parts:
            part_barrier(cx)
            with contextlib.ExitStack() as st:
                cx.st = st
                PART_EMITTERS[p](cx)
                part_barrier(cx)
                for tp, name, fn in cx.debug_taps:
                    if tp == p and seq == n_seq - 1:
                        emit_debug_tap(cx, name, fn)
                part_barrier(cx)
        emit_x_store(cx, seq, seq == n_seq - 1)

    part_barrier(cx)
    for e in ENGINES:
        cx.em.wait("sync", cx.em.last[e])
    for s in cx.all_sems:
        if s.n:
            cx.em.wait("sync", (s, s.n, None))
    for s in cx.all_sems:
        em.emit("sync", "sem_clear", s.h)

    with nc.Block() as blk:
        em.flush(blk)
    return nc


_anon = [0]

def A(cx, shape, dtype):
    _anon[0] += 1
    return cx.st.enter_context(
        cx.nc.sbuf_tensor(f"t{_anon[0]}", list(shape), dtype))


def part_barrier(cx):
    em = cx.em
    if em.pending["vector"]:
        em.emit("vector", "engine_nop", inc=cx.s_dve)
    if em.pending["gpsimd"]:
        em.emit("gpsimd", "engine_nop", inc=cx.s_pool)
    if em.pending["scalar"]:
        em.emit("scalar", "copy", cx.rs1[0:1, 0:1], cx.rs1[0:1, 0:1],
                inc=cx.s_act)
    assert em.pending["tensor"] == 0, em.pending["tensor"]
    assert em.pending["sync"] == 0
    em.bar = [t for t in (em.last[e] for e in ENGINES) if t is not None]


def t_mm(cx, out_ap, lhsT, rhs, start, stop, waits=(), inc=None, tp=None):
    kw = {"tile_position": tp} if tp is not None else {}
    return cx.em.emit("tensor", "matmul", out_ap, lhsT, rhs, start=start,
                      stop=stop, waits=waits, inc=inc, **kw)


def tiny(cx, eng, method, *args, waits=(), **kw):
    sem = {"vector": cx.s_dve, "scalar": cx.s_act, "gpsimd": cx.s_pool}[eng]
    t = cx.em.emit(eng, method, *args, waits=waits, inc=sem, **kw)
    return (t[0], t[1], None)


def emit_setup(cx):
    em, nc = cx.em, cx.nc
    iota_cm = nc.sbuf_tensor("iota_row", [128, 128], F32)
    iota_row = iota_cm.__enter__()
    pc_cm = nc.sbuf_tensor("part_col", [128, 1], F32)
    part_col = pc_cm.__enter__()
    em.emit("gpsimd", "iota", iota_row[:], pattern=[[1, 128]], base=0,
            channel_multiplier=0, allow_small_or_imprecise_dtypes=True)
    tok = em.emit("gpsimd", "iota", part_col[:], pattern=[[1, 1]], base=0,
                  channel_multiplier=1, allow_small_or_imprecise_dtypes=True,
                  inc=cx.s_pool)
    em.emit("vector", "tensor_scalar", cx.ident_f[:], iota_row[:],
            part_col[:, 0:1], None, ALU.is_equal, waits=[tok])
    em.emit("vector", "tensor_copy", cx.ident_bf[:], cx.ident_f[:])
    em.emit("vector", "tensor_scalar", cx.mask[:], iota_row[:],
            part_col[:, 0:1], float(NEGBIG), ALU.is_gt, ALU.mult)
    em.emit("vector", "memset", cx.ones_bf[:], 1.0)
    em.emit("vector", "memset", cx.ones_f[:], 1.0)
    em.emit("vector", "memset", cx.rs1[:], 0.0)
    cx.tok_setup = em.emit("vector", "memset", cx.ones_r[:], 1.0, inc=cx.s_dve)
    pc_cm.__exit__(None, None, None)
    iota_cm.__exit__(None, None, None)


def _flat_view(scr_ap):
    dims = " ".join(f"a{i}" for i in range(scr_ap.ndim))
    flat = scr_ap.rearrange(f"{dims} -> ({dims})")
    return flat.rearrange("(n p c) -> n p c", p=128, c=4096)


def emit_prologue(cx, need):
    em, nc = cx.em, cx.nc
    with contextlib.ExitStack() as st:
        st_b = st.enter_context(nc.sbuf_tensor([128, 2, 4096], BF16))
        free = [None, None]

        def cast(host_ap, scr_ap):
            blocks = _flat_view(scr_ap)
            for i in range(host_ap.shape[0]):
                par = i % 2
                ld = em.emit("gpsimd", "dma_start", st_b[:, par], host_ap[i],
                             waits=[free[par]],
                             inc=(cx.s_g, cx.s_g2)[par], amt=16)
                free[par] = em.emit("sync", "dma_start", blocks[i], st_b[:, par],
                                    waits=[ld], inc=cx.s_ld, amt=16)
            return free[(host_ap.shape[0] - 1) % 2]

        toks = {}
        if "l0a" in need:
            toks["wqk"] = cast(cx.H["h_wqk"], cx.S["wqk"])
            toks["wv"] = cast(cx.H["h_wv"], cx.S["wv"])
            toks["wproj"] = cast(cx.H["h_wproj"], cx.S["wproj"])
        if "l0f" in need:
            gv = cx.H["h_gatew"].rearrange("(a p) e -> p a e", p=128)
            sv = cx.S["gatew"].rearrange("(a p) e -> p a e", p=128)
            stg = st_b[:, 0, 0:64].rearrange("p (a e) -> p a e", a=8)
            ld = em.emit("gpsimd", "dma_start", stg, gv,
                         waits=[free[0]], inc=cx.s_g, amt=16)
            free[0] = toks["gatew"] = em.emit(
                "sync", "dma_start", sv, stg, waits=[ld],
                inc=cx.s_ld, amt=16)
            toks["w1t"] = cast(cx.H["h_w1t"], cx.S["w1t"])
            toks["w2t"] = cast(cx.H["h_w2t"], cx.S["w2t"])
        if "l1a" in need:
            toks["wq"] = cast(cx.H["h_wq"], cx.S["wq"])
            toks["wk"] = cast(cx.H["h_wk"], cx.S["wk"])
            toks["wvT"] = cast(cx.H["h_wvT"], cx.S["wvT"])
            toks["wo"] = cast(cx.H["h_wo"], cx.S["wo"])
        if "l1f" in need:
            toks["w1f"] = _ternary(cx, st, cx.H["h_w1f"], cx.S["w1f"], "w1f")
            toks["w2f"] = _ternary(cx, st, cx.H["h_w2f"], cx.S["w2f"], "w2f")
        cx.pro = toks
        part_barrier_prologue(cx)


def part_barrier_prologue(cx):
    em = cx.em
    if em.pending["vector"]:
        em.emit("vector", "engine_nop", inc=cx.s_dve)
    if em.pending["gpsimd"]:
        em.emit("gpsimd", "engine_nop", inc=cx.s_pool)
    em.bar = [t for t in (em.last[e] for e in ENGINES) if t is not None]


def _ternary(cx, st_outer, host_ap, scr_ap, name):
    em, nc = cx.em, cx.nc
    n = host_ap.shape[0]
    st = contextlib.ExitStack()
    stf = st.enter_context(nc.sbuf_tensor(f"tf_{name}", [128, 2, 4096], F32))
    sb = st.enter_context(nc.sbuf_tensor(f"tb_{name}", [128, 4096], BF16))
    acc = st.enter_context(nc.sbuf_tensor(f"ta_{name}", [128, 8], F32))
    rsc = st.enter_context(nc.sbuf_tensor(f"tr_{name}", [128, 2], F32))
    t1 = st.enter_context(nc.sbuf_tensor(f"t1_{name}", [128, 4096], F32))
    t2 = st.enter_context(nc.sbuf_tensor(f"t2_{name}", [128, 4096], F32))
    lds = [None, None]
    cons = [None, None]
    vtok = None
    for i in range(n):
        par = i % 2
        lds[par] = em.emit("sync", "dma_start", stf[:, par], host_ap[i],
                           waits=[cons[par]], inc=cx.s_ld, amt=16)
        cons[par] = em.emit("scalar", "activation", t1[:], stf[:, par],
                            AF.Abs, waits=[lds[par], vtok], inc=cx.s_act)
        w_ = 2048
        first = True
        while w_ >= 64:
            em.emit("vector", "tensor_add", t1[:, 0:w_], t1[:, 0:w_],
                    t1[:, w_:2 * w_], waits=[cons[par]] if first else ())
            first = False
            w_ //= 2
        tk = None
        while w_ >= 1:
            tk = tiny(cx, "vector", "tensor_add", t1[:, 0:w_], t1[:, 0:w_],
                      t1[:, w_:2 * w_], waits=[tk])
            w_ //= 2
        vtok = tiny(cx, "vector", "tensor_copy", acc[:, i:i + 1],
                    t1[:, 0:1], waits=[tk])
    w_ = n // 2
    while w_ >= 1:
        vtok = tiny(cx, "vector", "tensor_add", acc[:, 0:w_], acc[:, 0:w_],
                    acc[:, w_:2 * w_], waits=[vtok])
        w_ //= 2
    bk = cx.banks[6]
    mm = em.emit("tensor", "transpose", bk.ap[0:1, 0:128], acc[:, 0:1],
                 cx.ident_f[:], waits=bk.pre_write() + [vtok, cx.tok_setup],
                 inc=cx.s_pe)
    bk.set_write(mm)
    rr = tiny(cx, "vector", "tensor_copy", cx.rs0[0:1, 0:128],
              bk.ap[0:1, 0:128], waits=[bk.read()])
    bk.rtoks.append(rr)
    w_ = 64
    while w_ >= 1:
        rr = tiny(cx, "vector", "tensor_add", cx.rs0[0:1, 0:w_],
                  cx.rs0[0:1, 0:w_], cx.rs0[0:1, w_:2 * w_], waits=[rr])
        w_ //= 2
    assert n * 128 * 4096 == 1 << 22
    a = tiny(cx, "vector", "tensor_scalar", cx.rs0[0:1, 0:1],
             cx.rs0[0:1, 0:1], float(2.0 ** -22), None, ALU.mult, waits=[rr])
    a = tiny(cx, "vector", "tensor_scalar_max", cx.rs0[0:1, 0:1],
             cx.rs0[0:1, 0:1], 1e-8, waits=[a])
    a = tiny(cx, "vector", "tensor_scalar", cx.rs0[0:1, 1:2],
             cx.rs0[0:1, 0:1], 0.5, None, ALU.mult, waits=[a])
    stok = tiny(cx, "vector", "tensor_scalar", cx.rs0[0:1, 2:3],
                cx.rs0[0:1, 1:2], -1.0, None, ALU.mult, waits=[a])
    mm2 = t_mm(cx, bk.ap[:, 0:2], cx.ones_r[:], cx.rs0[0:1, 1:3], True, True,
               waits=bk.pre_write() + [stok], inc=cx.s_pe)
    bk.set_write(mm2)
    cp = tiny(cx, "vector", "tensor_copy", rsc[:, 0:2], bk.ap[:, 0:2],
              waits=[bk.read()])
    bk.rtoks.append(cp)
    blocks = _flat_view(scr_ap)
    out_tok = None
    for i in range(n):
        par = i % 2
        lds[par] = em.emit("sync", "dma_start", stf[:, par], host_ap[i],
                           waits=[cp if i < 2 else None, out_tok],
                           inc=cx.s_ld, amt=16)
        em.emit("vector", "tensor_scalar", t2[:], stf[:, par],
                rsc[:, 0:1], None, ALU.is_gt, waits=[lds[par], cp])
        em.emit("vector", "tensor_scalar", t1[:], stf[:, par],
                rsc[:, 1:2], None, ALU.is_lt)
        wtk = em.emit("vector", "tensor_sub", sb[:], t2[:], t1[:],
                      inc=cx.s_dve)
        out_tok = em.emit("sync", "dma_start", blocks[i], sb[:], waits=[wtk],
                          inc=cx.s_ld, amt=16)
    st.close()
    return out_tok


def emit_x_load(cx, seq):
    cx.tok_x = cx.em.emit("sync", "dma_start", cx.X[:], cx.x_in[seq],
                          waits=[getattr(cx, "tok_xstore", None)],
                          inc=cx.s_x, amt=16)


def emit_x_store(cx, seq, final):
    em = cx.em
    cx.tok_xstore = em.emit("scalar", "dma_start", cx.x_out[seq], cx.X[:],
                            waits=[cx.tok_x], inc=cx.s_sc, amt=16)
    if final:
        em.wait("scalar", cx.tok_xstore)


def emit_debug_tap(cx, name, fn):
    ap = fn(cx)
    out = cx.nc.dram_tensor(f"dbg_{name}", list(ap.shape), ap.dtype,
                            kind="ExternalOutput").ap()
    tok = cx.em.emit("scalar", "dma_start", out, ap, inc=cx.s_sc, amt=16)
    cx.em.wait("scalar", tok)



def emit_ln_stats(cx, r_row, b_row, xb):
    em = cx.em
    last = None
    sq_free = [None, None]
    _anon[0] += 1
    sq_cm = cx.nc.sbuf_tensor(f"sq{_anon[0]}", [128, 2, 512], BF16)
    sq = sq_cm.__enter__()
    for tt in range(4):
        sl = slice(tt * 512, (tt + 1) * 512)
        b0, b1 = cx.banks[0], cx.banks[1]
        w0 = b0.pre_write() + [cx.tok_setup]
        w1 = b1.pre_write()
        for c in range(NCH):
            ct = em.emit("vector", "tensor_copy", xb[:, c], cx.X[:, c, sl],
                         waits=[cx.tok_x], inc=cx.s_dve)
            sqt = em.emit("gpsimd", "tensor_mul", sq[:, c % 2], xb[:, c],
                          xb[:, c], waits=[ct, sq_free[c % 2]],
                          inc=cx.s_pool)
            mm0 = t_mm(cx, b0.ap[0:1, :], cx.ones_bf[:], xb[:, c], c == 0,
                       c == NCH - 1, waits=w0 + [ct],
                       inc=cx.s_pe if c == NCH - 1 else None)
            mm1 = t_mm(cx, b1.ap[0:1, :], cx.ones_bf[:], sq[:, c % 2], c == 0,
                       c == NCH - 1, waits=w1 + [sqt], inc=cx.s_pe)
            sq_free[c % 2] = mm1
        b0.set_write(mm0)
        b1.set_write(mm1)
        a0 = tiny(cx, "vector", "tensor_scalar", cx.rs0[0:1, :],
                  b0.ap[0:1, :], 1.0 / D, None, ALU.mult, waits=[b0.read()])
        b0.rtoks.append(a0)
        a1 = tiny(cx, "vector", "tensor_scalar", cx.rs1[0:1, :],
                  b1.ap[0:1, :], 1.0 / D, None, ALU.mult, waits=[b1.read()])
        b1.rtoks.append(a1)
        m2 = tiny(cx, "vector", "tensor_mul", cx.xs[0:1, :], cx.rs0[0:1, :],
                  cx.rs0[0:1, :], waits=[a0])
        s1 = tiny(cx, "vector", "tensor_sub", cx.rs1[0:1, :], cx.rs1[0:1, :],
                  cx.xs[0:1, :], waits=[a1, m2])
        vt = tiny(cx, "vector", "tensor_scalar_add", cx.rs1[0:1, :],
                  cx.rs1[0:1, :], EPS_LN, waits=[s1])
        sq2 = tiny(cx, "scalar", "activation", cx.rs1[0:1, :],
                   cx.rs1[0:1, :], AF.Sqrt, waits=[vt])
        rr = tiny(cx, "vector", "reciprocal", r_row[0:1, sl],
                  cx.rs1[0:1, :], waits=[sq2])
        bb = tiny(cx, "vector", "tensor_mul", b_row[0:1, sl],
                  cx.rs0[0:1, :], r_row[0:1, sl], waits=[rr, a0])
        last = tiny(cx, "vector", "tensor_scalar_mul", b_row[0:1, sl],
                    b_row[0:1, sl], -1.0, waits=[bb])
    sq_cm.__exit__(None, None, None)
    part_barrier(cx)
    return last


def emit_bcast_rb(cx, bank, r_row, b_row, t0, L, stats_tok):
    w = bank.pre_write() + [stats_tok, cx.tok_setup]
    t_mm(cx, bank.ap[:, 0:L], cx.ones_r[:], r_row[0:1, t0:t0 + L], True, True,
         waits=w)
    mm = t_mm(cx, bank.ap[:, 256:256 + L], cx.ones_r[:],
              b_row[0:1, t0:t0 + L], True, True, inc=cx.s_pe)
    bank.set_write(mm)
    return mm


def emit_xn_tile(cx, bank, r_row, b_row, t0, L, out, stats_tok,
                 out_free=None):
    em = cx.em
    last = None
    for h0 in range(0, L, 256):
        Lh = min(256, L - h0)
        emit_bcast_rb(cx, bank, r_row, b_row, t0 + h0, Lh, stats_tok)
        for c in range(NCH):
            em.emit("vector", "tensor_mul", cx.xs[:, 0:Lh],
                    cx.X[:, c, t0 + h0:t0 + h0 + Lh], bank.ap[:, 0:Lh],
                    waits=[bank.read(), cx.tok_x, out_free])
            last = em.emit("vector", "tensor_add", out[:, c, h0:h0 + Lh],
                           cx.xs[:, 0:Lh], bank.ap[:, 256:256 + Lh],
                           inc=cx.s_dve)
            bank.rtoks.append(last)
    return last


class WStream:

    def __init__(self, cx, nkc=8, nbuf=2, sems=None):
        self.cx = cx
        self.nkc = nkc
        self.buf = A(cx, [128, nbuf, nkc, 128], BF16)
        self.nbuf = nbuf
        self.free = [None] * nbuf
        self.cnt = 0
        self.sems = sems or [cx.s_ld] * nbuf

    def load(self, scr, mc, kc0=0, pro=None):
        i = self.cnt % self.nbuf
        self.cnt += 1
        src = scr[mc][:, kc0 * 128:(kc0 + self.nkc) * 128].rearrange(
            "p (kc j) -> p kc j", j=128)
        tok = self.cx.em.emit("sync", "dma_start", self.buf[:, i], src,
                              waits=[self.free[i], pro],
                              inc=self.sems[i], amt=16)
        return i, tok

    def done(self, i, tok):
        self.free[i] = tok



def emit_attn_core(cx, Q, K_, V, AO, P, PT, rc, qk_tok, v_tok, causal,
                   split_pv, T34=None, M3=None, M4=None, t34_toks=None):
    em = cx.em
    banks = cx.banks
    for qt in range(16):
        q0 = qt * 128
        L = q0 + 128 if causal else T
        nb = (L + 511) // 512
        nkb = L // 128

        def s_phase(pair, hh):
            rows = slice(hh * 64, (hh + 1) * 64)
            for kb in range(nb):
                N = min(512, L - kb * 512)
                bk = banks[kb]
                mm = t_mm(cx, bk.ap[:, 0:N],
                          Q[:, pair, q0:q0 + 128][rows, :],
                          K_[:, pair, kb * 512:kb * 512 + N][rows, :],
                          True, True, waits=bk.pre_write() + [qk_tok],
                          inc=cx.s_pe)
                bk.set_write(mm)
            if causal:
                dkb, dof = divmod(L - 128, 512)
                mt = em.emit("vector", "tensor_add",
                             banks[dkb].ap[:, dof:dof + 128],
                             banks[dkb].ap[:, dof:dof + 128], cx.mask[:],
                             waits=[banks[dkb].read()], inc=cx.s_dve)
                banks[dkb].set_write(mt)
            rms = []
            for kb in range(nb):
                N = min(512, L - kb * 512)
                rms.append(tiny(cx, "vector", "reduce_max",
                                rc[:, kb:kb + 1], banks[kb].ap[:, 0:N],
                                axis=AX.X, waits=[banks[kb].read()]))
            mx = tiny(cx, "vector", "reduce_max", rc[:, 4:5], rc[:, 0:nb],
                      axis=AX.X, waits=rms)
            return tiny(cx, "vector", "tensor_scalar_mul", rc[:, 4:5],
                        rc[:, 4:5], -0.125, waits=[mx])

        def softmax_phase(nm, extra):
            ex = None
            for kb in range(nb):
                N = min(512, L - kb * 512)
                ex = em.emit("scalar", "activation",
                             P[:, kb * 512:kb * 512 + N],
                             cx.S4[:, kb * 512:kb * 512 + N], AF.Exp,
                             bias=rc[:, 4:5], scale=0.125,
                             accum_out=rc[:, kb:kb + 1],
                             waits=[nm] + ([extra] if kb == 0 else []),
                             inc=cx.s_act)
                banks[kb].rtoks.append(ex)
            sm = tiny(cx, "vector", "reduce_sum", rc[:, 5:6], rc[:, 0:nb],
                      axis=AX.X, waits=[ex])
            rp = tiny(cx, "vector", "reciprocal", rc[:, 6:7],
                      rc[:, 5:6], waits=[sm])
            return em.emit("vector", "tensor_scalar_mul", P[:, 0:L],
                           P[:, 0:L], rc[:, 6:7], waits=[rp],
                           inc=cx.s_dve)

        def ptpv_phase(pair, hh, ptok, out_bk, ow):
            head = pair * 2 + hh
            pv_last = None
            last_tr = None
            for g0 in range(0, nkb, 4):
                kbs = list(range(g0, min(g0 + 4, nkb)))
                tb = banks[7]
                wtb = tb.pre_write()
                for kb in kbs:
                    j = kb - g0
                    tr = em.emit("tensor", "transpose",
                                 tb.ap[:, j * 128:(j + 1) * 128],
                                 P[:, kb * 128:(kb + 1) * 128],
                                 cx.ident_bf[:],
                                 waits=(wtb if j == 0 else [])
                                 + [ptok, cx.tok_setup],
                                 inc=cx.s_pe if kb == kbs[-1] else None)
                tb.set_write(tr)
                last_tr = tr
                pt = PT[(g0 // 4) % 2]
                ev = em.emit("scalar", "copy",
                             pt[:, 0:len(kbs) * 128],
                             tb.ap[:, 0:len(kbs) * 128],
                             waits=[tb.read()], inc=cx.s_act)
                tb.rtoks.append(ev)
                for kb in kbs:
                    j = kb - g0
                    ptj = pt[:, j * 128:(j + 1) * 128]
                    w0 = [ev, v_tok] + (ow if kb == 0 and hh == 0 else [])
                    if not split_pv:
                        pv_last = t_mm(
                            cx, out_bk.ap[slice(hh * 64, hh * 64 + 64),
                                          0:128],
                            V[:, kb, head * 64:(head + 1) * 64], ptj,
                            kb == 0, kb == nkb - 1, waits=w0,
                            tp=(0, hh * 64),
                            inc=cx.s_pe if kb == nkb - 1 else None)
                    else:
                        vh = V[:, kb, head * 64:(head + 1) * 64]
                        vv = vh.rearrange("p (j two) -> p two j", two=2)
                        for par in range(2):
                            r0 = hh * 64 + par * 32
                            pv_last = t_mm(
                                cx, out_bk.ap[r0:r0 + 32, 0:128],
                                vv[:, par, :], ptj, kb == 0,
                                kb == nkb - 1, waits=w0, tp=(0, r0),
                                inc=cx.s_pe if kb == nkb - 1 and par == 1
                                else None)
            return pv_last, last_tr

        for pair in range(8):
            out_bk = banks[4 + (pair % 2)]
            ow = out_bk.pre_write()
            nm0 = s_phase(pair, 0)
            ptok0 = softmax_phase(nm0, None)
            nm1 = s_phase(pair, 1)
            _, tr0 = ptpv_phase(pair, 0, ptok0, out_bk, ow)
            ptok1 = softmax_phase(nm1, tr0)
            pv_last, _ = ptpv_phase(pair, 1, ptok1, out_bk, ow)
            out_bk.set_write(pv_last)
            if T34 is None:
                ao_tok = em.emit("vector", "tensor_copy", AO[:, pair, :],
                                 out_bk.ap[:, 0:128], waits=[out_bk.read()],
                                 inc=cx.s_dve)
                out_bk.rtoks.append(ao_tok)
            else:
                tts = qt // 4
                tj = (qt % 4) * 128
                a = em.emit("vector", "tensor_mul", M3[:],
                            out_bk.ap[:, 0:128],
                            T34[0][tts % 2][:, tj:tj + 128],
                            waits=[out_bk.read(), t34_toks[tts % 2]],
                            inc=cx.s_dve)
                out_bk.rtoks.append(a)
                a = em.emit("vector", "tensor_mul", M4[:],
                            out_bk.ap[:, 0:128],
                            T34[1][tts % 2][:, tj:tj + 128], inc=cx.s_dve)
                out_bk.rtoks.append(a)
                rb = banks[6]
                w = rb.pre_write() + [a, cx.tok_l1tab]
                t_mm(cx, rb.ap[:, 0:128], cx.sel1[:], M3[:], True, False,
                     waits=w)
                mm2 = t_mm(cx, rb.ap[:, 0:128], cx.sel2[:], M4[:], False,
                           True, inc=cx.s_pe)
                rb.set_write(mm2)
                ao_tok = em.emit("vector", "tensor_copy", AO[:, pair, :],
                                 rb.ap[:, 0:128], waits=[rb.read()],
                                 inc=cx.s_dve)
                rb.rtoks.append(ao_tok)
        yield qt, q0, ao_tok



def emit_qk_fmajor(cx, ws, Q, K_, r_row, b_row, stats_tok, xb, rope_bufs=None,
                   pros=(None, None), scrs=(None, None)):
    em = cx.em
    rope = rope_bufs is not None
    if rope:
        bq, bk_, TTd, TTs, U1, U2 = rope_bufs
        tt_free = None
    ev = None
    for tt in range(4):
        t0 = tt * 512
        xn_tok = emit_xn_tile(cx, cx.banks[6], r_row, b_row, t0, 512, xb,
                              stats_tok)
        if rope:
            t12 = em.emit("gpsimd", "dma_start", TTs[0][:],
                          TTd[0][:, t0:t0 + 512], waits=[tt_free],
                          inc=cx.s_g3, amt=16)
            t12 = em.emit("gpsimd", "dma_start", TTs[1][:],
                          TTd[1][:, t0:t0 + 512], inc=cx.s_g3, amt=16)
        for fc in range(16):
            if scrs[1] is None:
                scr_w, mc, pro = scrs[0], fc, pros[0]
            else:
                scr_w, mc, pro = ((scrs[0], fc, pros[0]) if fc < 8 else
                                  (scrs[1], fc - 8, pros[1]))
            wi, wtok = ws.load(scr_w, mc, pro=pro)
            bk = cx.banks[fc % 2]
            w = bk.pre_write() + [wtok, xn_tok]
            for kc in range(NCH):
                mm = t_mm(cx, bk.ap, ws.buf[:, wi, kc], xb[:, kc],
                          kc == 0, kc == NCH - 1, waits=w if kc == 0 else (),
                          inc=cx.s_pe if kc == NCH - 1 else None)
            bk.set_write(mm)
            ws.done(wi, mm)
            dst = (Q if fc < 8 else K_)[:, fc % 8, t0:t0 + 512]
            if not rope:
                ev = em.emit("vector", "tensor_copy", dst, bk.ap,
                             waits=[bk.read()], inc=cx.s_dve)
                bk.rtoks.append(ev)
            else:
                bias = (bq if fc < 8 else bk_)[:, fc % 8:fc % 8 + 1]
                a = em.emit("vector", "tensor_scalar_add", cx.xs[:], bk.ap,
                            bias, waits=[bk.read(), t12], inc=cx.s_dve)
                bk.rtoks.append(a)
                u1 = em.emit("vector", "tensor_mul", U1[:], cx.xs[:],
                             TTs[0][:, :])
                u2 = em.emit("vector", "tensor_mul", U2[:], cx.xs[:],
                             TTs[1][:, :], inc=cx.s_dve)
                rb = cx.banks[2]
                w = rb.pre_write() + [u2, cx.tok_l1tab]
                t_mm(cx, rb.ap, cx.sel1[:], U1[:], True, False, waits=w)
                mm2 = t_mm(cx, rb.ap, cx.sel2[:], U2[:], False, True,
                           inc=cx.s_pe)
                rb.set_write(mm2)
                ev = em.emit("vector", "tensor_copy", dst, rb.ap,
                             waits=[rb.read()], inc=cx.s_dve)
                rb.rtoks.append(ev)
                if fc == 15:
                    tt_free = ev
    return ev


def emit_v_tmajor(cx, scr_wv, V, r_row, b_row, stats_tok, xb, wvbuf,
                  rope_bufs=None, pro=None, nbuf=2, sems=None):
    em = cx.em
    rope = rope_bufs is not None
    if rope:
        bvb, ct, st_ = rope_bufs
    wv = scr_wv
    free = [None, None]
    ev = None
    for tt in range(4):
        t0 = tt * 512
        xn_tok = emit_xn_tile(cx, cx.banks[6], r_row, b_row, t0, 512, xb,
                              stats_tok)
        for g in range(4):
            vi = g % nbuf
            vtok = em.emit("sync", "dma_start", wvbuf[:, vi],
                           wv[g].rearrange("p (kc j) -> p kc j", j=256),
                           waits=[pro, free[vi]],
                           inc=(sems or (cx.s_w2,) * nbuf)[vi], amt=16)
            for q4 in range(4):
                tq = tt * 4 + q4
                bk = cx.banks[2 + (q4 % 2)]
                w = bk.pre_write() + [vtok, xn_tok]
                for kc in range(NCH):
                    mm = t_mm(cx, bk.ap[:, 0:256],
                              xb[:, kc, q4 * 128:(q4 + 1) * 128],
                              wvbuf[:, vi, kc], kc == 0, kc == NCH - 1,
                              waits=w if kc == 0 else (),
                              inc=cx.s_pe if kc == NCH - 1 else None)
                bk.set_write(mm)
                last_mm = mm
                dst = V[:, tq, g * 256:(g + 1) * 256]
                if not rope:
                    ev = em.emit("vector", "tensor_copy", dst,
                                 bk.ap[:, 0:256], waits=[bk.read()],
                                 inc=cx.s_dve)
                    bk.rtoks.append(ev)
                else:
                    a = em.emit("vector", "tensor_add", cx.xs[:, 0:256],
                                bk.ap[:, 0:256],
                                bvb[:, g * 256:(g + 1) * 256],
                                waits=[bk.read()], inc=cx.s_dve)
                    bk.rtoks.append(a)
                    vv = cx.xs[:, 0:256].rearrange(
                        "p (h j two) -> p h j two", h=4, two=2)
                    evn, od = vv[:, :, :, 0], vv[:, :, :, 1]
                    ctb = ct[:, tq:tq + 1, :].broadcast_to([128, 4, 32])
                    stb = st_[:, tq:tq + 1, :].broadcast_to([128, 4, 32])
                    dh = dst.rearrange("p (h half j) -> p h half j", h=4,
                                       half=2)
                    xA = cx.xsB[:, 0:128].rearrange("p (h j) -> p h j", h=4)
                    xBv = cx.xsB[:, 128:256].rearrange("p (h j) -> p h j", h=4)
                    em.emit("vector", "tensor_mul", xA, evn, ctb)
                    em.emit("vector", "tensor_mul", xBv, od, stb)
                    em.emit("vector", "tensor_sub", dh[:, :, 0], xA, xBv)
                    em.emit("vector", "tensor_mul", xA, evn, stb)
                    em.emit("vector", "tensor_mul", xBv, od, ctb)
                    ev = em.emit("vector", "tensor_add", dh[:, :, 1], xA, xBv,
                                 inc=cx.s_dve)
            free[vi] = last_mm
    return ev



def part_l0a(cx):
    em, nc = cx.em, cx.nc
    if not hasattr(cx, "bias_bproj"):
        cx.bias_bproj = nc.alloc_sbuf_tensor("bias_bproj", [128, 8], F32)
        cx.bias_gam = nc.alloc_sbuf_tensor("bias_gam", [128, 8], F32)
        cx.bias_bet = nc.alloc_sbuf_tensor("bias_bet", [128, 8], F32)
        em.emit("sync", "dma_start", cx.bias_bproj[:], cx.H["h_bproj"],
                inc=cx.s_ld, amt=16)
        em.emit("sync", "dma_start", cx.bias_gam[:], cx.H["h_gam"],
                inc=cx.s_ld, amt=16)
        cx.tok_l0bias = em.emit("sync", "dma_start", cx.bias_bet[:],
                                cx.H["h_bet"], inc=cx.s_ld, amt=16)

    if cx.debug_taps:
        cx.dbg_ln2 = cx.nc.dram_tensor("dbg_ln2", [1, 4 * T], F32,
                                       kind="ExternalOutput").ap()
    Q = A(cx, [128, NCH, T], BF16)
    K_ = A(cx, [128, NCH, T], BF16)
    r_row = A(cx, [1, T], F32)
    b_row = A(cx, [1, T], F32)
    V = A(cx, [128, 16, 1024], BF16)

    with contextlib.ExitStack() as st2:
        old = cx.st
        cx.st = st2
        xb = A(cx, [128, NCH, 512], BF16)
        stats_tok = emit_ln_stats(cx, r_row, b_row, xb)
        cx.tok_stats_l0a = stats_tok
        ws = WStream(cx, sems=(cx.s_w0, cx.s_w1))
        qk_tok = emit_qk_fmajor(cx, ws, Q, K_, r_row, b_row, stats_tok, xb,
                                pros=(cx.pro.get("wqk"), None),
                                scrs=(cx.S["wqk"], None))
        wvbuf = A(cx, [128, 1, NCH, 256], BF16)
        v_tok = emit_v_tmajor(cx, cx.S["wv"], V, r_row, b_row, stats_tok, xb,
                              wvbuf, pro=cx.pro.get("wv"), nbuf=1,
                              sems=(cx.s_w2,))
        cx.st = old
    part_barrier(cx)

    P = A(cx, [128, T], BF16)
    PT = [A(cx, [128, 512], BF16) for _ in range(2)]
    AO = A(cx, [128, NCH, 128], BF16)
    rc = A(cx, [128, 8], F32)
    Tsc = A(cx, [128, NCH, 128], F32)
    Tb = A(cx, [128, NCH, 128], BF16)
    sq2 = A(cx, [128, 2, 128], BF16)
    wp = WStream(cx, nbuf=1, sems=(cx.s_w3,))
    cx._sq2_free = [None, None]

    cx.dbg = dict(Q=Q, K=K_, V=V, P=P, AO=AO, r_row=r_row, b_row=b_row,
                  rc=rc, Tsc=Tsc)
    for qt, q0, ao_tok in emit_attn_core(cx, Q, K_, V, AO, P, PT, rc,
                                         qk_tok, v_tok, causal=True,
                                         split_pv=False):
        _l0a_epilogue(cx, AO, ao_tok, q0, r_row, b_row, Tsc, Tb, sq2, wp)


def _l0a_epilogue(cx, AO, ao_tok, q0, r_row, b_row, Tsc, Tb, sq2, wp):
    em = cx.em
    banks = cx.banks
    bc1 = banks[2]
    emit_bcast_rb(cx, bc1, r_row, b_row, q0, 128, cx.tok_stats_l0a)
    proj_toks = []
    for mc in range(NCH):
        wi, wtok = wp.load(cx.S["wproj"], mc, pro=cx.pro.get("wproj"))
        bk = banks[mc % 2]
        w = bk.pre_write() + [wtok, ao_tok, cx.tok_l0bias]
        for kc in range(NCH):
            mm = t_mm(cx, bk.ap[:, 0:128], wp.buf[:, wi, kc], AO[:, kc, :],
                      kc == 0, kc == NCH - 1, waits=w if kc == 0 else (),
                      inc=cx.s_pe if kc == NCH - 1 else None)
        bk.set_write(mm)
        wp.done(wi, mm)
        em.emit("vector", "tensor_mul", cx.xs[:, 0:128],
                cx.X[:, mc, q0:q0 + 128], bc1.ap[:, 0:128],
                waits=[bc1.read(), bk.read()])
        em.emit("vector", "tensor_add", cx.xs[:, 0:128], cx.xs[:, 0:128],
                bc1.ap[:, 256:384])
        tk = em.emit("vector", "scalar_tensor_tensor", Tsc[:, mc],
                     bk.ap[:, 0:128], cx.bias_bproj[:, mc:mc + 1],
                     cx.xs[:, 0:128], ALU.add, ALU.add, inc=cx.s_dve)
        bk.rtoks.append(tk)
        bc1.rtoks.append(tk)
        pt = em.emit("vector", "tensor_copy", Tb[:, mc], Tsc[:, mc],
                     inc=cx.s_dve)
        proj_toks.append(pt)
    b0, b1 = banks[5], banks[6]
    w0 = b0.pre_write()
    w1 = b1.pre_write()
    for c in range(NCH):
        sqt = em.emit("gpsimd", "tensor_mul", sq2[:, c % 2], Tb[:, c],
                      Tb[:, c], waits=[proj_toks[c], cx._sq2_free[c % 2]],
                      inc=cx.s_pool)
        mm0 = t_mm(cx, b0.ap[0:1, 0:128], cx.ones_bf[:], Tb[:, c], c == 0,
                   c == NCH - 1, waits=w0 + [proj_toks[c]],
                   inc=cx.s_pe if c == NCH - 1 else None)
        mm1 = t_mm(cx, b1.ap[0:1, 0:128], cx.ones_bf[:], sq2[:, c % 2],
                   c == 0, c == NCH - 1, waits=w1 + [sqt],
                   inc=cx.s_pe)
        cx._sq2_free[c % 2] = mm1
    b0.set_write(mm0)
    b1.set_write(mm1)
    a0 = tiny(cx, "vector", "tensor_scalar", cx.rs0[0:1, 0:128],
              b0.ap[0:1, 0:128], 1.0 / D, None, ALU.mult, waits=[b0.read()])
    b0.rtoks.append(a0)
    a1 = tiny(cx, "vector", "tensor_scalar", cx.rs1[0:1, 0:128],
              b1.ap[0:1, 0:128], 1.0 / D, None, ALU.mult, waits=[b1.read()])
    b1.rtoks.append(a1)
    if hasattr(cx, "dbg_ln2"):
        d1 = em.emit("scalar", "dma_start",
                     cx.dbg_ln2[0:1, 2 * T + q0:2 * T + q0 + 128],
                     cx.rs0[0:1, 0:128], waits=[a0], inc=cx.s_sc, amt=16)
        d2 = em.emit("scalar", "dma_start",
                     cx.dbg_ln2[0:1, 3 * T + q0:3 * T + q0 + 128],
                     cx.rs1[0:1, 0:128], waits=[a1], inc=cx.s_sc, amt=16)
        em.wait("vector", (d2[0], d2[1], None))
    t1 = tiny(cx, "vector", "tensor_mul", cx.xs[0:1, 0:128],
              cx.rs0[0:1, 0:128], cx.rs0[0:1, 0:128], waits=[a0])
    t1 = tiny(cx, "vector", "tensor_sub", cx.rs1[0:1, 0:128],
              cx.rs1[0:1, 0:128], cx.xs[0:1, 0:128], waits=[a1, t1])
    vt = tiny(cx, "vector", "tensor_scalar_add", cx.rs1[0:1, 0:128],
              cx.rs1[0:1, 0:128], EPS_LN, waits=[t1])
    sqr = tiny(cx, "scalar", "activation", cx.rs1[0:1, 0:128],
               cx.rs1[0:1, 0:128], AF.Sqrt, waits=[vt])
    t1 = tiny(cx, "vector", "reciprocal", cx.xs[0:1, 0:128],
              cx.rs1[0:1, 0:128], waits=[sqr])
    t1 = tiny(cx, "vector", "tensor_mul", cx.rs0[0:1, 0:128],
              cx.rs0[0:1, 0:128], cx.xs[0:1, 0:128], waits=[t1, a0])
    nmu = tiny(cx, "vector", "tensor_scalar_mul", cx.rs0[0:1, 0:128],
               cx.rs0[0:1, 0:128], -1.0, waits=[t1])
    if hasattr(cx, "dbg_ln2"):
        d1 = em.emit("scalar", "dma_start", cx.dbg_ln2[0:1, q0:q0 + 128],
                     cx.xs[0:1, 0:128], waits=[t1], inc=cx.s_sc, amt=16)
        d2 = em.emit("scalar", "dma_start",
                     cx.dbg_ln2[0:1, T + q0:T + q0 + 128],
                     cx.rs0[0:1, 0:128], waits=[nmu], inc=cx.s_sc, amt=16)
        em.wait("vector", (d2[0], d2[1], None))
    bch = banks[3]
    wb = bch.pre_write() + [nmu]
    t_mm(cx, bch.ap[:, 0:128], cx.ones_r[:], cx.xs[0:1, 0:128], True, True,
         waits=wb)
    mmb = t_mm(cx, bch.ap[:, 256:384], cx.ones_r[:], cx.rs0[0:1, 0:128],
               True, True, inc=cx.s_pe)
    bch.set_write(mmb)
    for mc in range(NCH):
        em.emit("vector", "tensor_mul", cx.xs[:, 0:128], Tsc[:, mc],
                bch.ap[:, 0:128], waits=[bch.read()])
        em.emit("vector", "tensor_add", cx.xs[:, 0:128], cx.xs[:, 0:128],
                bch.ap[:, 256:384])
        em.emit("vector", "tensor_scalar", cx.xs[:, 0:128], cx.xs[:, 0:128],
                cx.bias_gam[:, mc:mc + 1], cx.bias_bet[:, mc:mc + 1],
                ALU.mult, ALU.add)
        xt = em.emit("vector", "tensor_add", cx.X[:, mc, q0:q0 + 128],
                     cx.X[:, mc, q0:q0 + 128], cx.xs[:, 0:128], inc=cx.s_dve)
        bch.rtoks.append(xt)


def part_l0f(cx):
    em, nc = cx.em, cx.nc
    if not hasattr(cx, "b1c"):
        cx.b1c = nc.alloc_sbuf_tensor("b1c", [128, 128], F32)
        em.emit("sync", "dma_start", cx.b1c[:], cx.H["h_b1c"],
                inc=cx.s_ld, amt=16)
        cx.gbb = nc.alloc_sbuf_tensor("gbb", [128, 8], F32)
        cx.rwb = nc.alloc_sbuf_tensor("rwb", [128, 8], F32)
        cx.b2fm = nc.alloc_sbuf_tensor("b2fm", [128, 8], F32)
        gb_row = nc.alloc_sbuf_tensor("gb_row", [1, 8], F32)
        rw_row = nc.alloc_sbuf_tensor("rw_row", [1, 8], F32)
        em.emit("sync", "dma_start", gb_row[:], cx.H["h_gateb"],
                inc=cx.s_ld, amt=16)
        tld = em.emit("sync", "dma_start", rw_row[:], cx.H["h_rmsw"],
                      inc=cx.s_ld, amt=16)
        bk = cx.banks[6]
        w = bk.pre_write() + [tld, cx.tok_setup]
        t_mm(cx, bk.ap[:, 0:8], cx.ones_r[:], gb_row[0:1, :], True, True,
             waits=w)
        mm = t_mm(cx, bk.ap[:, 8:16], cx.ones_r[:], rw_row[0:1, :],
                  True, True, inc=cx.s_pe)
        bk.set_write(mm)
        g1 = tiny(cx, "vector", "tensor_copy", cx.gbb[:], bk.ap[:, 0:8],
                  waits=[bk.read()])
        a = tiny(cx, "vector", "tensor_copy", cx.rwb[:], bk.ap[:, 8:16],
                 waits=[g1])
        bk.rtoks.append(a)
        b2stage = nc.alloc_sbuf_tensor("b2stage", [128, 8], F32)
        t0 = em.emit("sync", "dma_start", cx.b2fm[:], cx.H["h_b2fm"][0],
                     inc=cx.s_ld, amt=16)
        cx.tok_l0f = t0
        prev = None
        for e in range(1, 8):
            te = em.emit("sync", "dma_start", b2stage[:], cx.H["h_b2fm"][e],
                         waits=[prev], inc=cx.s_ld, amt=16)
            prev = tiny(cx, "vector", "tensor_add", cx.b2fm[:], cx.b2fm[:],
                        b2stage[:], waits=[te, t0, prev])
        cx.tok_l0f = prev

    ngcol = A(cx, [128, 16], F32)
    with contextlib.ExitStack() as st1:
        old = cx.st
        cx.st = st1
        r_row = A(cx, [1, T], F32)
        b_row = A(cx, [1, T], F32)
        xb = A(cx, [128, NCH, 512], BF16)
        gwb = A(cx, [128, NCH, 8], BF16)
        gcol = A(cx, [128, 8], F32)
        stats_tok = emit_ln_stats(cx, r_row, b_row, xb)
        gw_tok = em.emit("sync", "dma_start", gwb[:],
                         cx.S["gatew"].rearrange("(kc p) e -> p kc e", p=128),
                         waits=[cx.pro.get("gatew")], inc=cx.s_ld, amt=16)
        xb_dma = None
        for tt in range(4):
            t0 = tt * 512
            xn_tok = emit_xn_tile(cx, cx.banks[6], r_row, b_row, t0, 512, xb,
                                  stats_tok, out_free=xb_dma)
            xb_dma = em.emit("sync", "dma_start",
                             cx.S["xb"][tt].rearrange("p (c l) -> p c l", l=512),
                             xb[:], waits=[xn_tok, xb_dma], inc=cx.s_cw,
                             amt=16)
            for q4 in range(4):
                bk = cx.banks[2]
                w = bk.pre_write() + [gw_tok, xn_tok]
                for kc in range(NCH):
                    mm = t_mm(cx, bk.ap[:, 0:8],
                              xb[:, kc, q4 * 128:(q4 + 1) * 128],
                              gwb[:, kc], kc == 0, kc == NCH - 1,
                              waits=w if kc == 0 else (),
                              inc=cx.s_pe if kc == NCH - 1 else None)
                bk.set_write(mm)
                a = tiny(cx, "vector", "tensor_add", gcol[:, 0:8],
                         bk.ap[:, 0:8], cx.gbb[:],
                         waits=[bk.read(), cx.tok_l0f])
                bk.rtoks.append(a)
                t1 = tiny(cx, "vector", "tensor_mul", cx.xs[:, 0:8],
                          gcol[:, 0:8], gcol[:, 0:8], waits=[a])
                t1 = tiny(cx, "vector", "reduce_sum", cx.xs[:, 8:9],
                          cx.xs[:, 0:8], axis=AX.X, waits=[t1])
                vt = tiny(cx, "vector", "tensor_scalar", cx.xs[:, 8:9],
                          cx.xs[:, 8:9], 0.125, EPS_F32, ALU.mult,
                          ALU.add, waits=[t1])
                sqt = tiny(cx, "scalar", "activation", cx.xs[:, 8:9],
                           cx.xs[:, 8:9], AF.Sqrt, waits=[vt])
                t1 = tiny(cx, "vector", "reciprocal", cx.xs[:, 9:10],
                          cx.xs[:, 8:9], waits=[sqt])
                t1 = tiny(cx, "vector", "tensor_scalar_mul", gcol[:, 0:8],
                          gcol[:, 0:8], cx.xs[:, 9:10], waits=[t1])
                t1 = tiny(cx, "vector", "tensor_mul", gcol[:, 0:8],
                          gcol[:, 0:8], cx.rwb[:], waits=[t1])
                t1 = tiny(cx, "vector", "reduce_max", cx.xs[:, 10:11],
                          gcol[:, 0:8], axis=AX.X, waits=[t1])
                nm = tiny(cx, "vector", "tensor_scalar_mul", cx.xs[:, 10:11],
                          cx.xs[:, 10:11], -2.0, waits=[t1])
                ext = tiny(cx, "scalar", "activation", gcol[:, 0:8],
                           gcol[:, 0:8], AF.Exp, bias=cx.xs[:, 10:11],
                           scale=2.0, accum_out=cx.xs[:, 11:12],
                           waits=[nm])
                t1 = tiny(cx, "vector", "reciprocal", cx.xs[:, 12:13],
                          cx.xs[:, 11:12], waits=[ext])
                t2 = tiny(cx, "vector", "tensor_scalar_add", cx.xs[:, 13:14],
                          cx.xs[:, 12:13], 1e-6, waits=[t1])
                t2 = tiny(cx, "vector", "reciprocal", cx.xs[:, 13:14],
                          cx.xs[:, 13:14], waits=[t2])
                gi = tt * 4 + q4
                em.emit("vector", "tensor_mul", ngcol[:, gi:gi + 1],
                        cx.xs[:, 12:13], cx.xs[:, 13:14],
                        waits=[t1, t2], inc=cx.s_dve)
        ng_tok = part_barrier(cx) or cx.em.last["vector"]
        cx.st = old
    part_barrier(cx)

    ysb = A(cx, [128, NCH, T], F32)
    Hb = A(cx, [128, NCH, T], BF16)
    xb2 = A(cx, [128, 2, NCH, 512], BF16)
    cx.dbg = dict(ysb=ysb, Hb=Hb, ng=ngcol)
    w1s = WStream(cx, sems=(cx.s_w0, cx.s_w1))
    w2s = WStream(cx, sems=(cx.s_w2, cx.s_w3))
    xb_free = [None, None]
    ytoks = [[None] * 4 for _ in range(NCH)]
    for hbp in range(16):
        h_last = None
        for ttp in range(2):
            for par in range(2):
                tt = ttp * 2 + par
                xb_free[par] = em.emit(
                    "sync", "dma_start", xb2[:, par],
                    cx.S["xb"][tt].rearrange("p (c l) -> p c l", l=512),
                    waits=[xb_free[par], xb_dma],
                    inc=(cx.s_w4, cx.s_w5)[par], amt=16)
            for mc in range(NCH):
                wi, wtok = w1s.load(cx.S["w1t"], hbp * 8 + mc,
                                    pro=cx.pro.get("w1t"))
                for par in range(2):
                    tt = ttp * 2 + par
                    bk = cx.banks[par]
                    w = bk.pre_write() + [wtok, xb_free[par]]
                    for kc in range(NCH):
                        mm = t_mm(cx, bk.ap, w1s.buf[:, wi, kc],
                                  xb2[:, par, kc], kc == 0, kc == NCH - 1,
                                  waits=w if kc == 0 else (),
                                  inc=cx.s_pe if kc == NCH - 1 else None)
                    bk.set_write(mm)
                    g = em.emit("scalar", "activation",
                                Hb[:, mc, tt * 512:(tt + 1) * 512], bk.ap,
                                AF.Gelu,
                                bias=cx.b1c[:, hbp * 8 + mc:hbp * 8 + mc + 1],
                                waits=[bk.read()], inc=cx.s_act)
                    bk.rtoks.append(g)
                    h_last = g
                w1s.done(wi, mm)
            xb_free = [h_last, h_last]
        for mc in range(NCH):
            wi, wtok = w2s.load(cx.S["w2t"], mc, kc0=hbp * 8,
                                pro=cx.pro.get("w2t"))
            for tt in range(4):
                bk = cx.banks[2 + (tt % 2)]
                w = bk.pre_write() + [wtok, h_last]
                for kc in range(NCH):
                    mm = t_mm(cx, bk.ap, w2s.buf[:, wi, kc],
                              Hb[:, kc, tt * 512:(tt + 1) * 512],
                              kc == 0, kc == NCH - 1,
                              waits=w if kc == 0 else (),
                              inc=cx.s_pe if kc == NCH - 1 else None)
                bk.set_write(mm)
                dst = ysb[:, mc, tt * 512:(tt + 1) * 512]
                if hbp == 0:
                    a = em.emit("vector", "tensor_copy", dst, bk.ap,
                                waits=[bk.read()], inc=cx.s_dve)
                else:
                    a = em.emit("vector", "tensor_add", dst, dst, bk.ap,
                                waits=[bk.read(), ytoks[mc][tt]],
                                inc=cx.s_dve)
                bk.rtoks.append(a)
                ytoks[mc][tt] = a
            w2s.done(wi, ytoks[mc][3])
    for tt in range(4):
        t0 = tt * 512
        tb = cx.banks[6]
        wtb = tb.pre_write()
        for q4 in range(4):
            gi = tt * 4 + q4
            tr = em.emit("tensor", "transpose",
                         tb.ap[0:1, q4 * 128:(q4 + 1) * 128],
                         ngcol[:, gi:gi + 1], cx.ident_f[:],
                         waits=(wtb if q4 == 0 else []) + [ng_tok],
                         inc=cx.s_pe if q4 == 3 else None)
        tb.set_write(tr)
        rcp = em.emit("vector", "tensor_copy", cx.rs0[0:1, :],
                      tb.ap[0:1, 0:512], waits=[tb.read()], inc=cx.s_dve)
        tb.rtoks.append(rcp)
        bk2 = cx.banks[5]
        w = bk2.pre_write() + [rcp]
        mm = t_mm(cx, bk2.ap, cx.ones_r[:], cx.rs0[0:1, :], True, True,
                  waits=w, inc=cx.s_pe)
        bk2.set_write(mm)
        ngb = em.emit("vector", "tensor_copy", cx.xsB[:], bk2.ap,
                      waits=[bk2.read()], inc=cx.s_dve)
        bk2.rtoks.append(ngb)
        for mc in range(NCH):
            em.emit("vector", "tensor_scalar_add", cx.xs[:],
                    ysb[:, mc, t0:t0 + 512], cx.b2fm[:, mc:mc + 1],
                    waits=[ytoks[mc][tt], cx.tok_l0f])
            em.emit("vector", "tensor_mul", cx.xs[:], cx.xs[:], cx.xsB[:])
            em.emit("vector", "tensor_add", cx.X[:, mc, t0:t0 + 512],
                    cx.X[:, mc, t0:t0 + 512], cx.xs[:], inc=cx.s_dve)


def part_l1a(cx):
    em, nc = cx.em, cx.nc
    if not hasattr(cx, "bias_bo"):
        cx.bias_bo = nc.alloc_sbuf_tensor("bias_bo", [128, 8], F32)
        cx.bias_bq = nc.alloc_sbuf_tensor("bias_bq", [128, 8], F32)
        cx.bias_bk = nc.alloc_sbuf_tensor("bias_bk", [128, 8], F32)
        em.emit("sync", "dma_start", cx.bias_bo[:], cx.H["h_bo"],
                inc=cx.s_ld, amt=16)
        em.emit("sync", "dma_start", cx.bias_bq[:], cx.H["h_bq"],
                inc=cx.s_ld, amt=16)
        cx.tok_l1bias = em.emit("sync", "dma_start", cx.bias_bk[:],
                                cx.H["h_bk"], inc=cx.s_ld, amt=16)
        cx.ct = nc.alloc_sbuf_tensor("ct", [128, 16, 32], F32)
        cx.stb = nc.alloc_sbuf_tensor("stb", [128, 16, 32], F32)
        em.emit("sync", "dma_start", cx.ct[:], cx.H["h_ct"],
                inc=cx.s_ld, amt=16)
        em.emit("sync", "dma_start", cx.stb[:], cx.H["h_st"],
                inc=cx.s_ld, amt=16)
        cx.sel1 = nc.alloc_sbuf_tensor("sel1", [128, 128], BF16)
        cx.sel2 = nc.alloc_sbuf_tensor("sel2", [128, 128], BF16)
        em.emit("gpsimd", "dma_start", cx.sel1[:], cx.H["h_SEL1"],
                inc=cx.s_g3, amt=16)
        cx.tok_l1tab = em.emit("gpsimd", "dma_start", cx.sel2[:],
                               cx.H["h_SEL2"], inc=cx.s_g3, amt=16)

    Q = A(cx, [128, NCH, T], BF16)
    K_ = A(cx, [128, NCH, T], BF16)
    r_row = A(cx, [1, T], F32)
    b_row = A(cx, [1, T], F32)
    V = A(cx, [128, 16, 1024], BF16)

    with contextlib.ExitStack() as st2:
        old = cx.st
        cx.st = st2
        xb = A(cx, [128, NCH, 512], BF16)
        stats_tok = emit_ln_stats(cx, r_row, b_row, xb)
        with contextlib.ExitStack() as st3:
            old3 = cx.st
            cx.st = st3
            ws = WStream(cx, sems=(cx.s_w0, cx.s_w1))
            TTs = [A(cx, [128, 512], BF16), A(cx, [128, 512], BF16)]
            U1 = A(cx, [128, 512], BF16)
            U2 = A(cx, [128, 512], BF16)
            qk_tok = emit_qk_fmajor(
                cx, ws, Q, K_, r_row, b_row, stats_tok, xb,
                rope_bufs=(cx.bias_bq, cx.bias_bk,
                           (cx.H["h_T1"], cx.H["h_T2"]), TTs, U1, U2),
                pros=(cx.pro.get("wq"), cx.pro.get("wk")),
                scrs=(cx.S["wq"], cx.S["wk"]))
            cx.st = old3
        part_barrier(cx)
        bvb = A(cx, [128, 1024], F32)
        bvt0 = em.emit("sync", "dma_start", cx.rs0[:], cx.H["h_bv"][0:1, 0:512],
                       inc=cx.s_ld, amt=16)
        bvt1 = em.emit("sync", "dma_start", cx.rs1[:], cx.H["h_bv"][0:1, 512:1024],
                       inc=cx.s_ld, amt=16)
        for half, (rsx, bvt) in enumerate(((cx.rs0, bvt0), (cx.rs1, bvt1))):
            bk = cx.banks[6]
            w = bk.pre_write() + [bvt, cx.tok_setup]
            mm = t_mm(cx, bk.ap, cx.ones_r[:], rsx[0:1, :], True, True,
                      waits=w, inc=cx.s_pe)
            bk.set_write(mm)
            a = em.emit("vector", "tensor_copy",
                        bvb[:, half * 512:(half + 1) * 512], bk.ap,
                        waits=[bk.read()], inc=cx.s_dve)
            bk.rtoks.append(a)
        wvbuf = A(cx, [128, 1, NCH, 256], BF16)
        v_tok = emit_v_tmajor(cx, cx.S["wvT"], V, r_row, b_row, stats_tok,
                              xb, wvbuf, rope_bufs=(bvb, cx.ct, cx.stb),
                              pro=cx.pro.get("wvT"), nbuf=1,
                              sems=(cx.s_w2,))
        cx.st = old
    part_barrier(cx)

    P = A(cx, [128, T], BF16)
    PT = [A(cx, [128, 512], BF16) for _ in range(2)]
    AO = A(cx, [128, NCH, 128], BF16)
    rc = A(cx, [128, 8], F32)
    M3 = A(cx, [128, 128], BF16)
    M4 = A(cx, [128, 128], BF16)
    T34 = [[A(cx, [128, 512], BF16), A(cx, [128, 512], BF16)],
           [A(cx, [128, 512], BF16), A(cx, [128, 512], BF16)]]
    wo = WStream(cx, nbuf=1, sems=(cx.s_w3,))
    t34_toks = [None, None]

    def load_t34(par, t0, pre):
        em.emit("gpsimd", "dma_start", T34[0][par][:],
                cx.H["h_T3"][:, t0:t0 + 512], waits=[pre], inc=cx.s_g3,
                amt=16)
        return em.emit("gpsimd", "dma_start", T34[1][par][:],
                       cx.H["h_T4"][:, t0:t0 + 512], inc=cx.s_g3, amt=16)

    t34_toks[0] = load_t34(0, 0, qk_tok)
    t34_toks[1] = load_t34(1, 512, v_tok)
    cx.dbg = dict(Q=Q, K=K_, V=V, P=P, AO=AO, rc=rc)
    last_ao = [None, None]
    for qt, q0, ao_tok in emit_attn_core(cx, Q, K_, V, AO, P, PT, rc,
                                         qk_tok, v_tok, causal=False,
                                         split_pv=True,
                                         T34=(T34[0], T34[1]), M3=M3, M4=M4,
                                         t34_toks=t34_toks):
        tts = qt // 4
        last_ao[tts % 2] = ao_tok
        if qt % 4 == 3 and qt < 15:
            nxt = tts + 2
            if nxt < 4:
                t34_toks[nxt % 2] = load_t34(nxt % 2, nxt * 512,
                                             last_ao[nxt % 2])
        for mc in range(NCH):
            wi, wtok = wo.load(cx.S["wo"], mc, pro=cx.pro.get("wo"))
            bk = cx.banks[mc % 2]
            w = bk.pre_write() + [wtok, ao_tok, cx.tok_l1bias]
            for kc in range(NCH):
                mm = t_mm(cx, bk.ap[:, 0:128], wo.buf[:, wi, kc],
                          AO[:, kc, :], kc == 0, kc == NCH - 1,
                          waits=w if kc == 0 else (),
                          inc=cx.s_pe if kc == NCH - 1 else None)
            bk.set_write(mm)
            wo.done(wi, mm)
            xt = em.emit("vector", "scalar_tensor_tensor",
                         cx.X[:, mc, q0:q0 + 128], bk.ap[:, 0:128],
                         cx.bias_bo[:, mc:mc + 1], cx.X[:, mc, q0:q0 + 128],
                         ALU.add, ALU.add, waits=[bk.read(), cx.tok_x],
                         inc=cx.s_dve)
            bk.rtoks.append(xt)


def part_l1f(cx):
    em, nc = cx.em, cx.nc
    if not hasattr(cx, "b1f"):
        cx.b1f = nc.alloc_sbuf_tensor("b1f", [128, 32], F32)
        em.emit("sync", "dma_start", cx.b1f[:], cx.H["h_b1f"],
                inc=cx.s_ld, amt=16)
        cx.b2f = nc.alloc_sbuf_tensor("b2f", [128, 8], F32)
        cx.tok_l1f = em.emit("sync", "dma_start", cx.b2f[:], cx.H["h_b2f"],
                             inc=cx.s_ld, amt=16)

    r_row = A(cx, [1, T], F32)
    b_row = A(cx, [1, T], F32)
    xb = A(cx, [128, NCH, 512], BF16)
    stats_tok = emit_ln_stats(cx, r_row, b_row, xb)

    Hb = A(cx, [128, 32, 1024], BF16)
    xbf = A(cx, [128, NCH, 1024], BF16)
    w1s = WStream(cx, sems=(cx.s_w0, cx.s_w1))
    w2s = WStream(cx, sems=(cx.s_w2, cx.s_w3))
    cx.dbg = dict(Hb=Hb, xbf=xbf)

    for ft in range(2):
        f0 = ft * 1024
        emit_xn_tile(cx, cx.banks[6], r_row, b_row, f0, 512,
                     xbf[:, :, 0:512], stats_tok)
        xn_tok = emit_xn_tile(cx, cx.banks[6], r_row, b_row, f0 + 512, 512,
                              xbf[:, :, 512:1024], stats_tok)
        h_last = None
        for mc in range(32):
            wi, wtok = w1s.load(cx.S["w1f"], mc, pro=cx.pro.get("w1f"))
            for g2 in range(2):
                bk = cx.banks[g2 % 2]
                w = bk.pre_write() + [wtok, xn_tok]
                for kc in range(NCH):
                    mm = t_mm(cx, bk.ap, w1s.buf[:, wi, kc],
                              xbf[:, kc, g2 * 512:(g2 + 1) * 512],
                              kc == 0, kc == NCH - 1,
                              waits=w if kc == 0 else (),
                              inc=cx.s_pe if kc == NCH - 1 else None)
                bk.set_write(mm)
                h_last = em.emit("scalar", "activation",
                                 Hb[:, mc, g2 * 512:(g2 + 1) * 512], bk.ap,
                                 AF.Gelu, bias=cx.b1f[:, mc:mc + 1],
                                 waits=[bk.read(), cx.tok_l1f],
                                 inc=cx.s_act)
                bk.rtoks.append(h_last)
            w1s.done(wi, mm)
        for mc in range(NCH):
            pw = {g2: cx.banks[2 + g2].pre_write() for g2 in range(2)}
            for kq in range(4):
                wi, wtok = w2s.load(cx.S["w2f"], mc, kc0=kq * 8,
                                    pro=cx.pro.get("w2f"))
                for g2 in range(2):
                    bk = cx.banks[2 + g2]
                    w = (pw[g2] if kq == 0 else []) + [wtok, h_last]
                    for kc in range(NCH):
                        mm = t_mm(cx, bk.ap, w2s.buf[:, wi, kc],
                                  Hb[:, kq * 8 + kc, g2 * 512:(g2 + 1) * 512],
                                  kq == 0 and kc == 0,
                                  kq == 3 and kc == NCH - 1,
                                  waits=w if kc == 0 else (),
                                  inc=cx.s_pe if kc == NCH - 1 else None)
                    if kq == 3:
                        bk.set_write(mm)
                        sl = slice(f0 + g2 * 512, f0 + (g2 + 1) * 512)
                        xt = em.emit("vector", "scalar_tensor_tensor",
                                     cx.X[:, mc, sl], bk.ap,
                                     cx.b2f[:, mc:mc + 1], cx.X[:, mc, sl],
                                     ALU.add, ALU.add,
                                     waits=[bk.read(), cx.tok_x],
                                     inc=cx.s_dve)
                        bk.rtoks.append(xt)
                w2s.done(wi, mm)


PART_EMITTERS = {"l0a": part_l0a, "l0f": part_l0f, "l1a": part_l1a,
                 "l1f": part_l1f}

PART_GROUPS = [["l0a", "l0f", "l1a", "l1f"]]


def kernel(**inputs):
    common = prep_host(inputs)
    x = np.asarray(inputs["L0_attn_target"], np.float32)
    xs = x.reshape(NCORE, NSEQ, T, NCH, 128)
    cur = [_x_to_dev(xs[c]) for c in range(NCORE)]

    for parts in PART_GROUPS:
        nc = build_program(parts)
        keys = [k for p in parts for k in PART_HOST[p]]
        in_maps = []
        for c in range(NCORE):
            m = {k: common[k] for k in keys}
            m["x_in"] = cur[c]
            in_maps.append(m)
        res = run_bass_kernel_spmd(nc, in_maps, core_ids=list(range(NCORE)))
        cur = [np.asarray(res.results[c]["x_out"]) for c in range(NCORE)]

    out = np.empty((B, T, D), np.float32)
    for c in range(NCORE):
        out[c * NSEQ:(c + 1) * NSEQ] = _x_from_dev(cur[c])
    return out
